# version 25
# baseline (speedup 1.0000x reference)
import numpy as np
from contextlib import ExitStack
import concourse.bass as bass
import concourse.mybir as mybir
from concourse.bass_utils import run_bass_kernel_spmd
import ml_dtypes

F32 = mybir.dt.float32
BF16 = mybir.dt.bfloat16
A = mybir.AluOpType
AX = mybir.AxisListType
AF = mybir.ActivationFunctionType

N = 2048
D = 1024
H = 8
E = 8
TT = 16
CC = 8
TQ = 4
QC = 4
KT = 16
NP = 4
SCALE = 0.125
EPS = 1e-5
U0 = 521
BW = 1176


def tile_class(qc, kt):
    d0 = kt * 128 - qc * 512
    if d0 >= 521:
        return 2
    if d0 <= -137:
        return 1
    return 0

FIRST_BAND = None
for _qc in range(QC):
    for _kt in range(KT):
        if tile_class(_qc, _kt) == 0:
            FIRST_BAND = (_qc, _kt)
            break
    if FIRST_BAND:
        break


class Counter:
    def __init__(self):
        self.c = {}
    def inc(self, name, k=1):
        self.c[name] = self.c.get(name, 0) + k
        return self.c[name]
    def val(self, name):
        return self.c.get(name, 0)


def build_program(debug=False, phases=4):
    nc = bass.Bass("TRN2", target_bir_lowering=False)

    x = nc.dram_tensor("x", [N, D], F32, kind="ExternalInput").ap()
    wqk = nc.dram_tensor("wqk", [D, D], F32, kind="ExternalInput").ap()
    wv = nc.dram_tensor("wv", [D, D], F32, kind="ExternalInput").ap()
    wout = nc.dram_tensor("wout", [D, D], F32, kind="ExternalInput").ap()
    bout = nc.dram_tensor("bout", [1, D], F32, kind="ExternalInput").ap()
    rw = nc.dram_tensor("rw", [D, E], F32, kind="ExternalInput").ap()
    we = nc.dram_tensor("we", [E, D, D], F32, kind="ExternalInput").ap()
    btabd = nc.dram_tensor("btab", [H, 128, BW], F32, kind="ExternalInput").ap()
    biasseld = nc.dram_tensor("biassel", [1, 3 * H], F32, kind="ExternalInput").ap()
    convfd = nc.dram_tensor("convf", [1, 21 * H], F32, kind="ExternalInput").ap()
    identd = nc.dram_tensor("ident", [128, 128], BF16, kind="ExternalInput").ap()
    out = nc.dram_tensor("out", [N, D], F32, kind="ExternalOutput").ap()

    dbg = {}
    if debug:
        for nm, shp in [("d_xnh", [128, CC * N]), ("d_xnl", [128, CC * N]),
                        ("d_coef", [128, TT * E]), ("d_s", [128, TT]),
                        ("d_qt", [128, N]), ("d_kt", [128, N]),
                        ("d_v", [128, TT * 256]), ("d_l", [1, N]),
                        ("d_ot", [128, N]), ("d_accattn", [128, TT * D])]:
            dbg[nm] = nc.dram_tensor(nm, shp, F32, kind="ExternalOutput").ap()

    x_t = x.rearrange("(tt p) j -> p tt j", p=128)
    out_t = out.rearrange("(tt p) j -> p tt j", p=128)

    ops = {k: [] for k in ("sync", "tensor", "scalar", "vector", "gpsimd")}
    def emit(eng, fn):
        ops[eng].append(fn)
    cnt = Counter()

    es = ExitStack()
    def sem(name):
        return es.enter_context(nc.semaphore(name))

    with es:
        ld = sem("ld"); mm = sem("mm"); sc = sem("sc")
        ve = sem("ve"); gp = sem("gp"); st = sem("st"); sg = sem("sg")
        lr = sem("lr"); lw = [sem("lw0"), sem("lw1")]; lb = sem("lb")
        le = [sem("le0"), sem("le1"), sem("le2"), sem("le3")]
        la = [sem("la0"), sem("la1"), sem("la2"), sem("la3")]
        SEMS = {}

        acc = es.enter_context(nc.sbuf_tensor("acc", [128, TT * D], F32))
        xnt_hi = es.enter_context(nc.sbuf_tensor("xnt_hi", [128, CC * N], BF16))
        xnt_lo = es.enter_context(nc.sbuf_tensor("xnt_lo", [128, CC * N], BF16))
        ident = es.enter_context(nc.sbuf_tensor("ident_s", [128, 128], BF16))
        onesb = es.enter_context(nc.sbuf_tensor("onesb", [128, 1], BF16))
        ones_hi = es.enter_context(nc.sbuf_tensor("ones_hi", [128, 1], BF16))
        ones_lo = es.enter_context(nc.sbuf_tensor("ones_lo", [128, 1], BF16))
        onesf = es.enter_context(nc.sbuf_tensor("onesf", [1, 128], F32))
        convf = es.enter_context(nc.sbuf_tensor("convf_s", [1, 21 * H], F32))
        biassel = es.enter_context(nc.sbuf_tensor("biassel_s", [1, 3 * H], F32))
        coef = es.enter_context(nc.sbuf_tensor("coef", [128, TT * E], F32))
        s_all = es.enter_context(nc.sbuf_tensor("s_all", [128, TT], F32))

        acc_3d = acc[:].rearrange("p (tt j) -> p tt j", j=D)
        def _init_dma(sy):
            for i in range(4):
                sy.dma_start(out=acc_3d[:, 4 * i:4 * i + 4, :],
                             in_=x_t[:, 4 * i:4 * i + 4, :]).then_inc(la[i], 16)
            sy.dma_start(out=ident[:], in_=identd).then_inc(ld, 16)
            sy.dma_start(out=convf[:], in_=convfd).then_inc(ld, 16)
            sy.dma_start(out=biassel[:], in_=biasseld).then_inc(ld, 16)
        emit("sync", _init_dma)
        v_const_ld = cnt.inc("ld", 16 * 3)
        v_acc_ld = [cnt.inc("la%d" % i, 16) for i in range(4)]

        def _init_vec(v):
            v.memset(onesb[:], 1.0)
            v.memset(ones_hi[0:64, :], 1.0)
            v.memset(ones_hi[64:128, :], 0.0)
            v.memset(ones_lo[0:64, :], 0.0)
            v.memset(ones_lo[64:128, :], 1.0)
            v.drain()
            v.memset(onesf[:], 1.0).then_inc(ve, 1)
        emit("vector", _init_vec)
        v_ones = cnt.inc("ve")

        ph1 = ExitStack()
        trash = ph1.enter_context(nc.sbuf_tensor("trash", [128, D], F32))
        sum_t = [ph1.enter_context(nc.sbuf_tensor(f"sum{i}", [128, 1], F32)) for i in range(4)]
        ssq_t = [ph1.enter_context(nc.sbuf_tensor(f"ssq{i}", [128, 1], F32)) for i in range(4)]
        mean_t = [ph1.enter_context(nc.sbuf_tensor(f"mean{i}", [128, 1], F32)) for i in range(4)]
        rstd_t = [ph1.enter_context(nc.sbuf_tensor(f"rstd{i}", [128, 1], F32)) for i in range(4)]
        nmr_t = [ph1.enter_context(nc.sbuf_tensor(f"nmr{i}", [128, 1], F32)) for i in range(4)]
        tmp1 = [ph1.enter_context(nc.sbuf_tensor(f"ltmp{i}", [128, 1], F32)) for i in range(4)]
        xnf = [ph1.enter_context(nc.sbuf_tensor(f"xnf{i}", [128, D], F32)) for i in range(2)]
        tsbh = [ph1.enter_context(nc.sbuf_tensor(f"tsbh{i}", [128, D], BF16)) for i in range(2)]
        tsbl = [ph1.enter_context(nc.sbuf_tensor(f"tsbl{i}", [128, D], BF16)) for i in range(2)]
        pth = [ph1.enter_context(nc.psum_tensor(f"pth{i}", [128, D], BF16)) for i in range(2)]
        ptl = [ph1.enter_context(nc.psum_tensor(f"ptl{i}", [128, D], BF16)) for i in range(2)]

        xnt_hi_4d = xnt_hi[:].rearrange("p (cc n t) -> p cc n t", cc=CC, n=TT)
        xnt_lo_4d = xnt_lo[:].rearrange("p (cc n t) -> p cc n t", cc=CC, n=TT)

        ph2 = ExitStack()
        rwf = ph2.enter_context(nc.sbuf_tensor("rwf", [128, CC * E], F32))
        rwh = ph2.enter_context(nc.sbuf_tensor("rwh", [128, CC * E], BF16))
        rwl = ph2.enter_context(nc.sbuf_tensor("rwl", [128, CC * E], BF16))
        prt = ph2.enter_context(nc.psum_tensor("prt", [128, 512], F32))
        r_m = [ph2.enter_context(nc.sbuf_tensor(f"rm_{i}", [128, 1], F32)) for i in range(4)]
        e8b = ph2.enter_context(nc.sbuf_tensor("e8b", [128, 2 * 4 * E], F32))
        rpb = ph2.enter_context(nc.sbuf_tensor("rpb", [128, 2 * 4 * E], F32))
        t8b = ph2.enter_context(nc.sbuf_tensor("t8b", [128, 2 * 4 * E], F32))
        rdenb = ph2.enter_context(nc.sbuf_tensor("rdenb", [128, 8], F32))
        mb = ph2.enter_context(nc.sbuf_tensor("mb", [128, 8], F32))
        e8v = e8b[:].rearrange("p (b i e) -> p b i e", b=2, i=4)
        rpv = rpb[:].rearrange("p (b i e) -> p b i e", b=2, i=4)
        t8v = t8b[:].rearrange("p (b i e) -> p b i e", b=2, i=4)

        def _rw_dma(sy):
            sy.dma_start(out=rwf[:].rearrange("p (cc e) -> p cc e", cc=CC),
                         in_=rw.rearrange("(cc p) e -> p cc e", p=128)).then_inc(lr, 16)
        emit("sync", _rw_dma)
        v_rw_ld = cnt.inc("lr", 16)

        def _rw_hi(s):
            s.wait_ge(lr, v_rw_ld)
            s.copy(out=rwh[:], in_=rwf[:]).then_inc(sc, 1)
        emit("scalar", _rw_hi)
        v_rwh = cnt.inc("sc")

        def _rw_lo(v):
            v.wait_ge(sc, v_rwh)
            v.tensor_tensor(out=rwl[:], in0=rwf[:], in1=rwh[:], op=A.subtract).then_inc(ve, 1)
        emit("vector", _rw_lo)
        v_rwl = cnt.inc("ve")

        def emit_router_mm(tt):
            par2 = tt % 2
            def _rt_mm(t, tt=tt, par=par2):
                t.wait_ge(sc, SEMS[("ln_evac", tt)])
                if tt == 0:
                    t.wait_ge(ve, v_rwl)
                if tt >= 1:
                    t.wait_ge(sc, SEMS[("rt_read", tt - 1)])
                po = prt[:, 0:E]
                for cc in range(CC):
                    lh = xnt_hi[:, cc * N + tt * 128: cc * N + (tt + 1) * 128]
                    ll = xnt_lo[:, cc * N + tt * 128: cc * N + (tt + 1) * 128]
                    rh = rwh[:, cc * E:(cc + 1) * E]
                    rl = rwl[:, cc * E:(cc + 1) * E]
                    t.matmul(po, lh, rh, start=(cc == 0), stop=False)
                    t.matmul(po, lh, rl, start=False, stop=False)
                    i3 = t.matmul(po, ll, rh, start=False, stop=(cc == CC - 1))
                    if cc == CC - 1:
                        i3.then_inc(mm, 1)
            emit("tensor", _rt_mm)
            SEMS[("rt_mm", tt)] = cnt.inc("mm")

        def emit_router_c1(tt):
            par = tt % 4
            par2 = tt % 2
            def _rt_v1(v, tt=tt, par=par, par2=par2):
                v.wait_ge(mm, SEMS[("rt_mm", tt)])
                if tt >= 4:
                    v.wait_ge(sc, SEMS[("rt_read", tt - 4)])
                po = prt[:, 0:E]
                v.drain()
                v.reduce_max(out=r_m[par][:], in_=po, axis=AX.X, negate=True).then_inc(ve, 1)
            emit("vector", _rt_v1)
            SEMS[("rt_negm", tt)] = cnt.inc("ve")

            def _rt_s1(s, tt=tt, par=par, par2=par2):
                s.wait_ge(ve, SEMS[("rt_negm", tt)])
                b, i = tt // 4, tt % 4
                if b >= 2:
                    s.wait_ge(ve, SEMS[("rt_doneB", b - 2)])
                po = prt[:, 0:E]
                s.activation(out=e8v[:, b % 2, i, :], in_=po, func=AF.Exp,
                             bias=r_m[par][:, 0:1],
                             accum_out=rdenb[:, (b % 2) * 4 + i: (b % 2) * 4 + i + 1]).then_inc(sc, 1)
            emit("scalar", _rt_s1)
            SEMS[("rt_read", tt)] = cnt.inc("sc")

        def emit_router_batch(b):
            par = b % 2
            def _rt_vb(v, b=b, par=par):
                v.wait_ge(sc, SEMS[("rt_read", 4 * b + 3)])
                e8s = e8v[:, par]
                rps = rpv[:, par]
                t8s = t8v[:, par]
                rden4 = rdenb[:, par * 4:(par + 1) * 4]
                m4 = mb[:, par * 4:(par + 1) * 4]
                m4b = m4[:, :, None].broadcast_to([128, 4, E])
                rden4b = rden4[:, :, None].broadcast_to([128, 4, E])
                cfv = coef[:, 4 * b * E: (4 * b + 4) * E].rearrange("p (i e) -> p i e", i=4)
                v.drain()
                v.reciprocal(out=rden4, in_=rden4)
                v.drain()
                v.tensor_tensor(out=rps, in0=e8s, in1=rden4b, op=A.mult)
                v.drain()
                v.reduce_max(out=m4, in_=rps, axis=AX.X)
                v.drain()
                v.tensor_tensor(out=t8s, in0=rps, in1=m4b, op=A.is_equal)
                v.drain()
                v.scalar_tensor_tensor(out=t8s, in0=t8s, scalar=-1e30,
                                       in1=rps, op0=A.mult, op1=A.add)
                v.drain()
                v.reduce_max(out=m4, in_=t8s, axis=AX.X)
                v.drain()
                v.tensor_tensor(out=t8s, in0=rps, in1=m4b, op=A.is_ge)
                v.drain()
                v.tensor_tensor(out=t8s, in0=rps, in1=t8s, op=A.mult)
                v.drain()
                v.reduce_sum(out=m4, in_=t8s, axis=AX.X)
                v.drain()
                v.tensor_scalar(out=m4, in0=m4, scalar1=1e-6, scalar2=None, op0=A.add)
                v.drain()
                v.reciprocal(out=m4, in_=m4)
                v.drain()
                v.tensor_tensor(out=cfv, in0=t8s, in1=m4b, op=A.mult)
                v.drain()
                v.tensor_tensor(out=t8s, in0=cfv, in1=rps, op=A.mult)
                v.drain()
                v.reduce_sum(out=s_all[:, 4 * b: 4 * b + 4], in_=t8s, axis=AX.X).then_inc(ve, 1)
            emit("vector", _rt_vb)
            SEMS[("rt_doneB", b)] = cnt.inc("ve")

        ph4 = ExitStack()
        boutbc = ph4.enter_context(nc.sbuf_tensor("boutbc", [128, D], F32))
        bout_s = ph4.enter_context(nc.sbuf_tensor("bout_s", [1, D], F32))
        we_hi = [ph4.enter_context(nc.sbuf_tensor(f"we_hi{i}", [128, CC * 512], BF16)) for i in range(2)]
        we_lo = [ph4.enter_context(nc.sbuf_tensor(f"we_lo{i}", [128, CC * 512], BF16)) for i in range(2)]
        wes = [ph4.enter_context(nc.sbuf_tensor(f"wes{i}", [128, 512], F32)) for i in range(4)]
        pf = [ph4.enter_context(nc.psum_tensor(f"pf{i}", [128, 512], F32)) for i in range(3)]

        v_ffn_mm = {}; v_ffn_acc = {}
        NE = 2 * E
        v_we_split = [None] * NE

        def stage_expert_cc(eh, cc):
                e, half = eh // 2, eh % 2
                par = cc % 4
                def _d(sy, e=e, half=half, eh=eh, cc=cc, par=par):
                    if cc >= 4:
                        sy.wait_ge(ve, SEMS[("wes", eh, cc - 4)])
                    elif eh >= 1:
                        sy.wait_ge(ve, SEMS[("wes", eh - 1, CC - 4 + cc)])
                    sy.dma_start(out=wes[par][:],
                                 in_=we_v[e, :, cc, half * 512:(half + 1) * 512]).then_inc(le[par], 16)
                emit("sync", _d)
                v_l = cnt.inc("le%d" % par, 16)
                def _h(s, eh=eh, cc=cc, par=par, v_l=v_l):
                    s.wait_ge(le[par], v_l)
                    if eh >= 2 and cc < 2:
                        s.wait_ge(mm, v_ffn_mm[(eh - 2, TT - 1)])
                    s.drain()
                    s.copy(out=we_hi[eh % 2][:, cc * 512:(cc + 1) * 512], in_=wes[par][:]).then_inc(sc, 1)
                emit("scalar", _h)
                v_h2 = cnt.inc("sc")
                def _lo(v, eh=eh, cc=cc, par=par, v_h2=v_h2):
                    v.wait_ge(sc, v_h2)
                    v.drain()
                    v.scalar_tensor_tensor(out=we_lo[eh % 2][:, cc * 512:(cc + 1) * 512], in0=wes[par][:],
                                           scalar=1.0, in1=we_hi[eh % 2][:, cc * 512:(cc + 1) * 512],
                                           op0=A.mult, op1=A.subtract).then_inc(ve, 1)
                emit("vector", _lo)
                SEMS[("wes", eh, cc)] = cnt.inc("ve")

        def stage_expert(eh):
            for cc in range(CC):
                stage_expert_cc(eh, cc)
            return SEMS[("wes", eh, CC - 1)]

        def _bout_dma(sy):
            sy.dma_start(out=bout_s[:], in_=bout).then_inc(lb, 16)
        emit("sync", _bout_dma)
        v_bout_ld = cnt.inc("lb", 16)

        def _bb_mm(t):
            t.wait_ge(lb, v_bout_ld)
            t.wait_ge(ve, v_ones)
            t.matmul(pf[0][:], onesf[0:1, :], bout_s[0:1, 0:512], start=True, stop=True).then_inc(mm, 1)
            t.matmul(pf[1][:], onesf[0:1, :], bout_s[0:1, 512:1024], start=True, stop=True).then_inc(mm, 1)
        emit("tensor", _bb_mm)
        v_bb = cnt.inc("mm", 2)
        def _bb_ev(v):
            v.wait_ge(mm, v_bb)
            v.drain()
            v.tensor_copy(out=boutbc[:, 0:512], in_=pf[0][:])
            v.drain()
            v.tensor_copy(out=boutbc[:, 512:1024], in_=pf[1][:]).then_inc(ve, 1)
        emit("vector", _bb_ev)
        v_bbe = cnt.inc("ve")

        ln_evac_defer = []
        def emit_ln_evac(tt):
            def _sc5(s, tt=tt):
                s.wait_ge(mm, SEMS[("ln_tp", tt)])
                s.copy(out=xnt_hi_4d[:, :, tt, :], in_=pth[tt % 2][:].rearrange("p (cc t) -> p cc t", cc=CC)).then_inc(sc, 1)
                s.copy(out=xnt_lo_4d[:, :, tt, :], in_=ptl[tt % 2][:].rearrange("p (cc t) -> p cc t", cc=CC)).then_inc(sc, 1)
            emit("scalar", _sc5)
            SEMS[("ln_evac", tt)] = cnt.inc("sc", 2)

        for tt in range(TT):
            par = tt % 2
            par4 = tt % 4
            a_tile = acc[:, tt * D:(tt + 1) * D]

            def _sc(s, tt=tt, par=par4, a_tile=a_tile):
                if tt % 4 == 0:
                    s.wait_ge(la[tt // 4], v_acc_ld[tt // 4])
                if tt >= 4:
                    s.wait_ge(ve, SEMS[("ln_var", tt - 4)])
                s.activation(out=trash[:], in_=a_tile, func=AF.Copy,
                             accum_out=sum_t[par][:, 0:1])
                s.drain()
                s.activation(out=trash[:], in_=a_tile, func=AF.Square,
                             accum_out=ssq_t[par][:, 0:1]).then_inc(sc, 1)
            emit("scalar", _sc)
            SEMS[("ln_stats", tt)] = cnt.inc("sc")

            def _ve1(v, tt=tt, par=par4):
                v.wait_ge(sc, SEMS[("ln_stats", tt)])
                if tt >= 4:
                    v.wait_ge(gp, SEMS[("ln_gp", tt - 4)])
                v.tensor_scalar(out=mean_t[par][:], in0=sum_t[par][:], scalar1=1.0 / D,
                                scalar2=None, op0=A.mult)
                v.drain()
                v.tensor_tensor(out=tmp1[par][:], in0=mean_t[par][:], in1=mean_t[par][:], op=A.mult)
                v.drain()
                v.tensor_scalar(out=tmp1[par][:], in0=ssq_t[par][:], scalar1=1.0 / D,
                                scalar2=tmp1[par][:, 0:1], op0=A.mult, op1=A.subtract)
                v.drain()
                v.tensor_scalar(out=tmp1[par][:], in0=tmp1[par][:], scalar1=EPS, scalar2=None,
                                op0=A.add).then_inc(ve, 1)
            emit("vector", _ve1)
            SEMS[("ln_var", tt)] = cnt.inc("ve")

            def _sc2(s, tt=tt, par=par4):
                s.wait_ge(ve, SEMS[("ln_var", tt)])
                if tt >= 4:
                    s.wait_ge(gp, SEMS[("ln_gp", tt - 4)])
                s.activation(out=rstd_t[par][:], in_=tmp1[par][:], func=AF.Ln)
                s.drain()
                s.activation(out=rstd_t[par][:], in_=rstd_t[par][:], func=AF.Exp,
                             scale=-0.5).then_inc(sc, 1)
            emit("scalar", _sc2)
            SEMS[("ln_rstd", tt)] = cnt.inc("sc")

            def _gp(g, tt=tt, par=par, par4=par4, a_tile=a_tile):
                g.wait_ge(sc, SEMS[("ln_rstd", tt)])
                if tt >= 2:
                    g.wait_ge(ve, SEMS[("ln_split", tt - 2)])
                if tt >= 4:
                    g.wait_ge(ve, SEMS[("ln_hi", tt - 4)])
                g.tensor_scalar(out=nmr_t[par4][:], in0=mean_t[par4][:], scalar1=-1.0,
                                scalar2=rstd_t[par4][:, 0:1], op0=A.mult, op1=A.mult).then_inc(gp, 1)
                g.drain()
                g.tensor_scalar(out=xnf[par][:], in0=a_tile, scalar1=rstd_t[par4][:, 0:1],
                                scalar2=nmr_t[par4][:, 0:1], op0=A.mult, op1=A.add).then_inc(gp, 1)
            emit("gpsimd", _gp)
            SEMS[("ln_nmr", tt)] = cnt.inc("gp")
            SEMS[("ln_gp", tt)] = cnt.inc("gp")

            def _ve3(v, tt=tt, par=par, par4=par4, a_tile=a_tile):
                v.wait_ge(gp, SEMS[("ln_nmr", tt)])
                if tt >= 2:
                    v.wait_ge(mm, SEMS[("ln_tp", tt - 2)])
                v.drain()
                v.tensor_scalar(out=tsbh[par][:], in0=a_tile, scalar1=rstd_t[par4][:, 0:1],
                                scalar2=nmr_t[par4][:, 0:1], op0=A.mult, op1=A.add).then_inc(ve, 1)
            emit("vector", _ve3)
            SEMS[("ln_hi", tt)] = cnt.inc("ve")

            def _ve4(v, tt=tt, par=par):
                v.wait_ge(gp, SEMS[("ln_gp", tt)])
                v.drain()
                v.scalar_tensor_tensor(out=tsbl[par][:], in0=xnf[par][:], scalar=1.0,
                                       in1=tsbh[par][:], op0=A.mult, op1=A.subtract).then_inc(ve, 1)
            emit("vector", _ve4)
            SEMS[("ln_split", tt)] = cnt.inc("ve")

            def _tp(t, tt=tt, par=par):
                t.wait_ge(ve, SEMS[("ln_split", tt)])
                if tt == 0:
                    t.wait_ge(ld, v_const_ld)
                if tt >= 2:
                    t.wait_ge(sc, SEMS[("ln_evac", tt - 2)])
                for cc in range(CC):
                    last = (cc == CC - 1)
                    i1 = t.transpose(pth[tt % 2][:, cc * 128:(cc + 1) * 128], tsbh[par][:, cc * 128:(cc + 1) * 128], ident[:])
                    if last:
                        i1.then_inc(mm, 1)
                    i2 = t.transpose(ptl[tt % 2][:, cc * 128:(cc + 1) * 128], tsbl[par][:, cc * 128:(cc + 1) * 128], ident[:])
                    if last:
                        i2.then_inc(mm, 1)
            emit("tensor", _tp)
            SEMS[("ln_tp", tt)] = cnt.inc("mm", 2)

            stage_expert_cc(tt // 8, tt % 8)
            ln_evac_defer.append(tt)
            if len(ln_evac_defer) > 1:
                emit_ln_evac(ln_evac_defer.pop(0))
            if tt >= 1:
                emit_router_mm(tt - 1)
            if tt >= 2:
                emit_router_c1(tt - 2)
            if tt >= 5 and (tt - 5) % 4 == 0:
                emit_router_batch((tt - 5) // 4)
        while ln_evac_defer:
            emit_ln_evac(ln_evac_defer.pop(0))
        emit_router_mm(TT - 1)
        for _q in (TT - 2, TT - 1):
            emit_router_c1(_q)
        emit_router_batch(TT // 4 - 1)
        v_router_done = SEMS[("rt_doneB", TT // 4 - 1)]
        v_we_split[0] = SEMS[("wes", 0, CC - 1)]
        v_we_split[1] = SEMS[("wes", 1, CC - 1)]

        v_xnt_ready = SEMS[("ln_evac", TT - 1)]
        v_xnf_all = SEMS[("ln_gp", TT - 1)]

        def _truncate_store():
            def _store_tr(sy):
                sy.wait_ge(ve, cnt.val("ve"))
                sy.wait_ge(sc, cnt.val("sc"))
                sy.wait_ge(gp, cnt.val("gp"))
                sy.wait_ge(mm, cnt.val("mm"))
                sy.dma_start(out=out_t, in_=acc[:].rearrange("p (tt j) -> p tt j", j=D)).then_inc(st, 16)
            emit("sync", _store_tr)
            cnt.inc("st", 16)
            def _fw(sy):
                sy.wait_ge(st, cnt.val("st"))
                if cnt.val("sg"):
                    sy.wait_ge(sg, cnt.val("sg"))
            emit("sync", _fw)
            with nc.Block() as block:
                @block.sync
                def _(eng):
                    for f in ops["sync"]: f(eng)
                @block.tensor
                def _(eng):
                    for f in ops["tensor"]: f(eng)
                @block.scalar
                def _(eng):
                    for f in ops["scalar"]: f(eng)
                @block.vector
                def _(eng):
                    for f in ops["vector"]: f(eng)
                @block.gpsimd
                def _(eng):
                    for f in ops["gpsimd"]: f(eng)
            nc._dbg_cnt = dict(cnt.c)
            return nc

        if phases == 1:
            return _truncate_store()

        if debug:
            def _dbg_xnt(sy):
                sy.wait_ge(sc, v_xnt_ready)
                sy.dma_start(out=dbg["d_xnh"], in_=xnt_hi[:]).then_inc(sg, 16)
                sy.dma_start(out=dbg["d_xnl"], in_=xnt_lo[:]).then_inc(sg, 16)
            emit("gpsimd", _dbg_xnt)
            cnt.inc("sg", 32)


        if phases == 2:
            return _truncate_store()

        if debug:
            def _dbg_rt(sy):
                sy.wait_ge(ve, v_router_done)
                sy.dma_start(out=dbg["d_coef"], in_=coef[:]).then_inc(sg, 16)
                sy.dma_start(out=dbg["d_s"], in_=s_all[:]).then_inc(sg, 16)
            emit("gpsimd", _dbg_rt)
            cnt.inc("sg", 32)


        for eh in range(NE):
            e, half = eh // 2, eh % 2
            for tt in range(TT):
                p0 = pf[tt % 3]
                def _fmm(t, eh=eh, tt=tt, p0=p0):
                    if eh == 0:
                        t.wait_ge(sc, SEMS[("ln_evac", tt)])
                    if tt == 0:
                        t.wait_ge(ve, v_we_split[eh])
                        if eh == 0:
                            t.wait_ge(ve, v_bbe)
                    if tt >= 3:
                        t.wait_ge(ve, v_ffn_acc[(eh, tt - 3)])
                    elif eh >= 1:
                        t.wait_ge(ve, v_ffn_acc[(eh - 1, {0: 15, 1: 13, 2: 14}[tt])])
                    for cc in range(CC):
                        lh = xnt_hi[:, cc * N + tt * 128: cc * N + (tt + 1) * 128]
                        llo = xnt_lo[:, cc * N + tt * 128: cc * N + (tt + 1) * 128]
                        wh = we_hi[eh % 2][:, cc * 512:(cc + 1) * 512]
                        wl = we_lo[eh % 2][:, cc * 512:(cc + 1) * 512]
                        t.matmul(p0[:], lh, wh, start=(cc == 0), stop=False)
                        t.matmul(p0[:], lh, wl, start=False, stop=False)
                        i3 = t.matmul(p0[:], llo, wh, start=False, stop=(cc == CC - 1))
                        if cc == CC - 1:
                            i3.then_inc(mm, 1)
                emit("tensor", _fmm)
                v_ffn_mm[(eh, tt)] = cnt.inc("mm")

                def _facc(v, e=e, half=half, eh=eh, tt=tt, p0=p0):
                    v.wait_ge(mm, v_ffn_mm[(eh, tt)])
                    v.wait_ge(gp, SEMS[("ln_gp", tt)])
                    cf = coef[:, tt * E + e: tt * E + e + 1]
                    v.drain()
                    v.scalar_tensor_tensor(out=acc[:, tt * D + half * 512: tt * D + (half + 1) * 512],
                                           in0=p0[:], scalar=cf,
                                           in1=acc[:, tt * D + half * 512: tt * D + (half + 1) * 512],
                                           op0=A.mult, op1=A.add).then_inc(ve, 1)
                emit("vector", _facc)
                v_ffn_acc[(eh, tt)] = cnt.inc("ve")

            if eh + 2 < NE:
                v_we_split[eh + 2] = stage_expert(eh + 2)

        v_fin_last = None
        for tt in range(TT):
            def _fin(v, tt=tt):
                v.wait_ge(ve, v_ffn_acc[(NE - 1, tt)])
                v.drain()
                v.scalar_tensor_tensor(out=acc[:, tt * D:(tt + 1) * D], in0=boutbc[:],
                                       scalar=s_all[:, tt:tt + 1],
                                       in1=acc[:, tt * D:(tt + 1) * D], op0=A.add, op1=A.add).then_inc(ve, 1)
            emit("vector", _fin)
            v_fin_last = cnt.inc("ve")


        ph4.close()
        ph2.close()
        ph1.close()

        ph3 = ExitStack()
        def sb3(name, shape, dt):
            return ph3.enter_context(nc.sbuf_tensor(name, shape, dt))
        def ps3(name, shape, dt):
            return ph3.enter_context(nc.psum_tensor(name, shape, dt))

        wst = [sb3("wst0", [128, 256], F32), sb3("wst1", [128, 256], F32)]
        wqk_hi = sb3("wqk_hi", [128, 2048], BF16)
        wqk_lo = sb3("wqk_lo", [128, 2048], BF16)
        wv_hi = sb3("wv_hi", [128, 2048], BF16)
        wv_lo = sb3("wv_lo", [128, 2048], BF16)
        wo_hi = sb3("wo_hi", [128, D], BF16)
        wo_lo = sb3("wo_lo", [128, D], BF16)
        qt_hi = sb3("qt_hi", [128, N], BF16)
        qt_lo = sb3("qt_lo", [128, N], BF16)
        kt_hi = sb3("kt_hi", [128, N], BF16)
        kt_lo = sb3("kt_lo", [128, N], BF16)
        v_hi = sb3("v_hi", [128, TT * 256], BF16)
        v_lo = sb3("v_lo", [128, TT * 256], BF16)
        ot_hi = sb3("ot_hi", [128, N], BF16)
        ot_lo = sb3("ot_lo", [128, N], BF16)
        btab = sb3("btab_s", [128, BW], F32)
        phi = sb3("phi", [128, 2 * 1024], BF16)
        plo = sb3("plo", [128, 2 * 1024], BF16)
        l_small = sb3("l_small", [1, 512], F32)
        rlT = sb3("rlT", [128, TT], F32)
        nrm = sb3("nrm", [1, 24], F32)
        bvec = [sb3(f"bvec{i}", [128, 3], F32) for i in range(2)]
        vals = sb3("vals", [1, 3], F32)
        c_sc = sb3("c_sc", [1, 4], F32)

        scp = [ps3(f"scp{i}", [128, 1024], F32) for i in range(2)]
        psO = [ps3(f"psO{i}", [128, 512], F32) for i in range(2)]
        psl = [ps3(f"psl{i}", [128, 512], F32) for i in range(2)]

        with ph3:
            wqk_v = wqk.rearrange("(cc p) j -> p cc j", p=128)
            wv_v = wv.rearrange("(cc p) j -> p cc j", p=128)
            wout_v = wout.rearrange("(h p) j -> h p j", p=128)
            we_v = we.rearrange("e (cc p) j -> e p cc j", p=128)
            wst_v = [w[:].rearrange("p (a b) -> p a b", b=128) for w in wst]

            v_wst_free = [0, 0]
            wst_par = [0]
            first_ph3 = [True]

            def stage_weights(dst_hi, dst_lo, src_ap, width, dst3d=False, extra_tensor_wait=None):
                par = wst_par[0]
                wst_par[0] = 1 - par
                wait_ve = v_wst_free[par]
                barrier = first_ph3[0]
                first_ph3[0] = False
                cur_mm, cur_sc, cur_ve, cur_gp = (cnt.val("mm"), cnt.val("sc"), cnt.val("ve"), cnt.val("gp"))
                def _d(sy, wait_ve=wait_ve, src_ap=src_ap, width=width, dst3d=dst3d, barrier=barrier,
                       par=par, cur_mm=cur_mm, cur_sc=cur_sc, cur_ve=cur_ve, cur_gp=cur_gp):
                    if barrier:
                        sy.wait_ge(mm, cur_mm); sy.wait_ge(sc, cur_sc)
                        sy.wait_ge(ve, cur_ve); sy.wait_ge(gp, cur_gp)
                    if wait_ve:
                        sy.wait_ge(ve, wait_ve)
                    dst = wst_v[par][:, 0:width // 128, :] if dst3d else wst[par][:, 0:width]
                    sy.dma_start(out=dst, in_=src_ap).then_inc(lw[par], 16)
                emit("sync", _d)
                v_l = cnt.inc("lw%d" % par, 16)
                src_flat = wst_v[par][:, 0:width // 128, :] if dst3d else wst[par][:, 0:width]
                def _h(s, v_l=v_l, par=par, src_flat=src_flat, dst_hi=dst_hi, extra=extra_tensor_wait):
                    s.wait_ge(lw[par], v_l)
                    if extra is not None:
                        s.wait_ge(mm, extra)
                    s.copy(out=dst_hi, in_=src_flat).then_inc(sc, 1)
                emit("scalar", _h)
                v_h = cnt.inc("sc")
                def _lo(v, v_h=v_h, src_flat=src_flat, dst_hi=dst_hi, dst_lo=dst_lo):
                    v.wait_ge(sc, v_h)
                    v.drain()
                    v.scalar_tensor_tensor(out=dst_lo, in0=src_flat, scalar=1.0,
                                           in1=dst_hi, op0=A.mult, op1=A.subtract).then_inc(ve, 1)
                emit("vector", _lo)
                v_done = cnt.inc("ve")
                v_wst_free[par] = v_done
                return v_done

            wqk_hi_3 = wqk_hi[:].rearrange("p (cc two j) -> p cc two j", two=2, j=128)
            wqk_lo_3 = wqk_lo[:].rearrange("p (cc two j) -> p cc two j", two=2, j=128)
            wv_hi_3 = wv_hi[:].rearrange("p (cc two j) -> p cc two j", two=2, j=128)
            wv_lo_3 = wv_lo[:].rearrange("p (cc two j) -> p cc two j", two=2, j=128)
            wqk_src = wqk_v.rearrange("p cc j -> p cc j") if False else wqk_v

            def stage_pair_weights(hi4, lo4, src3, extra_tensor_wait=None):
                d = None
                for g in range(4):
                    ccs = slice(2 * g, 2 * g + 2)
                    for half in range(2):
                        d = stage_weights(hi4[:, ccs, half, :], lo4[:, ccs, half, :],
                                          src3[:, 2 * g:2 * g + 2, half * 128:(half + 1) * 128], 256,
                                          dst3d=True, extra_tensor_wait=extra_tensor_wait)
                return d

            v_qk_split = [None] * NP
            v_wv_split = [None] * NP
            v_wo_split = [None] * H
            v_btab_ld = [None] * H

            v_qk_split[0] = stage_pair_weights(wqk_hi_3, wqk_lo_3, wqk_v[:, :, 0:256])
            v_wv_split[0] = stage_pair_weights(wv_hi_3, wv_lo_3, wv_v[:, :, 0:256])
            def _btab0(sy):
                sy.dma_start(out=btab[:], in_=btabd[0]).then_inc(lb, 16)
            emit("sync", _btab0)
            v_btab_ld[0] = cnt.inc("lb", 16)
            def stage_wout(h, extra_tensor_wait=None):
                d = None
                for g in range(4):
                    d = stage_weights(wo_hi[:, g * 256:(g + 1) * 256], wo_lo[:, g * 256:(g + 1) * 256],
                                      wout_v[h][:, g * 256:(g + 1) * 256], 256,
                                      extra_tensor_wait=extra_tensor_wait)
                return d
            v_wo_split[0] = stage_wout(0)

            sqb = phi[:, 0:512]
            v_exp2 = {}; v_split = {}; v_pvl = {}; v_band = {}
            v_accst = {}; v_ot_evac = {}; v_rlte = {}

            def phi_prev(h, qc, kt):
                if kt >= 2:
                    return (h, qc, kt - 2)
                if qc >= 1:
                    return (h, qc - 1, KT - 2 + kt)
                if h >= 1:
                    return (h - 1, QC - 1, KT - 2 + kt)
                return None

            pair_v_evac = {}
            def emit_setup(p):
                    for tq in range(TQ):
                        qk_po = (psl[0], psl[1]) if tq % 2 == 0 else (psO[0], psO[1])
                        def _qkmm(t, p=p, tq=tq, qk_po=qk_po):
                            if tq == 0:
                                t.wait_ge(ve, v_qk_split[p])
                                if p == 0:
                                    t.wait_ge(ve, v_router_done)
                                else:
                                    t.wait_ge(ve, v_rlte[2 * p - 1])
                            elif tq >= 2:
                                t.wait_ge(ve, SEMS[("qk_evac", p, tq - 2)])
                            for blk, po in ((0, qk_po[0]), (1, qk_po[1])):
                                for cc in range(CC):
                                    lh = wqk_hi[:, cc * 256 + blk * 128: cc * 256 + (blk + 1) * 128]
                                    llo = wqk_lo[:, cc * 256 + blk * 128: cc * 256 + (blk + 1) * 128]
                                    rh = xnt_hi[:, cc * N + tq * 512: cc * N + (tq + 1) * 512]
                                    rl = xnt_lo[:, cc * N + tq * 512: cc * N + (tq + 1) * 512]
                                    t.matmul(po[:], lh, rh, start=(cc == 0), stop=False)
                                    t.matmul(po[:], lh, rl, start=False, stop=False)
                                    i3 = t.matmul(po[:], llo, rh, start=False, stop=(cc == CC - 1))
                                    if cc == CC - 1:
                                        i3.then_inc(mm, 1)
                        emit("tensor", _qkmm)
                        SEMS[("qk_mm", p, tq)] = cnt.inc("mm", 2)

                        def _qk_evh(s, p=p, tq=tq, qk_po=qk_po):
                            s.wait_ge(mm, SEMS[("qk_mm", p, tq)])
                            s.activation(out=qt_hi[:, tq * 512:(tq + 1) * 512], in_=qk_po[0][:],
                                         func=AF.Copy, scale=SCALE).then_inc(sc, 1)
                            s.activation(out=kt_hi[:, tq * 512:(tq + 1) * 512], in_=qk_po[1][:],
                                         func=AF.Copy).then_inc(sc, 1)
                        emit("scalar", _qk_evh)
                        SEMS[("qk_evh", p, tq)] = cnt.inc("sc", 2)

                        def _qk_evl(v, p=p, tq=tq, qk_po=qk_po):
                            v.wait_ge(sc, SEMS[("qk_evh", p, tq)])
                            v.drain()
                            v.scalar_tensor_tensor(out=qt_lo[:, tq * 512:(tq + 1) * 512], in0=qk_po[0][:],
                                                   scalar=SCALE, in1=qt_hi[:, tq * 512:(tq + 1) * 512],
                                                   op0=A.mult, op1=A.subtract)
                            v.drain()
                            v.scalar_tensor_tensor(out=kt_lo[:, tq * 512:(tq + 1) * 512], in0=qk_po[1][:],
                                                   scalar=1.0, in1=kt_hi[:, tq * 512:(tq + 1) * 512],
                                                   op0=A.mult, op1=A.subtract).then_inc(ve, 1)
                        emit("vector", _qk_evl)
                        SEMS[("qk_evac", p, tq)] = cnt.inc("ve")
                    v_qkT_evac = SEMS[("qk_evac", p, TQ - 1)]
                    v_qkproj_mm = SEMS[("qk_mm", p, TQ - 1)]


                    def _sqall(s, p=p):
                        s.wait_ge(ve, SEMS[("qk_evac", p, TQ - 1)])
                        if p >= 1:
                            s.wait_ge(mm, v_pvl[(2 * p - 1, QC - 1, KT // 2 - 1)])
                        s.activation(out=phi[:], in_=qt_hi[:], func=AF.Square).then_inc(sc, 1)
                        s.activation(out=plo[:], in_=kt_hi[:], func=AF.Square).then_inc(sc, 1)
                    emit("scalar", _sqall)
                    SEMS[("nrm_sq", p)] = cnt.inc("sc", 2)

                    nregions = [psO[0][0:1, :], psO[1][0:1, :]]
                    def emit_norm_i(i, p=p):
                        g, ch = divmod(i, 4)
                        ten, mask = divmod(g, 2)
                        sq_src = phi if ten == 0 else plo
                        def _nmm(t, p=p, i=i, ch=ch, mask=mask, sq_src=sq_src):
                            if i == 0:
                                t.wait_ge(sc, SEMS[("nrm_sq", p)])
                            if i >= 2:
                                t.wait_ge(ve, SEMS[("nrm_red", p, i - 2)])
                            src_ap = sq_src[:, ch * 512:(ch + 1) * 512]
                            t.matmul(nregions[i % 2], ones_hi[:] if mask == 0 else ones_lo[:],
                                     src_ap, start=True, stop=True).then_inc(mm, 1)
                        emit("tensor", _nmm)
                        SEMS[("nrm_mm", p, i)] = cnt.inc("mm")

                        def _nred(v, p=p, i=i):
                            v.wait_ge(mm, SEMS[("nrm_mm", p, i)])
                            v.reduce_max(out=nrm[0:1, i:i + 1], in_=nregions[i % 2], axis=AX.X).then_inc(ve, 1)
                        emit("vector", _nred)
                        SEMS[("nrm_red", p, i)] = cnt.inc("ve")


                    for tt in range(TT):
                        def _vmm(t, p=p, tt=tt):
                            if tt == 0:
                                t.wait_ge(ve, v_wv_split[p])
                                t.wait_ge(ve, SEMS[("qk_evac", p, TQ - 1)])
                            if tt >= 2:
                                t.wait_ge(ve, SEMS[("v_ev", p, tt - 2)])
                            po = psl[tt % 2]
                            for cc in range(CC):
                                lh = xnt_hi[:, cc * N + tt * 128: cc * N + (tt + 1) * 128]
                                llo = xnt_lo[:, cc * N + tt * 128: cc * N + (tt + 1) * 128]
                                rh = wv_hi[:, cc * 256:(cc + 1) * 256]
                                rl = wv_lo[:, cc * 256:(cc + 1) * 256]
                                t.matmul(po[:, 0:256], lh, rh, start=(cc == 0), stop=False)
                                t.matmul(po[:, 0:256], lh, rl, start=False, stop=False)
                                i3 = t.matmul(po[:, 0:256], llo, rh, start=False, stop=(cc == CC - 1))
                                if cc == CC - 1:
                                    i3.then_inc(mm, 1)
                        emit("tensor", _vmm)
                        SEMS[("v_mm", p, tt)] = cnt.inc("mm")

                        def _vevh(s, p=p, tt=tt):
                            s.wait_ge(mm, SEMS[("v_mm", p, tt)])
                            s.drain()
                            s.copy(out=v_hi[:, tt * 256:(tt + 1) * 256], in_=psl[tt % 2][:, 0:256]).then_inc(sc, 1)
                        emit("scalar", _vevh)
                        SEMS[("v_evh", p, tt)] = cnt.inc("sc")

                        def _vevl(v, p=p, tt=tt):
                            v.wait_ge(sc, SEMS[("v_evh", p, tt)])
                            v.drain()
                            v.scalar_tensor_tensor(out=v_lo[:, tt * 256:(tt + 1) * 256], in0=psl[tt % 2][:, 0:256],
                                                   scalar=1.0, in1=v_hi[:, tt * 256:(tt + 1) * 256],
                                                   op0=A.mult, op1=A.subtract).then_inc(ve, 1)
                        emit("vector", _vevl)
                        SEMS[("v_ev", p, tt)] = cnt.inc("ve")
                        if 1 <= tt <= 15:
                            emit_norm_i(tt - 1)
                    emit_norm_i(15)
                    def _ncmb(v, p=p):
                        v.wait_ge(ve, SEMS[("nrm_red", p, 15)])
                        v.drain()
                        v.reduce_max(out=nrm[0:1, 16:17], in_=nrm[0:1, 0:4], axis=AX.X)
                        v.reduce_max(out=nrm[0:1, 17:18], in_=nrm[0:1, 8:12], axis=AX.X)
                        v.reduce_max(out=nrm[0:1, 18:19], in_=nrm[0:1, 4:8], axis=AX.X)
                        v.drain()
                        v.reduce_max(out=nrm[0:1, 19:20], in_=nrm[0:1, 12:16], axis=AX.X).then_inc(ve, 1)
                    emit("vector", _ncmb)
                    v_nrm_done = cnt.inc("ve")

                    v_v_evac = SEMS[("v_ev", p, TT - 1)]
                    v_vproj_mm = SEMS[("v_mm", p, TT - 1)]


                    if debug and p == 0:
                        def _dbg_v(sy, v_v_evac=v_v_evac):
                            sy.wait_ge(ve, v_v_evac)
                            sy.dma_start(out=dbg["d_v"], in_=v_hi[:]).then_inc(sg, 16)
                        emit("gpsimd", _dbg_v)
                        cnt.inc("sg", 16)

                    for hh in range(2):
                        h = 2 * p + hh
                        def _chv(v, h=h, hh=hh, v_nrm_done=v_nrm_done):
                            v.wait_ge(ve, v_nrm_done)
                            if h == 0:
                                v.wait_ge(ld, v_const_ld)
                            if h >= 2:
                                v.wait_ge(sc, v_exp2[(h - 2, QC - 1, KT // 2 - 1)])
                            v.drain()
                            v.reduce_sum(out=c_sc[0:1, 2:3], in_=nrm[0:1, 16 + 2 * hh:18 + 2 * hh], axis=AX.X)
                            v.drain()
                            v.reduce_max(out=c_sc[0:1, 3:4], in_=convf[0:1, h * 21:(h + 1) * 21], axis=AX.X)
                            v.drain()
                            v.tensor_scalar(out=c_sc[0:1, 2:3], in0=c_sc[0:1, 2:3], scalar1=0.5,
                                            scalar2=c_sc[0:1, 3:4], op0=A.mult, op1=A.add)
                            v.drain()
                            v.tensor_scalar(out=c_sc[0:1, 2:3], in0=c_sc[0:1, 2:3], scalar1=1.0,
                                            scalar2=None, op0=A.add)
                            v.drain()
                            v.tensor_scalar(out=vals[0:1, :], in0=biassel[0:1, 3 * h:3 * h + 3],
                                            scalar1=c_sc[0:1, 2:3], scalar2=None, op0=A.subtract).then_inc(ve, 1)
                        emit("vector", _chv)
                        SEMS[("ch_v", h)] = cnt.inc("ve")

                        def _chmm(t, h=h):
                            t.wait_ge(ve, SEMS[("ch_v", h)])
                            t.matmul(psl[0][:, 0:3], onesf[0:1, :], vals[0:1, :], start=True, stop=True).then_inc(mm, 1)
                        emit("tensor", _chmm)
                        SEMS[("ch_mm", h)] = cnt.inc("mm")

                        def _chev(v, h=h, hh=hh):
                            v.wait_ge(mm, SEMS[("ch_mm", h)])
                            v.drain()
                            v.tensor_copy(out=bvec[hh][:], in_=psl[0][:, 0:3]).then_inc(ve, 1)
                        emit("vector", _chev)
                        SEMS[("ch", h)] = cnt.inc("ve")

                    if debug and p == 0:
                        def _dbg_ch(sy):
                            sy.wait_ge(ve, SEMS[("ch", 1)])
                            sy.dma_start(out=dbg["d_qt"], in_=qt_hi[:]).then_inc(sg, 16)
                            sy.dma_start(out=dbg["d_kt"], in_=kt_hi[:]).then_inc(sg, 16)
                        emit("gpsimd", _dbg_ch)
                        cnt.inc("sg", 32)


                    if p + 1 < NP:
                        v_qk_split[p + 1] = stage_pair_weights(
                            wqk_hi_3, wqk_lo_3, wqk_v[:, :, (p + 1) * 256:(p + 2) * 256],
                            extra_tensor_wait=v_qkproj_mm)
                        v_wv_split[p + 1] = stage_pair_weights(
                            wv_hi_3, wv_lo_3, wv_v[:, :, (p + 1) * 256:(p + 2) * 256],
                            extra_tensor_wait=v_vproj_mm)
                    pair_v_evac[p] = v_v_evac
                    return v_v_evac

            for p in range(NP):
                h0, h1 = 2 * p, 2 * p + 1
                if p == 0:
                    v_v_evac0 = emit_setup(0)
                v_v_evac = pair_v_evac[p]

                for hh in range(2):
                    h = 2 * p + hh
                    qthi = qt_hi[64 * hh: 64 + 64 * hh, :]
                    qtlo = qt_lo[64 * hh: 64 + 64 * hh, :]
                    kthi = kt_hi[64 * hh: 64 + 64 * hh, :]
                    ktlo = kt_lo[64 * hh: 64 + 64 * hh, :]
                    scph = (scp[0], scp[1])

                    def emit_evac_rlt(qc, h=h, hh=hh):
                            def _rlt_mm(t, h=h, hh=hh, qc=qc):
                                t.wait_ge(ve, SEMS[("l_rel", h, qc)])
                                for i in range(4):
                                    t.matmul(psl[qc % 2][:, 100 + i: 101 + i],
                                             l_small[0:1, i * 128:(i + 1) * 128],
                                             onesf[0:1, 0:1], start=True, stop=True).then_inc(mm, 1)
                            emit("tensor", _rlt_mm)
                            SEMS[("rlt_mm", h, qc)] = cnt.inc("mm", 4)

                            def _rlt_ev(v, h=h, hh=hh, qc=qc):
                                v.wait_ge(mm, SEMS[("rlt_mm", h, qc)])
                                v.drain()
                                v.tensor_copy(out=rlT[:, qc * 4:(qc + 1) * 4],
                                              in_=psl[qc % 2][:, 100:104])
                                v.drain()
                                v.reciprocal(out=rlT[:, qc * 4:(qc + 1) * 4],
                                             in_=rlT[:, qc * 4:(qc + 1) * 4]).then_inc(ve, 1)
                            emit("vector", _rlt_ev)
                            SEMS[("rlt_ev", h, qc)] = cnt.inc("ve")

                    def emit_evac(qc, h=h, hh=hh):
                            def _otla(v, h=h, hh=hh, qc=qc):
                                v.wait_ge(mm, v_pvl[(h, qc, KT // 2 - 1)])
                                v.drain()
                                v.tensor_copy(out=l_small[0:1, :], in_=psl[qc % 2][0:1, :]).then_inc(ve, 1)
                            emit("vector", _otla)
                            SEMS[("l_rel", h, qc)] = cnt.inc("ve")

                            def _oth(s, h=h, hh=hh, qc=qc):
                                s.wait_ge(mm, v_pvl[(h, qc, KT // 2 - 1)])
                                s.drain()
                                s.copy(out=ot_hi[:, qc * 512:(qc + 1) * 512], in_=psO[qc % 2][:]).then_inc(sc, 1)
                            emit("scalar", _oth)
                            SEMS[("ot_h", h, qc)] = cnt.inc("sc")

                            def _otl(v, h=h, hh=hh, qc=qc):
                                v.wait_ge(sc, SEMS[("ot_h", h, qc)])
                                v.drain()
                                v.scalar_tensor_tensor(out=ot_lo[:, qc * 512:(qc + 1) * 512], in0=psO[qc % 2][:],
                                                       scalar=1.0, in1=ot_hi[:, qc * 512:(qc + 1) * 512],
                                                       op0=A.mult, op1=A.subtract).then_inc(ve, 1)
                            emit("vector", _otl)
                            v_ot_evac[(h, qc)] = cnt.inc("ve")

                    v_v_evac_pair = v_v_evac
                    evac_defer = []
                    rlt_defer = []
                    sched = [(qc_, ktp_) for qc_ in range(QC) for ktp_ in range(KT // 2)]
                    pv_defer = []
                    for qc, ktp in sched:
                        if True:
                            while rlt_defer:
                                emit_evac_rlt(rlt_defer.pop(0))
                            while evac_defer:
                                _q2 = evac_defer.pop(0)
                                emit_evac(_q2)
                                rlt_defer.append(_q2)
                            kt0, kt1 = 2 * ktp, 2 * ktp + 1
                            parp = ktp % 2
                            po = scph[parp]
                            cls0, cls1 = tile_class(qc, kt0), tile_class(qc, kt1)
                            if ktp >= 2:
                                prev_sc = (h, qc, ktp - 2)
                            elif qc >= 1:
                                prev_sc = (h, qc - 1, KT // 2 - 2 + ktp)
                            elif h >= 1:
                                prev_sc = (h - 1, QC - 1, KT // 2 - 2 + ktp)
                            else:
                                prev_sc = None
                            if ktp >= 2:
                                prev_p = (h, qc, ktp - 2)
                            elif qc >= 1:
                                prev_p = (h, qc - 1, KT // 2 - 2 + ktp)
                            elif h >= 1:
                                prev_p = (h - 1, QC - 1, KT // 2 - 2 + ktp)
                            else:
                                prev_p = None

                            def _smm(t, h=h, hh=hh, qc=qc, kt0=kt0, kt1=kt1, po=po, prev_sc=prev_sc,
                                     qthi=qthi, qtlo=qtlo, kthi=kthi, ktlo=ktlo, v_v_evac=v_v_evac):
                                if qc == 0 and kt0 == 0 and h >= 1:
                                    t.wait_ge(ve, v_accst[(h - 1, TT - 1)])
                                if prev_sc is not None:
                                    t.wait_ge(ve, v_split[prev_sc])
                                tp = (64 * hh, 0)
                                rh = qthi[:, qc * 512:(qc + 1) * 512]
                                rl = qtlo[:, qc * 512:(qc + 1) * 512]
                                for half, kt in ((0, kt0), (1, kt1)):
                                    lh = kthi[:, kt * 128:(kt + 1) * 128]
                                    llo = ktlo[:, kt * 128:(kt + 1) * 128]
                                    ph = po[:, half * 512:(half + 1) * 512]
                                    t.matmul(ph, lh, rh, start=True, stop=False, tile_position=tp)
                                    t.matmul(ph, lh, rl, start=False, stop=False, tile_position=tp)
                                    t.matmul(ph, llo, rh, start=False, stop=True, tile_position=tp).then_inc(mm, 1)
                            emit("tensor", _smm)
                            SEMS[("sc_mm", h, qc, ktp)] = cnt.inc("mm", 2)

                            nband = (1 if cls0 == 0 else 0) + (1 if cls1 == 0 else 0)
                            if nband:
                                def _badd(v, h=h, qc=qc, kt0=kt0, kt1=kt1, po=po, ktp=ktp,
                                          cls0=cls0, cls1=cls1):
                                    v.wait_ge(mm, SEMS[("sc_mm", h, qc, ktp)])
                                    if (qc, kt0) == FIRST_BAND or (qc, kt1) == FIRST_BAND:
                                        v.wait_ge(lb, v_btab_ld[h])
                                    first = True
                                    for half, kt, cls in ((0, kt0, cls0), (1, kt1, cls1)):
                                        if cls != 0:
                                            continue
                                        d0 = kt * 128 - qc * 512
                                        off = U0 - d0
                                        if first:
                                            v.drain()
                                            first = False
                                        ph = po[:, half * 512:(half + 1) * 512]
                                        v.tensor_tensor(out=ph, in0=ph,
                                                        in1=btab[:, off:off + 512], op=A.add).then_inc(ve, 1)
                                emit("vector", _badd)
                                v_badd = cnt.inc("ve", nband)
                                wait_exp = ("ve", v_badd)
                            else:
                                wait_exp = ("mm", SEMS[("sc_mm", h, qc, ktp)])

                            def _exp(s, h=h, hh=hh, qc=qc, ktp=ktp, po=po, parp=parp,
                                     wait_exp=wait_exp, cls0=cls0, cls1=cls1, prev_p=prev_p):
                                eng_s, v_w = wait_exp
                                s.wait_ge({"ve": ve, "mm": mm}[eng_s], v_w)
                                if qc == 0 and ktp == 0:
                                    s.wait_ge(ve, SEMS[("ch", h)])
                                if prev_p is not None:
                                    s.wait_ge(mm, v_pvl[prev_p])
                                b0 = 0 if cls0 == 0 else cls0
                                b1 = 0 if cls1 == 0 else cls1
                                if b0 == b1:
                                    s.activation(out=po[:], in_=po[:], func=AF.Exp,
                                                 bias=bvec[hh][:, b0:b0 + 1])
                                else:
                                    s.activation(out=po[:, 0:512], in_=po[:, 0:512], func=AF.Exp,
                                                 bias=bvec[hh][:, b0:b0 + 1])
                                    s.activation(out=po[:, 512:1024], in_=po[:, 512:1024], func=AF.Exp,
                                                 bias=bvec[hh][:, b1:b1 + 1])
                                s.drain()
                                s.copy(out=phi[:, parp * 1024:(parp + 1) * 1024], in_=po[:]).then_inc(sc, 1)
                            emit("scalar", _exp)
                            v_exp2[(h, qc, ktp)] = cnt.inc("sc")

                            def _spl(v, h=h, qc=qc, ktp=ktp, parp=parp, po=po, prev_p=prev_p):
                                v.wait_ge(sc, v_exp2[(h, qc, ktp)])
                                if prev_p is not None:
                                    v.wait_ge(mm, v_pvl[prev_p])
                                v.drain()
                                v.scalar_tensor_tensor(out=plo[:, parp * 1024:(parp + 1) * 1024],
                                                       in0=po[:], scalar=1.0,
                                                       in1=phi[:, parp * 1024:(parp + 1) * 1024],
                                                       op0=A.mult, op1=A.subtract).then_inc(ve, 1)
                            emit("vector", _spl)
                            v_split[(h, qc, ktp)] = cnt.inc("ve")

                            def _mk_pv(h=h, hh=hh, qc=qc, ktp=ktp, kt0=kt0, kt1=kt1, parp=parp,
                                       v_v_evac_pair=v_v_evac_pair):
                              def _pv(t, h=h, hh=hh, qc=qc, ktp=ktp, kt0=kt0, kt1=kt1, parp=parp,
                                      v_v_evac_pair=v_v_evac_pair):
                                  t.wait_ge(ve, v_split[(h, qc, ktp)])
                                  if ktp == 0 and qc == 0:
                                      t.wait_ge(ve, v_v_evac_pair)
                                  if ktp == 0 and qc >= 2:
                                      t.wait_ge(ve, SEMS[("rlt_ev", h, qc - 2)])
                                      t.wait_ge(ve, v_ot_evac[(h, qc - 2)])
                                  for half, kt in ((0, kt0), (1, kt1)):
                                      vh = v_hi[:, kt * 256 + hh * 128: kt * 256 + (hh + 1) * 128]
                                      vl = v_lo[:, kt * 256 + hh * 128: kt * 256 + (hh + 1) * 128]
                                      ph_ = phi[:, parp * 1024 + half * 512: parp * 1024 + (half + 1) * 512]
                                      pl_ = plo[:, parp * 1024 + half * 512: parp * 1024 + (half + 1) * 512]
                                      st0 = (kt == 0)
                                      pso_ = psO[qc % 2]
                                      t.matmul(pso_[:], vh, ph_, start=st0, stop=False)
                                      t.matmul(pso_[:], vh, pl_, start=False, stop=False)
                                      t.matmul(pso_[:], vl, ph_, start=False, stop=(kt == KT - 1))
                                      psl_ = psl[qc % 2]
                                      t.matmul(psl_[0:1, :], onesb[:], ph_, start=st0, stop=False)
                                      t.matmul(psl_[0:1, :], onesb[:], pl_,
                                               start=False, stop=(kt == KT - 1)).then_inc(mm, 1)
                              emit("tensor", _pv)
                              v_pvl[(h, qc, ktp)] = cnt.inc("mm", 2)
                            pv_defer.append((qc, ktp, _mk_pv))
                            if len(pv_defer) > 2:
                                _q, _k, _f = pv_defer.pop(0)
                                _f()
                                if _k == KT // 2 - 1:
                                    evac_defer.append(_q)
                    while pv_defer:
                        _q, _k, _f = pv_defer.pop(0)
                        _f()
                        if _k == KT // 2 - 1:
                            evac_defer.append(_q)
                    while rlt_defer:
                        emit_evac_rlt(rlt_defer.pop(0))
                    while evac_defer:
                        _q2 = evac_defer.pop(0)
                        emit_evac(_q2)
                        emit_evac_rlt(_q2)

                    bands_h = [v_band[k] for k in v_band if k[0] == h and v_band[k] is not None]
                    v_band_last_h = max(bands_h) if bands_h else cnt.val("ve")
                    if h + 1 < H:
                        def _btd(sy, h=h, v_band_last_h=v_band_last_h):
                            sy.wait_ge(ve, v_band_last_h)
                            sy.dma_start(out=btab[:], in_=btabd[h + 1]).then_inc(lb, 16)
                        emit("sync", _btd)
                        v_btab_ld[h + 1] = cnt.inc("lb", 16)

                    v_rlte[h] = SEMS[("rlt_ev", h, QC - 1)]

                    if debug and h == 0:
                        def _dbg_l(sy):
                            sy.wait_ge(ve, v_rlte[0])
                            sy.dma_start(out=dbg["d_l"], in_=rlT[:]).then_inc(sg, 16)
                        emit("gpsimd", _dbg_l)
                        cnt.inc("sg", 16)

                    if hh == 1 and p + 1 < NP:
                        emit_setup(p + 1)

                    for tt in range(TT):
                        poj = (scp[tt % 2][:, 0:512], scp[tt % 2][:, 512:1024])
                        def _omm(t, hh=hh, h=h, tt=tt, poj=poj):
                            if tt == 0:
                                t.wait_ge(ve, v_wo_split[h])
                                t.wait_ge(ve, v_split[(h, QC - 1, KT // 2 - 1)])
                                if h == 0:
                                    t.wait_ge(gp, v_xnf_all)
                            if tt >= 2:
                                t.wait_ge(ve, v_accst[(h, tt - 2)])
                            elif h >= 1:
                                t.wait_ge(ve, v_accst[(h - 1, TT - 2 + tt)])
                            oth_ = ot_hi[:, tt * 128:(tt + 1) * 128]
                            otl_ = ot_lo[:, tt * 128:(tt + 1) * 128]
                            for half in range(2):
                                po2 = poj[half]
                                wh = wo_hi[:, half * 512:(half + 1) * 512]
                                wl = wo_lo[:, half * 512:(half + 1) * 512]
                                t.matmul(po2, oth_, wh, start=True, stop=False)
                                t.matmul(po2, oth_, wl, start=False, stop=False)
                                t.matmul(po2, otl_, wh, start=False, stop=True).then_inc(mm, 1)
                        emit("tensor", _omm)
                        SEMS[("op_mm", h, tt)] = cnt.inc("mm", 2)

                        def _oacc(v, hh=hh, h=h, tt=tt):
                            v.wait_ge(mm, SEMS[("op_mm", h, tt)])
                            v.drain()
                            v.scalar_tensor_tensor(out=acc[:, tt * D: (tt + 1) * D],
                                                   in0=scp[tt % 2][:, 0:1024], scalar=rlT[:, tt:tt + 1],
                                                   in1=acc[:, tt * D: (tt + 1) * D],
                                                   op0=A.mult, op1=A.add).then_inc(ve, 1)
                        emit("vector", _oacc)
                        v_accst[(h, tt)] = cnt.inc("ve", 1)

                    if h + 1 < H:
                        v_wo_split[h + 1] = stage_wout(h + 1, extra_tensor_wait=SEMS[("op_mm", h, TT - 1)])

                    if debug and h == 0:
                        def _dbg_ot(sy):
                            sy.wait_ge(ve, v_accst[(0, TT - 1)])
                            sy.dma_start(out=dbg["d_ot"], in_=ot_hi[:]).then_inc(sg, 16)
                        emit("gpsimd", _dbg_ot)
                        cnt.inc("sg", 16)

            v_attn_done = v_accst[(H - 1, TT - 1)]
            if phases == 3:
                return _truncate_store()

        if debug:
            def _dbg_acc(g):
                g.wait_ge(ve, v_attn_done)
                g.dma_start(out=dbg["d_accattn"], in_=acc[:]).then_inc(sg, 16)
            emit("gpsimd", _dbg_acc)
        v_dbgacc = cnt.inc("sg", 16) if debug else 0

        def _store(sy, v_fin_last=v_fin_last):
            sy.wait_ge(ve, v_accst[(H - 1, TT - 1)])
            sy.wait_ge(ve, v_fin_last)
            sy.dma_start(out=out_t, in_=acc[:].rearrange("p (tt j) -> p tt j", j=D)).then_inc(st, 16)
        emit("sync", _store)
        cnt.inc("st", 16)

        def _final_wait(sy):
            sy.wait_ge(st, cnt.val("st"))
            if cnt.val("sg"):
                sy.wait_ge(sg, cnt.val("sg"))
        emit("sync", _final_wait)

        with nc.Block() as block:
            @block.sync
            def _(eng):
                for f in ops["sync"]:
                    f(eng)
            @block.tensor
            def _(eng):
                for f in ops["tensor"]:
                    f(eng)
            @block.scalar
            def _(eng):
                for f in ops["scalar"]:
                    f(eng)
            @block.vector
            def _(eng):
                for f in ops["vector"]:
                    f(eng)
            @block.gpsimd
            def _(eng):
                for f in ops["gpsimd"]:
                    f(eng)

    nc._dbg_cnt = dict(cnt.c)
    return nc


def prep_inputs(L0_attn_x, L0_attn_W_qk, L0_attn_W_v, L0_attn_W_out, L0_attn_b_out,
                L0_attn_conv, L0_ffn_router_w, L0_ffn_expert_weights):
    Wqk = np.ascontiguousarray(np.asarray(L0_attn_W_qk), dtype=np.float32)
    cols = []
    for p in range(NP):
        cols.append(Wqk[:, (2 * p) * 64:(2 * p + 2) * 64])
        cols.append(Wqk[:, 512 + (2 * p) * 64: 512 + (2 * p + 2) * 64])
    wqk_perm = np.ascontiguousarray(np.concatenate(cols, axis=1))

    conv = np.asarray(L0_attn_conv, dtype=np.float32)
    r = np.arange(128)[:, None]
    u = np.arange(BW)[None, :]
    idx = np.clip(r - u + U0, -10, 10) + 10
    btab = np.ascontiguousarray(conv[:, idx])
    biassel = np.ascontiguousarray(
        np.stack([np.zeros(H, np.float32), conv[:, 0], conv[:, 20]], axis=1).reshape(1, 3 * H))
    convf = np.ascontiguousarray(conv.reshape(1, H * 21))
    ident = np.eye(128, dtype=ml_dtypes.bfloat16)

    common = {
        "wqk": wqk_perm,
        "wv": np.ascontiguousarray(np.asarray(L0_attn_W_v), dtype=np.float32),
        "wout": np.ascontiguousarray(np.asarray(L0_attn_W_out), dtype=np.float32),
        "bout": np.ascontiguousarray(np.asarray(L0_attn_b_out, dtype=np.float32).reshape(1, D)),
        "rw": np.ascontiguousarray(np.asarray(L0_ffn_router_w, dtype=np.float32).T),
        "we": np.ascontiguousarray(np.asarray(L0_ffn_expert_weights), dtype=np.float32),
        "btab": btab,
        "biassel": biassel,
        "convf": convf,
        "ident": ident,
    }
    xs = np.asarray(L0_attn_x, dtype=np.float32)
    in_maps = []
    for b in range(xs.shape[0]):
        m = dict(common)
        m["x"] = np.ascontiguousarray(xs[b])
        in_maps.append(m)
    return in_maps


def kernel(L0_attn_x, L0_attn_mask, L0_attn_W_qk, L0_attn_W_v, L0_attn_W_out,
           L0_attn_b_out, L0_attn_conv, L0_ffn_router_w, L0_ffn_expert_weights):
    in_maps = prep_inputs(L0_attn_x, L0_attn_W_qk, L0_attn_W_v, L0_attn_W_out,
                          L0_attn_b_out, L0_attn_conv, L0_ffn_router_w, L0_ffn_expert_weights)
    nc = build_program(debug=False)
    res = run_bass_kernel_spmd(nc, in_maps, core_ids=list(range(8)))
    return np.stack([r["out"] for r in res.results], axis=0).astype(np.float32)


# version 26
# speedup vs baseline: 1.0464x; 1.0464x over previous
import numpy as np
from contextlib import ExitStack
import concourse.bass as bass
import concourse.mybir as mybir
from concourse.bass_utils import run_bass_kernel_spmd
import ml_dtypes

F32 = mybir.dt.float32
BF16 = mybir.dt.bfloat16
A = mybir.AluOpType
AX = mybir.AxisListType
AF = mybir.ActivationFunctionType

N = 2048
D = 1024
H = 8
E = 8
TT = 16
CC = 8
TQ = 4
QC = 4
KT = 16
NP = 4
SCALE = 0.125
EPS = 1e-5
U0 = 521
BW = 1176


def tile_class(qc, kt):
    d0 = kt * 128 - qc * 512
    if d0 >= 521:
        return 2
    if d0 <= -137:
        return 1
    return 0

FIRST_BAND = None
for _qc in range(QC):
    for _kt in range(KT):
        if tile_class(_qc, _kt) == 0:
            FIRST_BAND = (_qc, _kt)
            break
    if FIRST_BAND:
        break


class Counter:
    def __init__(self):
        self.c = {}
    def inc(self, name, k=1):
        self.c[name] = self.c.get(name, 0) + k
        return self.c[name]
    def val(self, name):
        return self.c.get(name, 0)


def build_program(debug=False, phases=4):
    nc = bass.Bass("TRN2", target_bir_lowering=False)

    x = nc.dram_tensor("x", [N, D], F32, kind="ExternalInput").ap()
    wqk = nc.dram_tensor("wqk", [D, D], F32, kind="ExternalInput").ap()
    wv = nc.dram_tensor("wv", [D, D], F32, kind="ExternalInput").ap()
    wout = nc.dram_tensor("wout", [D, D], F32, kind="ExternalInput").ap()
    bout = nc.dram_tensor("bout", [1, D], F32, kind="ExternalInput").ap()
    rw = nc.dram_tensor("rw", [D, E], F32, kind="ExternalInput").ap()
    we = nc.dram_tensor("we", [E, D, D], F32, kind="ExternalInput").ap()
    btabd = nc.dram_tensor("btab", [H, 128, BW], F32, kind="ExternalInput").ap()
    biasseld = nc.dram_tensor("biassel", [1, 3 * H], F32, kind="ExternalInput").ap()
    convfd = nc.dram_tensor("convf", [1, 21 * H], F32, kind="ExternalInput").ap()
    identd = nc.dram_tensor("ident", [128, 128], BF16, kind="ExternalInput").ap()
    out = nc.dram_tensor("out", [N, D], F32, kind="ExternalOutput").ap()

    dbg = {}
    if debug:
        for nm, shp in [("d_xnh", [128, CC * N]), ("d_xnl", [128, CC * N]),
                        ("d_coef", [128, TT * E]), ("d_s", [128, TT]),
                        ("d_qt", [128, N]), ("d_kt", [128, N]),
                        ("d_v", [128, TT * 256]), ("d_l", [1, N]),
                        ("d_ot", [128, N]), ("d_accattn", [128, TT * D])]:
            dbg[nm] = nc.dram_tensor(nm, shp, F32, kind="ExternalOutput").ap()

    x_t = x.rearrange("(tt p) j -> p tt j", p=128)
    out_t = out.rearrange("(tt p) j -> p tt j", p=128)

    ops = {k: [] for k in ("sync", "tensor", "scalar", "vector", "gpsimd")}
    def emit(eng, fn):
        ops[eng].append(fn)
    cnt = Counter()

    es = ExitStack()
    def sem(name):
        return es.enter_context(nc.semaphore(name))

    with es:
        ld = sem("ld"); mm = sem("mm"); sc = sem("sc")
        ve = sem("ve"); gp = sem("gp"); st = sem("st"); sg = sem("sg")
        lr = sem("lr"); lw = [sem("lw0"), sem("lw1")]; lb = sem("lb")
        le = [sem("le0"), sem("le1"), sem("le2"), sem("le3")]
        la = [sem("la0"), sem("la1"), sem("la2"), sem("la3")]
        SEMS = {}

        acc = es.enter_context(nc.sbuf_tensor("acc", [128, TT * D], F32))
        xnt_hi = es.enter_context(nc.sbuf_tensor("xnt_hi", [128, CC * N], BF16))
        xnt_lo = es.enter_context(nc.sbuf_tensor("xnt_lo", [128, CC * N], BF16))
        onesb = es.enter_context(nc.sbuf_tensor("onesb", [128, 1], BF16))
        ones_hi = es.enter_context(nc.sbuf_tensor("ones_hi", [128, 1], BF16))
        ones_lo = es.enter_context(nc.sbuf_tensor("ones_lo", [128, 1], BF16))
        onesf = es.enter_context(nc.sbuf_tensor("onesf", [1, 128], F32))
        convf = es.enter_context(nc.sbuf_tensor("convf_s", [1, 21 * H], F32))
        biassel = es.enter_context(nc.sbuf_tensor("biassel_s", [1, 3 * H], F32))
        coef = es.enter_context(nc.sbuf_tensor("coef", [128, TT * E], F32))
        s_all = es.enter_context(nc.sbuf_tensor("s_all", [128, TT], F32))

        acc_3d = acc[:].rearrange("p (tt j) -> p tt j", j=D)
        def _init_dma(sy):
            for i in range(4):
                sy.dma_start(out=acc_3d[:, 4 * i:4 * i + 4, :],
                             in_=x_t[:, 4 * i:4 * i + 4, :]).then_inc(la[i], 16)
            sy.dma_start(out=ident[:], in_=identd).then_inc(ld, 16)
            sy.dma_start(out=convf[:], in_=convfd).then_inc(ld, 16)
            sy.dma_start(out=biassel[:], in_=biasseld).then_inc(ld, 16)
        emit("sync", _init_dma)
        v_const_ld = cnt.inc("ld", 16 * 3)
        v_acc_ld = [cnt.inc("la%d" % i, 16) for i in range(4)]

        def _init_vec(v):
            v.memset(onesb[:], 1.0)
            v.memset(ones_hi[0:64, :], 1.0)
            v.memset(ones_hi[64:128, :], 0.0)
            v.memset(ones_lo[0:64, :], 0.0)
            v.memset(ones_lo[64:128, :], 1.0)
            v.drain()
            v.memset(onesf[:], 1.0).then_inc(ve, 1)
        emit("vector", _init_vec)
        v_ones = cnt.inc("ve")

        ph1 = ExitStack()
        ident = ph1.enter_context(nc.sbuf_tensor("ident_s", [128, 128], BF16))
        sum_t = [ph1.enter_context(nc.sbuf_tensor(f"sum{i}", [128, 1], F32)) for i in range(4)]
        ssq_t = [ph1.enter_context(nc.sbuf_tensor(f"ssq{i}", [128, 1], F32)) for i in range(4)]
        mean_t = [ph1.enter_context(nc.sbuf_tensor(f"mean{i}", [128, 1], F32)) for i in range(4)]
        rstd_t = [ph1.enter_context(nc.sbuf_tensor(f"rstd{i}", [128, 1], F32)) for i in range(4)]
        nmr_t = [ph1.enter_context(nc.sbuf_tensor(f"nmr{i}", [128, 1], F32)) for i in range(4)]
        tmp1 = [ph1.enter_context(nc.sbuf_tensor(f"ltmp{i}", [128, 1], F32)) for i in range(4)]
        xnf = [ph1.enter_context(nc.sbuf_tensor(f"xnf{i}", [128, D], F32)) for i in range(2)]
        tsbh = [ph1.enter_context(nc.sbuf_tensor(f"tsbh{i}", [128, D], BF16)) for i in range(2)]
        tsbl = [ph1.enter_context(nc.sbuf_tensor(f"tsbl{i}", [128, D], BF16)) for i in range(2)]
        pth = ph1.enter_context(nc.psum_tensor("pth", [128, D], BF16))
        ptl = ph1.enter_context(nc.psum_tensor("ptl", [128, D], BF16))

        xnt_hi_4d = xnt_hi[:].rearrange("p (cc n t) -> p cc n t", cc=CC, n=TT)
        xnt_lo_4d = xnt_lo[:].rearrange("p (cc n t) -> p cc n t", cc=CC, n=TT)

        ph2 = ExitStack()
        rwf = ph2.enter_context(nc.sbuf_tensor("rwf", [128, CC * E], F32))
        rwh = ph2.enter_context(nc.sbuf_tensor("rwh", [128, CC * E], BF16))
        rwl = ph2.enter_context(nc.sbuf_tensor("rwl", [128, CC * E], BF16))
        prt = ph2.enter_context(nc.psum_tensor("prt", [128, 1024], F32))
        r_m = [ph2.enter_context(nc.sbuf_tensor(f"rm_{i}", [128, 1], F32)) for i in range(4)]
        e8b = ph2.enter_context(nc.sbuf_tensor("e8b", [128, 2 * 4 * E], F32))
        rpb = ph2.enter_context(nc.sbuf_tensor("rpb", [128, 2 * 4 * E], F32))
        t8b = ph2.enter_context(nc.sbuf_tensor("t8b", [128, 2 * 4 * E], F32))
        rdenb = ph2.enter_context(nc.sbuf_tensor("rdenb", [128, 8], F32))
        mb = ph2.enter_context(nc.sbuf_tensor("mb", [128, 8], F32))
        e8v = e8b[:].rearrange("p (b i e) -> p b i e", b=2, i=4)
        rpv = rpb[:].rearrange("p (b i e) -> p b i e", b=2, i=4)
        t8v = t8b[:].rearrange("p (b i e) -> p b i e", b=2, i=4)

        def _rw_dma(sy):
            sy.dma_start(out=rwf[:].rearrange("p (cc e) -> p cc e", cc=CC),
                         in_=rw.rearrange("(cc p) e -> p cc e", p=128)).then_inc(lr, 16)
        emit("sync", _rw_dma)
        v_rw_ld = cnt.inc("lr", 16)

        def _rw_hi(s):
            s.wait_ge(lr, v_rw_ld)
            s.copy(out=rwh[:], in_=rwf[:]).then_inc(sc, 1)
        emit("scalar", _rw_hi)
        v_rwh = cnt.inc("sc")

        def _rw_lo(v):
            v.wait_ge(sc, v_rwh)
            v.tensor_tensor(out=rwl[:], in0=rwf[:], in1=rwh[:], op=A.subtract).then_inc(ve, 1)
        emit("vector", _rw_lo)
        v_rwl = cnt.inc("ve")

        def emit_router_mm(tt):
            par2 = tt % 2
            def _rt_mm(t, tt=tt, par=par2):
                t.wait_ge(sc, SEMS[("ln_evac", tt)])
                if tt == 0:
                    t.wait_ge(ve, v_rwl)
                if tt >= 2:
                    t.wait_ge(sc, SEMS[("rt_read", tt - 2)])
                po = prt[:, par * 512: par * 512 + E]
                for cc in range(CC):
                    lh = xnt_hi[:, cc * N + tt * 128: cc * N + (tt + 1) * 128]
                    ll = xnt_lo[:, cc * N + tt * 128: cc * N + (tt + 1) * 128]
                    rh = rwh[:, cc * E:(cc + 1) * E]
                    rl = rwl[:, cc * E:(cc + 1) * E]
                    t.matmul(po, lh, rh, start=(cc == 0), stop=False)
                    t.matmul(po, lh, rl, start=False, stop=False)
                    i3 = t.matmul(po, ll, rh, start=False, stop=(cc == CC - 1))
                    if cc == CC - 1:
                        i3.then_inc(mm, 1)
            emit("tensor", _rt_mm)
            SEMS[("rt_mm", tt)] = cnt.inc("mm")

        def emit_router_c1(tt):
            par = tt % 4
            par2 = tt % 2
            def _rt_v1(v, tt=tt, par=par, par2=par2):
                v.wait_ge(mm, SEMS[("rt_mm", tt)])
                if tt >= 4:
                    v.wait_ge(sc, SEMS[("rt_read", tt - 4)])
                po = prt[:, par2 * 512: par2 * 512 + E]
                v.drain()
                v.reduce_max(out=r_m[par][:], in_=po, axis=AX.X, negate=True).then_inc(ve, 1)
            emit("vector", _rt_v1)
            SEMS[("rt_negm", tt)] = cnt.inc("ve")

            def _rt_s1(s, tt=tt, par=par, par2=par2):
                s.wait_ge(ve, SEMS[("rt_negm", tt)])
                b, i = tt // 4, tt % 4
                if b >= 2:
                    s.wait_ge(ve, SEMS[("rt_doneB", b - 2)])
                po = prt[:, par2 * 512: par2 * 512 + E]
                s.activation(out=e8v[:, b % 2, i, :], in_=po, func=AF.Exp,
                             bias=r_m[par][:, 0:1],
                             accum_out=rdenb[:, (b % 2) * 4 + i: (b % 2) * 4 + i + 1]).then_inc(sc, 1)
            emit("scalar", _rt_s1)
            SEMS[("rt_read", tt)] = cnt.inc("sc")

        def emit_router_batch(b):
            par = b % 2
            def _rt_vb(v, b=b, par=par):
                v.wait_ge(sc, SEMS[("rt_read", 4 * b + 3)])
                e8s = e8v[:, par]
                rps = rpv[:, par]
                t8s = t8v[:, par]
                rden4 = rdenb[:, par * 4:(par + 1) * 4]
                m4 = mb[:, par * 4:(par + 1) * 4]
                m4b = m4[:, :, None].broadcast_to([128, 4, E])
                rden4b = rden4[:, :, None].broadcast_to([128, 4, E])
                cfv = coef[:, 4 * b * E: (4 * b + 4) * E].rearrange("p (i e) -> p i e", i=4)
                v.drain()
                v.reciprocal(out=rden4, in_=rden4)
                v.drain()
                v.tensor_tensor(out=rps, in0=e8s, in1=rden4b, op=A.mult)
                v.drain()
                v.reduce_max(out=m4, in_=rps, axis=AX.X)
                v.drain()
                v.tensor_tensor(out=t8s, in0=rps, in1=m4b, op=A.is_equal)
                v.drain()
                v.scalar_tensor_tensor(out=t8s, in0=t8s, scalar=-1e30,
                                       in1=rps, op0=A.mult, op1=A.add)
                v.drain()
                v.reduce_max(out=m4, in_=t8s, axis=AX.X)
                v.drain()
                v.tensor_tensor(out=t8s, in0=rps, in1=m4b, op=A.is_ge)
                v.drain()
                v.tensor_tensor(out=t8s, in0=rps, in1=t8s, op=A.mult)
                v.drain()
                v.reduce_sum(out=m4, in_=t8s, axis=AX.X)
                v.drain()
                v.tensor_scalar(out=m4, in0=m4, scalar1=1e-6, scalar2=None, op0=A.add)
                v.drain()
                v.reciprocal(out=m4, in_=m4)
                v.drain()
                v.tensor_tensor(out=cfv, in0=t8s, in1=m4b, op=A.mult)
                v.drain()
                v.tensor_tensor(out=t8s, in0=cfv, in1=rps, op=A.mult)
                v.drain()
                v.reduce_sum(out=s_all[:, 4 * b: 4 * b + 4], in_=t8s, axis=AX.X).then_inc(ve, 1)
            emit("vector", _rt_vb)
            SEMS[("rt_doneB", b)] = cnt.inc("ve")

        ph4 = ExitStack()
        boutbc = ph4.enter_context(nc.sbuf_tensor("boutbc", [128, D], F32))
        bout_s = ph4.enter_context(nc.sbuf_tensor("bout_s", [1, D], F32))
        we_hi = [ph4.enter_context(nc.sbuf_tensor(f"we_hi{i}", [128, CC * 512], BF16)) for i in range(2)]
        we_lo = [ph4.enter_context(nc.sbuf_tensor(f"we_lo{i}", [128, CC * 512], BF16)) for i in range(2)]
        wes = [ph4.enter_context(nc.sbuf_tensor(f"wes{i}", [128, 512], F32)) for i in range(4)]
        pf = [ph4.enter_context(nc.psum_tensor(f"pf{i}", [128, 512], F32)) for i in range(4)]

        v_ffn_mm = {}; v_ffn_acc = {}
        NE = 2 * E
        v_we_split = [None] * NE

        def stage_expert_cc(eh, cc):
                e, half = eh // 2, eh % 2
                par = cc % 4
                def _d(sy, e=e, half=half, eh=eh, cc=cc, par=par):
                    if cc >= 4:
                        sy.wait_ge(ve, SEMS[("wes", eh, cc - 4)])
                    elif eh >= 1:
                        sy.wait_ge(ve, SEMS[("wes", eh - 1, CC - 4 + cc)])
                    sy.dma_start(out=wes[par][:],
                                 in_=we_v[e, :, cc, half * 512:(half + 1) * 512]).then_inc(le[par], 16)
                emit("sync", _d)
                v_l = cnt.inc("le%d" % par, 16)
                def _h(s, eh=eh, cc=cc, par=par, v_l=v_l):
                    s.wait_ge(le[par], v_l)
                    if eh >= 2 and cc < 2:
                        s.wait_ge(mm, v_ffn_mm[(eh - 2, TT - 1)])
                    s.drain()
                    s.copy(out=we_hi[eh % 2][:, cc * 512:(cc + 1) * 512], in_=wes[par][:]).then_inc(sc, 1)
                emit("scalar", _h)
                v_h2 = cnt.inc("sc")
                def _lo(v, eh=eh, cc=cc, par=par, v_h2=v_h2):
                    v.wait_ge(sc, v_h2)
                    v.drain()
                    v.scalar_tensor_tensor(out=we_lo[eh % 2][:, cc * 512:(cc + 1) * 512], in0=wes[par][:],
                                           scalar=1.0, in1=we_hi[eh % 2][:, cc * 512:(cc + 1) * 512],
                                           op0=A.mult, op1=A.subtract).then_inc(ve, 1)
                emit("vector", _lo)
                SEMS[("wes", eh, cc)] = cnt.inc("ve")

        def stage_expert(eh):
            for cc in range(CC):
                stage_expert_cc(eh, cc)
            return SEMS[("wes", eh, CC - 1)]

        def _bout_dma(sy):
            sy.dma_start(out=bout_s[:], in_=bout).then_inc(lb, 16)
        emit("sync", _bout_dma)
        v_bout_ld = cnt.inc("lb", 16)

        def _bb_mm(t):
            t.wait_ge(lb, v_bout_ld)
            t.wait_ge(ve, v_ones)
            t.matmul(pf[0][:], onesf[0:1, :], bout_s[0:1, 0:512], start=True, stop=True).then_inc(mm, 1)
            t.matmul(pf[1][:], onesf[0:1, :], bout_s[0:1, 512:1024], start=True, stop=True).then_inc(mm, 1)
        emit("tensor", _bb_mm)
        v_bb = cnt.inc("mm", 2)
        def _bb_ev(v):
            v.wait_ge(mm, v_bb)
            v.drain()
            v.tensor_copy(out=boutbc[:, 0:512], in_=pf[0][:])
            v.drain()
            v.tensor_copy(out=boutbc[:, 512:1024], in_=pf[1][:]).then_inc(ve, 1)
        emit("vector", _bb_ev)
        v_bbe = cnt.inc("ve")

        ln_evac_defer = []
        def emit_ln_evac(tt):
            def _sc5(s, tt=tt):
                s.wait_ge(mm, SEMS[("ln_tp", tt)])
                s.drain()
                s.copy(out=xnt_hi_4d[:, :, tt, :], in_=pth[:].rearrange("p (cc t) -> p cc t", cc=CC)).then_inc(sc, 1)
                s.copy(out=xnt_lo_4d[:, :, tt, :], in_=ptl[:].rearrange("p (cc t) -> p cc t", cc=CC)).then_inc(sc, 1)
            emit("scalar", _sc5)
            SEMS[("ln_evac", tt)] = cnt.inc("sc", 2)

        for tt in range(TT):
            par = tt % 2
            par4 = tt % 4
            a_tile = acc[:, tt * D:(tt + 1) * D]

            def _sc(s, tt=tt, par=par4, a_tile=a_tile):
                if tt % 4 == 0:
                    s.wait_ge(la[tt // 4], v_acc_ld[tt // 4])
                if tt >= 4:
                    s.wait_ge(ve, SEMS[("ln_var", tt - 4)])
                s.activation(out=xnt_hi_4d[:, :, tt, :], in_=a_tile, func=AF.Copy,
                             accum_out=sum_t[par][:, 0:1])
                s.drain()
                s.activation(out=xnt_hi_4d[:, :, tt, :], in_=a_tile, func=AF.Square,
                             accum_out=ssq_t[par][:, 0:1]).then_inc(sc, 1)
            emit("scalar", _sc)
            SEMS[("ln_stats", tt)] = cnt.inc("sc")

            def _ve1(v, tt=tt, par=par4):
                v.wait_ge(sc, SEMS[("ln_stats", tt)])
                if tt >= 4:
                    v.wait_ge(gp, SEMS[("ln_gp", tt - 4)])
                v.tensor_scalar(out=mean_t[par][:], in0=sum_t[par][:], scalar1=1.0 / D,
                                scalar2=None, op0=A.mult)
                v.drain()
                v.tensor_tensor(out=tmp1[par][:], in0=mean_t[par][:], in1=mean_t[par][:], op=A.mult)
                v.drain()
                v.tensor_scalar(out=tmp1[par][:], in0=ssq_t[par][:], scalar1=1.0 / D,
                                scalar2=tmp1[par][:, 0:1], op0=A.mult, op1=A.subtract)
                v.drain()
                v.tensor_scalar(out=tmp1[par][:], in0=tmp1[par][:], scalar1=EPS, scalar2=None,
                                op0=A.add).then_inc(ve, 1)
            emit("vector", _ve1)
            SEMS[("ln_var", tt)] = cnt.inc("ve")

            def _sc2(s, tt=tt, par=par4):
                s.wait_ge(ve, SEMS[("ln_var", tt)])
                if tt >= 4:
                    s.wait_ge(gp, SEMS[("ln_gp", tt - 4)])
                s.activation(out=rstd_t[par][:], in_=tmp1[par][:], func=AF.Ln)
                s.drain()
                s.activation(out=rstd_t[par][:], in_=rstd_t[par][:], func=AF.Exp,
                             scale=-0.5).then_inc(sc, 1)
            emit("scalar", _sc2)
            SEMS[("ln_rstd", tt)] = cnt.inc("sc")

            def _gp(g, tt=tt, par=par, par4=par4, a_tile=a_tile):
                g.wait_ge(sc, SEMS[("ln_rstd", tt)])
                if tt >= 2:
                    g.wait_ge(ve, SEMS[("ln_split", tt - 2)])
                if tt >= 4:
                    g.wait_ge(ve, SEMS[("ln_hi", tt - 4)])
                g.tensor_scalar(out=nmr_t[par4][:], in0=mean_t[par4][:], scalar1=-1.0,
                                scalar2=rstd_t[par4][:, 0:1], op0=A.mult, op1=A.mult).then_inc(gp, 1)
                g.drain()
                g.tensor_scalar(out=xnf[par][:], in0=a_tile, scalar1=rstd_t[par4][:, 0:1],
                                scalar2=nmr_t[par4][:, 0:1], op0=A.mult, op1=A.add).then_inc(gp, 1)
            emit("gpsimd", _gp)
            SEMS[("ln_nmr", tt)] = cnt.inc("gp")
            SEMS[("ln_gp", tt)] = cnt.inc("gp")

            def _ve3(v, tt=tt, par=par, par4=par4, a_tile=a_tile):
                v.wait_ge(gp, SEMS[("ln_nmr", tt)])
                if tt >= 2:
                    v.wait_ge(mm, SEMS[("ln_tp", tt - 2)])
                v.drain()
                v.tensor_scalar(out=tsbh[par][:], in0=a_tile, scalar1=rstd_t[par4][:, 0:1],
                                scalar2=nmr_t[par4][:, 0:1], op0=A.mult, op1=A.add).then_inc(ve, 1)
            emit("vector", _ve3)
            SEMS[("ln_hi", tt)] = cnt.inc("ve")

            def _ve4(v, tt=tt, par=par):
                v.wait_ge(gp, SEMS[("ln_gp", tt)])
                v.drain()
                v.scalar_tensor_tensor(out=tsbl[par][:], in0=xnf[par][:], scalar=1.0,
                                       in1=tsbh[par][:], op0=A.mult, op1=A.subtract).then_inc(ve, 1)
            emit("vector", _ve4)
            SEMS[("ln_split", tt)] = cnt.inc("ve")

            def _tp(t, tt=tt, par=par):
                t.wait_ge(ve, SEMS[("ln_split", tt)])
                if tt == 0:
                    t.wait_ge(ld, v_const_ld)
                if tt >= 1:
                    t.wait_ge(sc, SEMS[("ln_evac", tt - 1)])
                for cc in range(CC):
                    last = (cc == CC - 1)
                    i1 = t.transpose(pth[:, cc * 128:(cc + 1) * 128], tsbh[par][:, cc * 128:(cc + 1) * 128], ident[:])
                    if last:
                        i1.then_inc(mm, 1)
                    i2 = t.transpose(ptl[:, cc * 128:(cc + 1) * 128], tsbl[par][:, cc * 128:(cc + 1) * 128], ident[:])
                    if last:
                        i2.then_inc(mm, 1)
            emit("tensor", _tp)
            SEMS[("ln_tp", tt)] = cnt.inc("mm", 2)

            stage_expert_cc(tt // 8, tt % 8)
            ln_evac_defer.append(tt)
            if len(ln_evac_defer) > 1:
                emit_ln_evac(ln_evac_defer.pop(0))
            if tt >= 1:
                emit_router_mm(tt - 1)
            if tt >= 2:
                emit_router_c1(tt - 2)
            if tt >= 5 and (tt - 5) % 4 == 0:
                emit_router_batch((tt - 5) // 4)
        while ln_evac_defer:
            emit_ln_evac(ln_evac_defer.pop(0))
        emit_router_mm(TT - 1)
        for _q in (TT - 2, TT - 1):
            emit_router_c1(_q)
        emit_router_batch(TT // 4 - 1)
        v_router_done = SEMS[("rt_doneB", TT // 4 - 1)]
        v_we_split[0] = SEMS[("wes", 0, CC - 1)]
        v_we_split[1] = SEMS[("wes", 1, CC - 1)]

        v_xnt_ready = SEMS[("ln_evac", TT - 1)]
        v_xnf_all = SEMS[("ln_gp", TT - 1)]

        def _truncate_store():
            def _store_tr(sy):
                sy.wait_ge(ve, cnt.val("ve"))
                sy.wait_ge(sc, cnt.val("sc"))
                sy.wait_ge(gp, cnt.val("gp"))
                sy.wait_ge(mm, cnt.val("mm"))
                sy.dma_start(out=out_t, in_=acc[:].rearrange("p (tt j) -> p tt j", j=D)).then_inc(st, 16)
            emit("sync", _store_tr)
            cnt.inc("st", 16)
            def _fw(sy):
                sy.wait_ge(st, cnt.val("st"))
                if cnt.val("sg"):
                    sy.wait_ge(sg, cnt.val("sg"))
            emit("sync", _fw)
            with nc.Block() as block:
                @block.sync
                def _(eng):
                    for f in ops["sync"]: f(eng)
                @block.tensor
                def _(eng):
                    for f in ops["tensor"]: f(eng)
                @block.scalar
                def _(eng):
                    for f in ops["scalar"]: f(eng)
                @block.vector
                def _(eng):
                    for f in ops["vector"]: f(eng)
                @block.gpsimd
                def _(eng):
                    for f in ops["gpsimd"]: f(eng)
            nc._dbg_cnt = dict(cnt.c)
            return nc

        if phases == 1:
            return _truncate_store()

        if debug:
            def _dbg_xnt(sy):
                sy.wait_ge(sc, v_xnt_ready)
                sy.dma_start(out=dbg["d_xnh"], in_=xnt_hi[:]).then_inc(sg, 16)
                sy.dma_start(out=dbg["d_xnl"], in_=xnt_lo[:]).then_inc(sg, 16)
            emit("gpsimd", _dbg_xnt)
            cnt.inc("sg", 32)


        if phases == 2:
            return _truncate_store()

        if debug:
            def _dbg_rt(sy):
                sy.wait_ge(ve, v_router_done)
                sy.dma_start(out=dbg["d_coef"], in_=coef[:]).then_inc(sg, 16)
                sy.dma_start(out=dbg["d_s"], in_=s_all[:]).then_inc(sg, 16)
            emit("gpsimd", _dbg_rt)
            cnt.inc("sg", 32)


        for eh in range(NE):
            e, half = eh // 2, eh % 2
            for tt in range(TT):
                p0 = pf[tt % 4]
                def _fmm(t, eh=eh, tt=tt, p0=p0):
                    if eh == 0:
                        t.wait_ge(sc, SEMS[("ln_evac", tt)])
                    if tt == 0:
                        t.wait_ge(ve, v_we_split[eh])
                        if eh == 0:
                            t.wait_ge(ve, v_bbe)
                    if tt >= 4:
                        t.wait_ge(ve, v_ffn_acc[(eh, tt - 4)])
                    elif eh >= 1:
                        t.wait_ge(ve, v_ffn_acc[(eh - 1, TT - 4 + tt)])
                    for cc in range(CC):
                        lh = xnt_hi[:, cc * N + tt * 128: cc * N + (tt + 1) * 128]
                        llo = xnt_lo[:, cc * N + tt * 128: cc * N + (tt + 1) * 128]
                        wh = we_hi[eh % 2][:, cc * 512:(cc + 1) * 512]
                        wl = we_lo[eh % 2][:, cc * 512:(cc + 1) * 512]
                        t.matmul(p0[:], lh, wh, start=(cc == 0), stop=False)
                        t.matmul(p0[:], lh, wl, start=False, stop=False)
                        i3 = t.matmul(p0[:], llo, wh, start=False, stop=(cc == CC - 1))
                        if cc == CC - 1:
                            i3.then_inc(mm, 1)
                emit("tensor", _fmm)
                v_ffn_mm[(eh, tt)] = cnt.inc("mm")

                def _facc(v, e=e, half=half, eh=eh, tt=tt, p0=p0):
                    v.wait_ge(mm, v_ffn_mm[(eh, tt)])
                    v.wait_ge(gp, SEMS[("ln_gp", tt)])
                    cf = coef[:, tt * E + e: tt * E + e + 1]
                    v.drain()
                    v.scalar_tensor_tensor(out=acc[:, tt * D + half * 512: tt * D + (half + 1) * 512],
                                           in0=p0[:], scalar=cf,
                                           in1=acc[:, tt * D + half * 512: tt * D + (half + 1) * 512],
                                           op0=A.mult, op1=A.add).then_inc(ve, 1)
                emit("vector", _facc)
                v_ffn_acc[(eh, tt)] = cnt.inc("ve")

            if eh + 2 < NE:
                v_we_split[eh + 2] = stage_expert(eh + 2)

        v_fin_last = None
        for tt in range(TT):
            def _fin(v, tt=tt):
                v.wait_ge(ve, v_ffn_acc[(NE - 1, tt)])
                v.drain()
                v.scalar_tensor_tensor(out=acc[:, tt * D:(tt + 1) * D], in0=boutbc[:],
                                       scalar=s_all[:, tt:tt + 1],
                                       in1=acc[:, tt * D:(tt + 1) * D], op0=A.add, op1=A.add).then_inc(ve, 1)
            emit("vector", _fin)
            v_fin_last = cnt.inc("ve")


        ph4.close()
        ph2.close()
        ph1.close()

        ph3 = ExitStack()
        def sb3(name, shape, dt):
            return ph3.enter_context(nc.sbuf_tensor(name, shape, dt))
        def ps3(name, shape, dt):
            return ph3.enter_context(nc.psum_tensor(name, shape, dt))

        wst = [sb3(f"wst{i}", [128, 256], F32) for i in range(3)]
        wqk_hi = sb3("wqk_hi", [128, 2048], BF16)
        wqk_lo = sb3("wqk_lo", [128, 2048], BF16)
        wv_hi = sb3("wv_hi", [128, 2048], BF16)
        wv_lo = sb3("wv_lo", [128, 2048], BF16)
        wo_hi = sb3("wo_hi", [128, D], BF16)
        wo_lo = sb3("wo_lo", [128, D], BF16)
        qt_hi = sb3("qt_hi", [128, N], BF16)
        qt_lo = sb3("qt_lo", [128, N], BF16)
        kt_hi = sb3("kt_hi", [128, N], BF16)
        kt_lo = sb3("kt_lo", [128, N], BF16)
        v_hi = sb3("v_hi", [128, TT * 256], BF16)
        v_lo = sb3("v_lo", [128, TT * 256], BF16)
        ot_hi = sb3("ot_hi", [128, N], BF16)
        ot_lo = sb3("ot_lo", [128, N], BF16)
        btab = sb3("btab_s", [128, BW], F32)
        phi = sb3("phi", [128, 2 * 1024], BF16)
        plo = sb3("plo", [128, 2 * 1024], BF16)
        l_small = sb3("l_small", [1, 512], F32)
        rlT = sb3("rlT", [128, TT], F32)
        nrm = sb3("nrm", [1, 24], F32)
        bvec = [sb3(f"bvec{i}", [128, 3], F32) for i in range(2)]
        vals = sb3("vals", [1, 3], F32)
        c_sc = sb3("c_sc", [1, 4], F32)

        scp = [ps3(f"scp{i}", [128, 1024], F32) for i in range(2)]
        psO = [ps3(f"psO{i}", [128, 512], F32) for i in range(2)]
        psl = [ps3(f"psl{i}", [128, 512], F32) for i in range(2)]

        with ph3:
            wqk_v = wqk.rearrange("(cc p) j -> p cc j", p=128)
            wv_v = wv.rearrange("(cc p) j -> p cc j", p=128)
            wout_v = wout.rearrange("(h p) j -> h p j", p=128)
            we_v = we.rearrange("e (cc p) j -> e p cc j", p=128)
            wst_v = [w[:].rearrange("p (a b) -> p a b", b=128) for w in wst]

            v_wst_free = [0, 0, 0]
            lws = [lw[0], lw[1], la[0]]
            lw_names = ["lw0", "lw1", "la0"]
            wst_par = [0]
            first_ph3 = [True]

            def stage_weights(dst_hi, dst_lo, src_ap, width, dst3d=False, extra_tensor_wait=None):
                par = wst_par[0]
                wst_par[0] = (par + 1) % 3
                wait_ve = v_wst_free[par]
                barrier = first_ph3[0]
                first_ph3[0] = False
                cur_mm, cur_sc, cur_ve, cur_gp = (cnt.val("mm"), cnt.val("sc"), cnt.val("ve"), cnt.val("gp"))
                def _d(sy, wait_ve=wait_ve, src_ap=src_ap, width=width, dst3d=dst3d, barrier=barrier,
                       par=par, cur_mm=cur_mm, cur_sc=cur_sc, cur_ve=cur_ve, cur_gp=cur_gp):
                    if barrier:
                        sy.wait_ge(mm, cur_mm); sy.wait_ge(sc, cur_sc)
                        sy.wait_ge(ve, cur_ve); sy.wait_ge(gp, cur_gp)
                    if wait_ve:
                        sy.wait_ge(ve, wait_ve)
                    dst = wst_v[par][:, 0:width // 128, :] if dst3d else wst[par][:, 0:width]
                    sy.dma_start(out=dst, in_=src_ap).then_inc(lws[par], 16)
                emit("sync", _d)
                v_l = cnt.inc(lw_names[par], 16)
                src_flat = wst_v[par][:, 0:width // 128, :] if dst3d else wst[par][:, 0:width]
                def _h(s, v_l=v_l, par=par, src_flat=src_flat, dst_hi=dst_hi, extra=extra_tensor_wait):
                    s.wait_ge(lws[par], v_l)
                    if extra is not None:
                        s.wait_ge(mm, extra)
                    s.copy(out=dst_hi, in_=src_flat).then_inc(sc, 1)
                emit("scalar", _h)
                v_h = cnt.inc("sc")
                def _lo(v, v_h=v_h, src_flat=src_flat, dst_hi=dst_hi, dst_lo=dst_lo):
                    v.wait_ge(sc, v_h)
                    v.drain()
                    v.scalar_tensor_tensor(out=dst_lo, in0=src_flat, scalar=1.0,
                                           in1=dst_hi, op0=A.mult, op1=A.subtract).then_inc(ve, 1)
                emit("vector", _lo)
                v_done = cnt.inc("ve")
                v_wst_free[par] = v_done
                return v_done

            wqk_hi_3 = wqk_hi[:].rearrange("p (cc two j) -> p cc two j", two=2, j=128)
            wqk_lo_3 = wqk_lo[:].rearrange("p (cc two j) -> p cc two j", two=2, j=128)
            wv_hi_3 = wv_hi[:].rearrange("p (cc two j) -> p cc two j", two=2, j=128)
            wv_lo_3 = wv_lo[:].rearrange("p (cc two j) -> p cc two j", two=2, j=128)
            wqk_src = wqk_v.rearrange("p cc j -> p cc j") if False else wqk_v

            def stage_pair_weights(hi4, lo4, src3, extra_tensor_wait=None):
                d = None
                for g in range(4):
                    ccs = slice(2 * g, 2 * g + 2)
                    for half in range(2):
                        d = stage_weights(hi4[:, ccs, half, :], lo4[:, ccs, half, :],
                                          src3[:, 2 * g:2 * g + 2, half * 128:(half + 1) * 128], 256,
                                          dst3d=True, extra_tensor_wait=extra_tensor_wait)
                return d

            v_qk_split = [None] * NP
            v_wv_split = [None] * NP
            v_wo_split = [None] * H
            v_btab_ld = [None] * H

            v_qk_split[0] = stage_pair_weights(wqk_hi_3, wqk_lo_3, wqk_v[:, :, 0:256])
            v_wv_split[0] = stage_pair_weights(wv_hi_3, wv_lo_3, wv_v[:, :, 0:256])
            def _btab0(sy):
                sy.dma_start(out=btab[:], in_=btabd[0]).then_inc(lb, 16)
            emit("sync", _btab0)
            v_btab_ld[0] = cnt.inc("lb", 16)
            def stage_wout(h, extra_tensor_wait=None):
                d = None
                for g in range(4):
                    d = stage_weights(wo_hi[:, g * 256:(g + 1) * 256], wo_lo[:, g * 256:(g + 1) * 256],
                                      wout_v[h][:, g * 256:(g + 1) * 256], 256,
                                      extra_tensor_wait=extra_tensor_wait)
                return d
            v_wo_split[0] = stage_wout(0)

            sqb = phi[:, 0:512]
            v_exp2 = {}; v_split = {}; v_pvl = {}; v_band = {}
            v_accst = {}; v_ot_evac = {}; v_rlte = {}

            def phi_prev(h, qc, kt):
                if kt >= 2:
                    return (h, qc, kt - 2)
                if qc >= 1:
                    return (h, qc - 1, KT - 2 + kt)
                if h >= 1:
                    return (h - 1, QC - 1, KT - 2 + kt)
                return None

            pair_v_evac = {}
            def emit_setup(p):
                    for tq in range(TQ):
                        qk_po = (psl[0], psl[1]) if tq % 2 == 0 else (psO[0], psO[1])
                        def _qkmm(t, p=p, tq=tq, qk_po=qk_po):
                            if tq == 0:
                                t.wait_ge(ve, v_qk_split[p])
                                if p == 0:
                                    t.wait_ge(ve, v_router_done)
                                else:
                                    t.wait_ge(ve, v_rlte[2 * p - 1])
                            elif tq >= 2:
                                t.wait_ge(ve, SEMS[("qk_evac", p, tq - 2)])
                            for blk, po in ((0, qk_po[0]), (1, qk_po[1])):
                                for cc in range(CC):
                                    lh = wqk_hi[:, cc * 256 + blk * 128: cc * 256 + (blk + 1) * 128]
                                    llo = wqk_lo[:, cc * 256 + blk * 128: cc * 256 + (blk + 1) * 128]
                                    rh = xnt_hi[:, cc * N + tq * 512: cc * N + (tq + 1) * 512]
                                    rl = xnt_lo[:, cc * N + tq * 512: cc * N + (tq + 1) * 512]
                                    t.matmul(po[:], lh, rh, start=(cc == 0), stop=False)
                                    t.matmul(po[:], lh, rl, start=False, stop=False)
                                    i3 = t.matmul(po[:], llo, rh, start=False, stop=(cc == CC - 1))
                                    if cc == CC - 1:
                                        i3.then_inc(mm, 1)
                        emit("tensor", _qkmm)
                        SEMS[("qk_mm", p, tq)] = cnt.inc("mm", 2)

                        def _qk_evh(s, p=p, tq=tq, qk_po=qk_po):
                            s.wait_ge(mm, SEMS[("qk_mm", p, tq)])
                            s.activation(out=qt_hi[:, tq * 512:(tq + 1) * 512], in_=qk_po[0][:],
                                         func=AF.Copy, scale=SCALE).then_inc(sc, 1)
                            s.activation(out=kt_hi[:, tq * 512:(tq + 1) * 512], in_=qk_po[1][:],
                                         func=AF.Copy).then_inc(sc, 1)
                        emit("scalar", _qk_evh)
                        SEMS[("qk_evh", p, tq)] = cnt.inc("sc", 2)

                        def _qk_evl(v, p=p, tq=tq, qk_po=qk_po):
                            v.wait_ge(sc, SEMS[("qk_evh", p, tq)])
                            v.drain()
                            v.scalar_tensor_tensor(out=qt_lo[:, tq * 512:(tq + 1) * 512], in0=qk_po[0][:],
                                                   scalar=SCALE, in1=qt_hi[:, tq * 512:(tq + 1) * 512],
                                                   op0=A.mult, op1=A.subtract)
                            v.drain()
                            v.scalar_tensor_tensor(out=kt_lo[:, tq * 512:(tq + 1) * 512], in0=qk_po[1][:],
                                                   scalar=1.0, in1=kt_hi[:, tq * 512:(tq + 1) * 512],
                                                   op0=A.mult, op1=A.subtract).then_inc(ve, 1)
                        emit("vector", _qk_evl)
                        SEMS[("qk_evac", p, tq)] = cnt.inc("ve")
                    v_qkT_evac = SEMS[("qk_evac", p, TQ - 1)]
                    v_qkproj_mm = SEMS[("qk_mm", p, TQ - 1)]


                    def _sqall(s, p=p):
                        s.wait_ge(ve, SEMS[("qk_evac", p, TQ - 1)])
                        if p >= 1:
                            s.wait_ge(mm, v_pvl[(2 * p - 1, QC - 1, KT // 2 - 1)])
                        s.activation(out=phi[:], in_=qt_hi[:], func=AF.Square).then_inc(sc, 1)
                        s.activation(out=plo[:], in_=kt_hi[:], func=AF.Square).then_inc(sc, 1)
                    emit("scalar", _sqall)
                    SEMS[("nrm_sq", p)] = cnt.inc("sc", 2)

                    nregions = [psO[0][0:1, :], psO[1][0:1, :]]
                    def emit_norm_i(i, p=p):
                        g, ch = divmod(i, 4)
                        ten, mask = divmod(g, 2)
                        sq_src = phi if ten == 0 else plo
                        def _nmm(t, p=p, i=i, ch=ch, mask=mask, sq_src=sq_src):
                            if i == 0:
                                t.wait_ge(sc, SEMS[("nrm_sq", p)])
                            if i >= 2:
                                t.wait_ge(ve, SEMS[("nrm_red", p, i - 2)])
                            src_ap = sq_src[:, ch * 512:(ch + 1) * 512]
                            t.matmul(nregions[i % 2], ones_hi[:] if mask == 0 else ones_lo[:],
                                     src_ap, start=True, stop=True).then_inc(mm, 1)
                        emit("tensor", _nmm)
                        SEMS[("nrm_mm", p, i)] = cnt.inc("mm")

                        def _nred(v, p=p, i=i):
                            v.wait_ge(mm, SEMS[("nrm_mm", p, i)])
                            v.reduce_max(out=nrm[0:1, i:i + 1], in_=nregions[i % 2], axis=AX.X).then_inc(ve, 1)
                        emit("vector", _nred)
                        SEMS[("nrm_red", p, i)] = cnt.inc("ve")


                    for tt in range(TT):
                        def _vmm(t, p=p, tt=tt):
                            if tt == 0:
                                t.wait_ge(ve, v_wv_split[p])
                                t.wait_ge(ve, SEMS[("qk_evac", p, TQ - 1)])
                            if tt >= 2:
                                t.wait_ge(ve, SEMS[("v_ev", p, tt - 2)])
                            po = psl[tt % 2]
                            for cc in range(CC):
                                lh = xnt_hi[:, cc * N + tt * 128: cc * N + (tt + 1) * 128]
                                llo = xnt_lo[:, cc * N + tt * 128: cc * N + (tt + 1) * 128]
                                rh = wv_hi[:, cc * 256:(cc + 1) * 256]
                                rl = wv_lo[:, cc * 256:(cc + 1) * 256]
                                t.matmul(po[:, 0:256], lh, rh, start=(cc == 0), stop=False)
                                t.matmul(po[:, 0:256], lh, rl, start=False, stop=False)
                                i3 = t.matmul(po[:, 0:256], llo, rh, start=False, stop=(cc == CC - 1))
                                if cc == CC - 1:
                                    i3.then_inc(mm, 1)
                        emit("tensor", _vmm)
                        SEMS[("v_mm", p, tt)] = cnt.inc("mm")

                        def _vevh(s, p=p, tt=tt):
                            s.wait_ge(mm, SEMS[("v_mm", p, tt)])
                            s.drain()
                            s.copy(out=v_hi[:, tt * 256:(tt + 1) * 256], in_=psl[tt % 2][:, 0:256]).then_inc(sc, 1)
                        emit("scalar", _vevh)
                        SEMS[("v_evh", p, tt)] = cnt.inc("sc")

                        def _vevl(v, p=p, tt=tt):
                            v.wait_ge(sc, SEMS[("v_evh", p, tt)])
                            v.drain()
                            v.scalar_tensor_tensor(out=v_lo[:, tt * 256:(tt + 1) * 256], in0=psl[tt % 2][:, 0:256],
                                                   scalar=1.0, in1=v_hi[:, tt * 256:(tt + 1) * 256],
                                                   op0=A.mult, op1=A.subtract).then_inc(ve, 1)
                        emit("vector", _vevl)
                        SEMS[("v_ev", p, tt)] = cnt.inc("ve")
                        if 1 <= tt <= 15:
                            emit_norm_i(tt - 1)
                    emit_norm_i(15)
                    def _ncmb(v, p=p):
                        v.wait_ge(ve, SEMS[("nrm_red", p, 15)])
                        v.drain()
                        v.reduce_max(out=nrm[0:1, 16:17], in_=nrm[0:1, 0:4], axis=AX.X)
                        v.reduce_max(out=nrm[0:1, 17:18], in_=nrm[0:1, 8:12], axis=AX.X)
                        v.reduce_max(out=nrm[0:1, 18:19], in_=nrm[0:1, 4:8], axis=AX.X)
                        v.drain()
                        v.reduce_max(out=nrm[0:1, 19:20], in_=nrm[0:1, 12:16], axis=AX.X).then_inc(ve, 1)
                    emit("vector", _ncmb)
                    v_nrm_done = cnt.inc("ve")

                    v_v_evac = SEMS[("v_ev", p, TT - 1)]
                    v_vproj_mm = SEMS[("v_mm", p, TT - 1)]


                    if debug and p == 0:
                        def _dbg_v(sy, v_v_evac=v_v_evac):
                            sy.wait_ge(ve, v_v_evac)
                            sy.dma_start(out=dbg["d_v"], in_=v_hi[:]).then_inc(sg, 16)
                        emit("gpsimd", _dbg_v)
                        cnt.inc("sg", 16)

                    for hh in range(2):
                        h = 2 * p + hh
                        def _chv(v, h=h, hh=hh, v_nrm_done=v_nrm_done):
                            v.wait_ge(ve, v_nrm_done)
                            if h == 0:
                                v.wait_ge(ld, v_const_ld)
                            if h >= 2:
                                v.wait_ge(sc, v_exp2[(h - 2, QC - 1, KT // 2 - 1)])
                            v.drain()
                            v.reduce_sum(out=c_sc[0:1, 2:3], in_=nrm[0:1, 16 + 2 * hh:18 + 2 * hh], axis=AX.X)
                            v.drain()
                            v.reduce_max(out=c_sc[0:1, 3:4], in_=convf[0:1, h * 21:(h + 1) * 21], axis=AX.X)
                            v.drain()
                            v.tensor_scalar(out=c_sc[0:1, 2:3], in0=c_sc[0:1, 2:3], scalar1=0.5,
                                            scalar2=c_sc[0:1, 3:4], op0=A.mult, op1=A.add)
                            v.drain()
                            v.tensor_scalar(out=c_sc[0:1, 2:3], in0=c_sc[0:1, 2:3], scalar1=1.0,
                                            scalar2=None, op0=A.add)
                            v.drain()
                            v.tensor_scalar(out=vals[0:1, :], in0=biassel[0:1, 3 * h:3 * h + 3],
                                            scalar1=c_sc[0:1, 2:3], scalar2=None, op0=A.subtract).then_inc(ve, 1)
                        emit("vector", _chv)
                        SEMS[("ch_v", h)] = cnt.inc("ve")

                        def _chmm(t, h=h):
                            t.wait_ge(ve, SEMS[("ch_v", h)])
                            t.matmul(psl[0][:, 0:3], onesf[0:1, :], vals[0:1, :], start=True, stop=True).then_inc(mm, 1)
                        emit("tensor", _chmm)
                        SEMS[("ch_mm", h)] = cnt.inc("mm")

                        def _chev(v, h=h, hh=hh):
                            v.wait_ge(mm, SEMS[("ch_mm", h)])
                            v.drain()
                            v.tensor_copy(out=bvec[hh][:], in_=psl[0][:, 0:3]).then_inc(ve, 1)
                        emit("vector", _chev)
                        SEMS[("ch", h)] = cnt.inc("ve")

                    if debug and p == 0:
                        def _dbg_ch(sy):
                            sy.wait_ge(ve, SEMS[("ch", 1)])
                            sy.dma_start(out=dbg["d_qt"], in_=qt_hi[:]).then_inc(sg, 16)
                            sy.dma_start(out=dbg["d_kt"], in_=kt_hi[:]).then_inc(sg, 16)
                        emit("gpsimd", _dbg_ch)
                        cnt.inc("sg", 32)


                    if p + 1 < NP:
                        v_qk_split[p + 1] = stage_pair_weights(
                            wqk_hi_3, wqk_lo_3, wqk_v[:, :, (p + 1) * 256:(p + 2) * 256],
                            extra_tensor_wait=v_qkproj_mm)
                        v_wv_split[p + 1] = stage_pair_weights(
                            wv_hi_3, wv_lo_3, wv_v[:, :, (p + 1) * 256:(p + 2) * 256],
                            extra_tensor_wait=v_vproj_mm)
                    pair_v_evac[p] = v_v_evac
                    return v_v_evac

            for p in range(NP):
                h0, h1 = 2 * p, 2 * p + 1
                if p == 0:
                    v_v_evac0 = emit_setup(0)
                v_v_evac = pair_v_evac[p]

                for hh in range(2):
                    h = 2 * p + hh
                    qthi = qt_hi[64 * hh: 64 + 64 * hh, :]
                    qtlo = qt_lo[64 * hh: 64 + 64 * hh, :]
                    kthi = kt_hi[64 * hh: 64 + 64 * hh, :]
                    ktlo = kt_lo[64 * hh: 64 + 64 * hh, :]
                    scph = (scp[0], scp[1])

                    def emit_evac_rlt(qc, h=h, hh=hh):
                            def _rlt_mm(t, h=h, hh=hh, qc=qc):
                                t.wait_ge(ve, SEMS[("l_rel", h, qc)])
                                for i in range(4):
                                    t.matmul(psl[qc % 2][:, 100 + i: 101 + i],
                                             l_small[0:1, i * 128:(i + 1) * 128],
                                             onesf[0:1, 0:1], start=True, stop=True).then_inc(mm, 1)
                            emit("tensor", _rlt_mm)
                            SEMS[("rlt_mm", h, qc)] = cnt.inc("mm", 4)

                            def _rlt_ev(v, h=h, hh=hh, qc=qc):
                                v.wait_ge(mm, SEMS[("rlt_mm", h, qc)])
                                v.drain()
                                v.tensor_copy(out=rlT[:, qc * 4:(qc + 1) * 4],
                                              in_=psl[qc % 2][:, 100:104])
                                v.drain()
                                v.reciprocal(out=rlT[:, qc * 4:(qc + 1) * 4],
                                             in_=rlT[:, qc * 4:(qc + 1) * 4]).then_inc(ve, 1)
                            emit("vector", _rlt_ev)
                            SEMS[("rlt_ev", h, qc)] = cnt.inc("ve")

                    def emit_evac(qc, h=h, hh=hh):
                            def _otla(v, h=h, hh=hh, qc=qc):
                                v.wait_ge(mm, v_pvl[(h, qc, KT // 2 - 1)])
                                v.drain()
                                v.tensor_copy(out=l_small[0:1, :], in_=psl[qc % 2][0:1, :]).then_inc(ve, 1)
                            emit("vector", _otla)
                            SEMS[("l_rel", h, qc)] = cnt.inc("ve")

                            def _oth(s, h=h, hh=hh, qc=qc):
                                s.wait_ge(mm, v_pvl[(h, qc, KT // 2 - 1)])
                                s.drain()
                                s.copy(out=ot_hi[:, qc * 512:(qc + 1) * 512], in_=psO[qc % 2][:]).then_inc(sc, 1)
                            emit("scalar", _oth)
                            SEMS[("ot_h", h, qc)] = cnt.inc("sc")

                            def _otl(v, h=h, hh=hh, qc=qc):
                                v.wait_ge(sc, SEMS[("ot_h", h, qc)])
                                v.drain()
                                v.scalar_tensor_tensor(out=ot_lo[:, qc * 512:(qc + 1) * 512], in0=psO[qc % 2][:],
                                                       scalar=1.0, in1=ot_hi[:, qc * 512:(qc + 1) * 512],
                                                       op0=A.mult, op1=A.subtract).then_inc(ve, 1)
                            emit("vector", _otl)
                            v_ot_evac[(h, qc)] = cnt.inc("ve")

                    v_v_evac_pair = v_v_evac
                    evac_defer = []
                    rlt_defer = []
                    sched = [(qc_, ktp_) for qc_ in range(QC) for ktp_ in range(KT // 2)]
                    pv_defer = []
                    for qc, ktp in sched:
                        if True:
                            while rlt_defer:
                                emit_evac_rlt(rlt_defer.pop(0))
                            while evac_defer:
                                _q2 = evac_defer.pop(0)
                                emit_evac(_q2)
                                rlt_defer.append(_q2)
                            kt0, kt1 = 2 * ktp, 2 * ktp + 1
                            parp = ktp % 2
                            po = scph[parp]
                            cls0, cls1 = tile_class(qc, kt0), tile_class(qc, kt1)
                            if ktp >= 2:
                                prev_sc = (h, qc, ktp - 2)
                            elif qc >= 1:
                                prev_sc = (h, qc - 1, KT // 2 - 2 + ktp)
                            elif h >= 1:
                                prev_sc = (h - 1, QC - 1, KT // 2 - 2 + ktp)
                            else:
                                prev_sc = None
                            if ktp >= 2:
                                prev_p = (h, qc, ktp - 2)
                            elif qc >= 1:
                                prev_p = (h, qc - 1, KT // 2 - 2 + ktp)
                            elif h >= 1:
                                prev_p = (h - 1, QC - 1, KT // 2 - 2 + ktp)
                            else:
                                prev_p = None

                            def _smm(t, h=h, hh=hh, qc=qc, kt0=kt0, kt1=kt1, po=po, prev_sc=prev_sc,
                                     qthi=qthi, qtlo=qtlo, kthi=kthi, ktlo=ktlo, v_v_evac=v_v_evac):
                                if qc == 0 and kt0 == 0 and h >= 1:
                                    t.wait_ge(ve, v_accst[(h - 1, TT - 1)])
                                if prev_sc is not None:
                                    t.wait_ge(ve, v_split[prev_sc])
                                tp = (64 * hh, 0)
                                rh = qthi[:, qc * 512:(qc + 1) * 512]
                                rl = qtlo[:, qc * 512:(qc + 1) * 512]
                                for half, kt in ((0, kt0), (1, kt1)):
                                    lh = kthi[:, kt * 128:(kt + 1) * 128]
                                    llo = ktlo[:, kt * 128:(kt + 1) * 128]
                                    ph = po[:, half * 512:(half + 1) * 512]
                                    t.matmul(ph, lh, rh, start=True, stop=False, tile_position=tp)
                                    t.matmul(ph, lh, rl, start=False, stop=False, tile_position=tp)
                                    t.matmul(ph, llo, rh, start=False, stop=True, tile_position=tp).then_inc(mm, 1)
                            emit("tensor", _smm)
                            SEMS[("sc_mm", h, qc, ktp)] = cnt.inc("mm", 2)

                            nband = (1 if cls0 == 0 else 0) + (1 if cls1 == 0 else 0)
                            if nband:
                                def _badd(v, h=h, qc=qc, kt0=kt0, kt1=kt1, po=po, ktp=ktp,
                                          cls0=cls0, cls1=cls1):
                                    v.wait_ge(mm, SEMS[("sc_mm", h, qc, ktp)])
                                    if (qc, kt0) == FIRST_BAND or (qc, kt1) == FIRST_BAND:
                                        v.wait_ge(lb, v_btab_ld[h])
                                    first = True
                                    for half, kt, cls in ((0, kt0, cls0), (1, kt1, cls1)):
                                        if cls != 0:
                                            continue
                                        d0 = kt * 128 - qc * 512
                                        off = U0 - d0
                                        if first:
                                            v.drain()
                                            first = False
                                        ph = po[:, half * 512:(half + 1) * 512]
                                        v.tensor_tensor(out=ph, in0=ph,
                                                        in1=btab[:, off:off + 512], op=A.add).then_inc(ve, 1)
                                emit("vector", _badd)
                                v_badd = cnt.inc("ve", nband)
                                wait_exp = ("ve", v_badd)
                            else:
                                wait_exp = ("mm", SEMS[("sc_mm", h, qc, ktp)])

                            def _exp(s, h=h, hh=hh, qc=qc, ktp=ktp, po=po, parp=parp,
                                     wait_exp=wait_exp, cls0=cls0, cls1=cls1, prev_p=prev_p):
                                eng_s, v_w = wait_exp
                                s.wait_ge({"ve": ve, "mm": mm}[eng_s], v_w)
                                if qc == 0 and ktp == 0:
                                    s.wait_ge(ve, SEMS[("ch", h)])
                                if prev_p is not None:
                                    s.wait_ge(mm, v_pvl[prev_p])
                                b0 = 0 if cls0 == 0 else cls0
                                b1 = 0 if cls1 == 0 else cls1
                                if b0 == b1:
                                    s.activation(out=po[:], in_=po[:], func=AF.Exp,
                                                 bias=bvec[hh][:, b0:b0 + 1])
                                else:
                                    s.activation(out=po[:, 0:512], in_=po[:, 0:512], func=AF.Exp,
                                                 bias=bvec[hh][:, b0:b0 + 1])
                                    s.activation(out=po[:, 512:1024], in_=po[:, 512:1024], func=AF.Exp,
                                                 bias=bvec[hh][:, b1:b1 + 1])
                                s.drain()
                                s.copy(out=phi[:, parp * 1024:(parp + 1) * 1024], in_=po[:]).then_inc(sc, 1)
                            emit("scalar", _exp)
                            v_exp2[(h, qc, ktp)] = cnt.inc("sc")

                            def _spl(v, h=h, qc=qc, ktp=ktp, parp=parp, po=po, prev_p=prev_p):
                                v.wait_ge(sc, v_exp2[(h, qc, ktp)])
                                if prev_p is not None:
                                    v.wait_ge(mm, v_pvl[prev_p])
                                v.drain()
                                v.scalar_tensor_tensor(out=plo[:, parp * 1024:(parp + 1) * 1024],
                                                       in0=po[:], scalar=1.0,
                                                       in1=phi[:, parp * 1024:(parp + 1) * 1024],
                                                       op0=A.mult, op1=A.subtract).then_inc(ve, 1)
                            emit("vector", _spl)
                            v_split[(h, qc, ktp)] = cnt.inc("ve")

                            def _mk_pv(h=h, hh=hh, qc=qc, ktp=ktp, kt0=kt0, kt1=kt1, parp=parp,
                                       v_v_evac_pair=v_v_evac_pair):
                              def _pv(t, h=h, hh=hh, qc=qc, ktp=ktp, kt0=kt0, kt1=kt1, parp=parp,
                                      v_v_evac_pair=v_v_evac_pair):
                                  t.wait_ge(ve, v_split[(h, qc, ktp)])
                                  if ktp == 0 and qc == 0:
                                      t.wait_ge(ve, v_v_evac_pair)
                                  if ktp == 0 and qc >= 2:
                                      t.wait_ge(ve, SEMS[("rlt_ev", h, qc - 2)])
                                      t.wait_ge(ve, v_ot_evac[(h, qc - 2)])
                                  for half, kt in ((0, kt0), (1, kt1)):
                                      vh = v_hi[:, kt * 256 + hh * 128: kt * 256 + (hh + 1) * 128]
                                      vl = v_lo[:, kt * 256 + hh * 128: kt * 256 + (hh + 1) * 128]
                                      ph_ = phi[:, parp * 1024 + half * 512: parp * 1024 + (half + 1) * 512]
                                      pl_ = plo[:, parp * 1024 + half * 512: parp * 1024 + (half + 1) * 512]
                                      st0 = (kt == 0)
                                      pso_ = psO[qc % 2]
                                      t.matmul(pso_[:], vh, ph_, start=st0, stop=False)
                                      t.matmul(pso_[:], vh, pl_, start=False, stop=False)
                                      t.matmul(pso_[:], vl, ph_, start=False, stop=(kt == KT - 1))
                                      psl_ = psl[qc % 2]
                                      t.matmul(psl_[0:1, :], onesb[:], ph_, start=st0, stop=False)
                                      t.matmul(psl_[0:1, :], onesb[:], pl_,
                                               start=False, stop=(kt == KT - 1)).then_inc(mm, 1)
                              emit("tensor", _pv)
                              v_pvl[(h, qc, ktp)] = cnt.inc("mm", 2)
                            pv_defer.append((qc, ktp, _mk_pv))
                            if len(pv_defer) > 2:
                                _q, _k, _f = pv_defer.pop(0)
                                _f()
                                if _k == KT // 2 - 1:
                                    evac_defer.append(_q)
                    while pv_defer:
                        _q, _k, _f = pv_defer.pop(0)
                        _f()
                        if _k == KT // 2 - 1:
                            evac_defer.append(_q)
                    while rlt_defer:
                        emit_evac_rlt(rlt_defer.pop(0))
                    while evac_defer:
                        _q2 = evac_defer.pop(0)
                        emit_evac(_q2)
                        emit_evac_rlt(_q2)

                    bands_h = [v_band[k] for k in v_band if k[0] == h and v_band[k] is not None]
                    v_band_last_h = max(bands_h) if bands_h else cnt.val("ve")
                    if h + 1 < H:
                        def _btd(sy, h=h, v_band_last_h=v_band_last_h):
                            sy.wait_ge(ve, v_band_last_h)
                            sy.dma_start(out=btab[:], in_=btabd[h + 1]).then_inc(lb, 16)
                        emit("sync", _btd)
                        v_btab_ld[h + 1] = cnt.inc("lb", 16)

                    v_rlte[h] = SEMS[("rlt_ev", h, QC - 1)]

                    if debug and h == 0:
                        def _dbg_l(sy):
                            sy.wait_ge(ve, v_rlte[0])
                            sy.dma_start(out=dbg["d_l"], in_=rlT[:]).then_inc(sg, 16)
                        emit("gpsimd", _dbg_l)
                        cnt.inc("sg", 16)

                    if hh == 1 and p + 1 < NP:
                        emit_setup(p + 1)

                    for tt in range(TT):
                        poj = (scp[tt % 2][:, 0:512], scp[tt % 2][:, 512:1024])
                        def _omm(t, hh=hh, h=h, tt=tt, poj=poj):
                            if tt == 0:
                                t.wait_ge(ve, v_wo_split[h])
                                t.wait_ge(ve, v_split[(h, QC - 1, KT // 2 - 1)])
                                if h == 0:
                                    t.wait_ge(gp, v_xnf_all)
                            if tt >= 2:
                                t.wait_ge(ve, v_accst[(h, tt - 2)])
                            elif h >= 1:
                                t.wait_ge(ve, v_accst[(h - 1, TT - 2 + tt)])
                            oth_ = ot_hi[:, tt * 128:(tt + 1) * 128]
                            otl_ = ot_lo[:, tt * 128:(tt + 1) * 128]
                            for half in range(2):
                                po2 = poj[half]
                                wh = wo_hi[:, half * 512:(half + 1) * 512]
                                wl = wo_lo[:, half * 512:(half + 1) * 512]
                                t.matmul(po2, oth_, wh, start=True, stop=False)
                                t.matmul(po2, oth_, wl, start=False, stop=False)
                                t.matmul(po2, otl_, wh, start=False, stop=True).then_inc(mm, 1)
                        emit("tensor", _omm)
                        SEMS[("op_mm", h, tt)] = cnt.inc("mm", 2)

                        def _oacc(v, hh=hh, h=h, tt=tt):
                            v.wait_ge(mm, SEMS[("op_mm", h, tt)])
                            v.drain()
                            v.scalar_tensor_tensor(out=acc[:, tt * D: (tt + 1) * D],
                                                   in0=scp[tt % 2][:, 0:1024], scalar=rlT[:, tt:tt + 1],
                                                   in1=acc[:, tt * D: (tt + 1) * D],
                                                   op0=A.mult, op1=A.add).then_inc(ve, 1)
                        emit("vector", _oacc)
                        v_accst[(h, tt)] = cnt.inc("ve", 1)

                    if h + 1 < H:
                        v_wo_split[h + 1] = stage_wout(h + 1, extra_tensor_wait=SEMS[("op_mm", h, TT - 1)])

                    if debug and h == 0:
                        def _dbg_ot(sy):
                            sy.wait_ge(ve, v_accst[(0, TT - 1)])
                            sy.dma_start(out=dbg["d_ot"], in_=ot_hi[:]).then_inc(sg, 16)
                        emit("gpsimd", _dbg_ot)
                        cnt.inc("sg", 16)

            v_attn_done = v_accst[(H - 1, TT - 1)]
            if phases == 3:
                return _truncate_store()

        if debug:
            def _dbg_acc(g):
                g.wait_ge(ve, v_attn_done)
                g.dma_start(out=dbg["d_accattn"], in_=acc[:]).then_inc(sg, 16)
            emit("gpsimd", _dbg_acc)
        v_dbgacc = cnt.inc("sg", 16) if debug else 0

        def _store(sy, v_fin_last=v_fin_last):
            sy.wait_ge(ve, v_accst[(H - 1, TT - 1)])
            sy.wait_ge(ve, v_fin_last)
            sy.dma_start(out=out_t, in_=acc[:].rearrange("p (tt j) -> p tt j", j=D)).then_inc(st, 16)
        emit("sync", _store)
        cnt.inc("st", 16)

        def _final_wait(sy):
            sy.wait_ge(st, cnt.val("st"))
            if cnt.val("sg"):
                sy.wait_ge(sg, cnt.val("sg"))
        emit("sync", _final_wait)

        with nc.Block() as block:
            @block.sync
            def _(eng):
                for f in ops["sync"]:
                    f(eng)
            @block.tensor
            def _(eng):
                for f in ops["tensor"]:
                    f(eng)
            @block.scalar
            def _(eng):
                for f in ops["scalar"]:
                    f(eng)
            @block.vector
            def _(eng):
                for f in ops["vector"]:
                    f(eng)
            @block.gpsimd
            def _(eng):
                for f in ops["gpsimd"]:
                    f(eng)

    nc._dbg_cnt = dict(cnt.c)
    return nc


def prep_inputs(L0_attn_x, L0_attn_W_qk, L0_attn_W_v, L0_attn_W_out, L0_attn_b_out,
                L0_attn_conv, L0_ffn_router_w, L0_ffn_expert_weights):
    Wqk = np.ascontiguousarray(np.asarray(L0_attn_W_qk), dtype=np.float32)
    cols = []
    for p in range(NP):
        cols.append(Wqk[:, (2 * p) * 64:(2 * p + 2) * 64])
        cols.append(Wqk[:, 512 + (2 * p) * 64: 512 + (2 * p + 2) * 64])
    wqk_perm = np.ascontiguousarray(np.concatenate(cols, axis=1))

    conv = np.asarray(L0_attn_conv, dtype=np.float32)
    r = np.arange(128)[:, None]
    u = np.arange(BW)[None, :]
    idx = np.clip(r - u + U0, -10, 10) + 10
    btab = np.ascontiguousarray(conv[:, idx])
    biassel = np.ascontiguousarray(
        np.stack([np.zeros(H, np.float32), conv[:, 0], conv[:, 20]], axis=1).reshape(1, 3 * H))
    convf = np.ascontiguousarray(conv.reshape(1, H * 21))
    ident = np.eye(128, dtype=ml_dtypes.bfloat16)

    common = {
        "wqk": wqk_perm,
        "wv": np.ascontiguousarray(np.asarray(L0_attn_W_v), dtype=np.float32),
        "wout": np.ascontiguousarray(np.asarray(L0_attn_W_out), dtype=np.float32),
        "bout": np.ascontiguousarray(np.asarray(L0_attn_b_out, dtype=np.float32).reshape(1, D)),
        "rw": np.ascontiguousarray(np.asarray(L0_ffn_router_w, dtype=np.float32).T),
        "we": np.ascontiguousarray(np.asarray(L0_ffn_expert_weights), dtype=np.float32),
        "btab": btab,
        "biassel": biassel,
        "convf": convf,
        "ident": ident,
    }
    xs = np.asarray(L0_attn_x, dtype=np.float32)
    in_maps = []
    for b in range(xs.shape[0]):
        m = dict(common)
        m["x"] = np.ascontiguousarray(xs[b])
        in_maps.append(m)
    return in_maps


def kernel(L0_attn_x, L0_attn_mask, L0_attn_W_qk, L0_attn_W_v, L0_attn_W_out,
           L0_attn_b_out, L0_attn_conv, L0_ffn_router_w, L0_ffn_expert_weights):
    in_maps = prep_inputs(L0_attn_x, L0_attn_W_qk, L0_attn_W_v, L0_attn_W_out,
                          L0_attn_b_out, L0_attn_conv, L0_ffn_router_w, L0_ffn_expert_weights)
    nc = build_program(debug=False)
    res = run_bass_kernel_spmd(nc, in_maps, core_ids=list(range(8)))
    return np.stack([r["out"] for r in res.results], axis=0).astype(np.float32)


# version 27
# speedup vs baseline: 1.0494x; 1.0029x over previous
import numpy as np
from contextlib import ExitStack
import concourse.bass as bass
import concourse.mybir as mybir
from concourse.bass_utils import run_bass_kernel_spmd
import ml_dtypes

F32 = mybir.dt.float32
BF16 = mybir.dt.bfloat16
A = mybir.AluOpType
AX = mybir.AxisListType
AF = mybir.ActivationFunctionType

N = 2048
D = 1024
H = 8
E = 8
TT = 16
CC = 8
TQ = 4
QC = 4
KT = 16
NP = 4
SCALE = 0.125
EPS = 1e-5
U0 = 521
BW = 1176


def tile_class(qc, kt):
    d0 = kt * 128 - qc * 512
    if d0 >= 521:
        return 2
    if d0 <= -137:
        return 1
    return 0

FIRST_BAND = None
for _qc in range(QC):
    for _kt in range(KT):
        if tile_class(_qc, _kt) == 0:
            FIRST_BAND = (_qc, _kt)
            break
    if FIRST_BAND:
        break


class Counter:
    def __init__(self):
        self.c = {}
    def inc(self, name, k=1):
        self.c[name] = self.c.get(name, 0) + k
        return self.c[name]
    def val(self, name):
        return self.c.get(name, 0)


def build_program(debug=False, phases=4):
    nc = bass.Bass("TRN2", target_bir_lowering=False)

    x = nc.dram_tensor("x", [N, D], F32, kind="ExternalInput").ap()
    wqk = nc.dram_tensor("wqk", [D, D], F32, kind="ExternalInput").ap()
    wv = nc.dram_tensor("wv", [D, D], F32, kind="ExternalInput").ap()
    wout = nc.dram_tensor("wout", [D, D], F32, kind="ExternalInput").ap()
    bout = nc.dram_tensor("bout", [1, D], F32, kind="ExternalInput").ap()
    rw = nc.dram_tensor("rw", [D, E], F32, kind="ExternalInput").ap()
    we = nc.dram_tensor("we", [E, D, D], F32, kind="ExternalInput").ap()
    btabd = nc.dram_tensor("btab", [H, 128, BW], F32, kind="ExternalInput").ap()
    biasseld = nc.dram_tensor("biassel", [1, 3 * H], F32, kind="ExternalInput").ap()
    convfd = nc.dram_tensor("convf", [1, 21 * H], F32, kind="ExternalInput").ap()
    identd = nc.dram_tensor("ident", [128, 128], BF16, kind="ExternalInput").ap()
    out = nc.dram_tensor("out", [N, D], F32, kind="ExternalOutput").ap()

    dbg = {}
    if debug:
        for nm, shp in [("d_xnh", [128, CC * N]), ("d_xnl", [128, CC * N]),
                        ("d_coef", [128, TT * E]), ("d_s", [128, TT]),
                        ("d_qt", [128, N]), ("d_kt", [128, N]),
                        ("d_v", [128, TT * 256]), ("d_l", [1, N]),
                        ("d_ot", [128, N]), ("d_accattn", [128, TT * D])]:
            dbg[nm] = nc.dram_tensor(nm, shp, F32, kind="ExternalOutput").ap()

    x_t = x.rearrange("(tt p) j -> p tt j", p=128)
    out_t = out.rearrange("(tt p) j -> p tt j", p=128)

    ops = {k: [] for k in ("sync", "tensor", "scalar", "vector", "gpsimd")}
    def emit(eng, fn):
        ops[eng].append(fn)
    cnt = Counter()

    es = ExitStack()
    def sem(name):
        return es.enter_context(nc.semaphore(name))

    with es:
        ld = sem("ld"); mm = sem("mm"); sc = sem("sc")
        ve = sem("ve"); gp = sem("gp"); st = sem("st"); sg = sem("sg")
        lr = sem("lr"); lw = [sem("lw0"), sem("lw1")]; lb = sem("lb")
        le = [sem("le0"), sem("le1"), sem("le2"), sem("le3")]
        la = [sem("la0"), sem("la1"), sem("la2"), sem("la3")]
        SEMS = {}

        acc = es.enter_context(nc.sbuf_tensor("acc", [128, TT * D], F32))
        xnt_hi = es.enter_context(nc.sbuf_tensor("xnt_hi", [128, CC * N], BF16))
        xnt_lo = es.enter_context(nc.sbuf_tensor("xnt_lo", [128, CC * N], BF16))
        onesb = es.enter_context(nc.sbuf_tensor("onesb", [128, 1], BF16))
        ones_hi = es.enter_context(nc.sbuf_tensor("ones_hi", [128, 1], BF16))
        ones_lo = es.enter_context(nc.sbuf_tensor("ones_lo", [128, 1], BF16))
        onesf = es.enter_context(nc.sbuf_tensor("onesf", [1, 128], F32))
        convf = es.enter_context(nc.sbuf_tensor("convf_s", [1, 21 * H], F32))
        biassel = es.enter_context(nc.sbuf_tensor("biassel_s", [1, 3 * H], F32))
        coef = es.enter_context(nc.sbuf_tensor("coef", [128, TT * E], F32))
        s_all = es.enter_context(nc.sbuf_tensor("s_all", [128, TT], F32))

        acc_3d = acc[:].rearrange("p (tt j) -> p tt j", j=D)
        def _init_dma(sy):
            for i in range(4):
                sy.dma_start(out=acc_3d[:, 4 * i:4 * i + 4, :],
                             in_=x_t[:, 4 * i:4 * i + 4, :]).then_inc(la[i], 16)
            sy.dma_start(out=ident[:], in_=identd).then_inc(ld, 16)
            sy.dma_start(out=convf[:], in_=convfd).then_inc(ld, 16)
            sy.dma_start(out=biassel[:], in_=biasseld).then_inc(ld, 16)
        emit("sync", _init_dma)
        v_const_ld = cnt.inc("ld", 16 * 3)
        v_acc_ld = [cnt.inc("la%d" % i, 16) for i in range(4)]

        def _init_vec(v):
            v.memset(onesb[:], 1.0)
            v.memset(ones_hi[0:64, :], 1.0)
            v.memset(ones_hi[64:128, :], 0.0)
            v.memset(ones_lo[0:64, :], 0.0)
            v.memset(ones_lo[64:128, :], 1.0)
            v.drain()
            v.memset(onesf[:], 1.0).then_inc(ve, 1)
        emit("vector", _init_vec)
        v_ones = cnt.inc("ve")

        ph1 = ExitStack()
        ident = ph1.enter_context(nc.sbuf_tensor("ident_s", [128, 128], BF16))
        sum_t = [ph1.enter_context(nc.sbuf_tensor(f"sum{i}", [128, 1], F32)) for i in range(4)]
        ssq_t = [ph1.enter_context(nc.sbuf_tensor(f"ssq{i}", [128, 1], F32)) for i in range(4)]
        mean_t = [ph1.enter_context(nc.sbuf_tensor(f"mean{i}", [128, 1], F32)) for i in range(4)]
        rstd_t = [ph1.enter_context(nc.sbuf_tensor(f"rstd{i}", [128, 1], F32)) for i in range(4)]
        nmr_t = [ph1.enter_context(nc.sbuf_tensor(f"nmr{i}", [128, 1], F32)) for i in range(4)]
        tmp1 = [ph1.enter_context(nc.sbuf_tensor(f"ltmp{i}", [128, 1], F32)) for i in range(4)]
        xnf = [ph1.enter_context(nc.sbuf_tensor(f"xnf{i}", [128, D], F32)) for i in range(2)]
        tsbh = [ph1.enter_context(nc.sbuf_tensor(f"tsbh{i}", [128, D], BF16)) for i in range(2)]
        tsbl = [ph1.enter_context(nc.sbuf_tensor(f"tsbl{i}", [128, D], BF16)) for i in range(2)]
        pth = ph1.enter_context(nc.psum_tensor("pth", [128, D], BF16))
        ptl = ph1.enter_context(nc.psum_tensor("ptl", [128, D], BF16))

        xnt_hi_4d = xnt_hi[:].rearrange("p (cc n t) -> p cc n t", cc=CC, n=TT)
        xnt_lo_4d = xnt_lo[:].rearrange("p (cc n t) -> p cc n t", cc=CC, n=TT)

        ph2 = ExitStack()
        rwf = ph2.enter_context(nc.sbuf_tensor("rwf", [128, CC * E], F32))
        rwh = ph2.enter_context(nc.sbuf_tensor("rwh", [128, CC * E], BF16))
        rwl = ph2.enter_context(nc.sbuf_tensor("rwl", [128, CC * E], BF16))
        prt = ph2.enter_context(nc.psum_tensor("prt", [128, 1024], F32))
        r_m = [ph2.enter_context(nc.sbuf_tensor(f"rm_{i}", [128, 1], F32)) for i in range(4)]
        e8b = ph2.enter_context(nc.sbuf_tensor("e8b", [128, 2 * 4 * E], F32))
        rpb = ph2.enter_context(nc.sbuf_tensor("rpb", [128, 2 * 4 * E], F32))
        t8b = ph2.enter_context(nc.sbuf_tensor("t8b", [128, 2 * 4 * E], F32))
        rdenb = ph2.enter_context(nc.sbuf_tensor("rdenb", [128, 8], F32))
        mb = ph2.enter_context(nc.sbuf_tensor("mb", [128, 8], F32))
        e8v = e8b[:].rearrange("p (b i e) -> p b i e", b=2, i=4)
        rpv = rpb[:].rearrange("p (b i e) -> p b i e", b=2, i=4)
        t8v = t8b[:].rearrange("p (b i e) -> p b i e", b=2, i=4)

        def _rw_dma(sy):
            sy.dma_start(out=rwf[:].rearrange("p (cc e) -> p cc e", cc=CC),
                         in_=rw.rearrange("(cc p) e -> p cc e", p=128)).then_inc(lr, 16)
        emit("sync", _rw_dma)
        v_rw_ld = cnt.inc("lr", 16)

        def _rw_hi(s):
            s.wait_ge(lr, v_rw_ld)
            s.copy(out=rwh[:], in_=rwf[:]).then_inc(sc, 1)
        emit("scalar", _rw_hi)
        v_rwh = cnt.inc("sc")

        def _rw_lo(v):
            v.wait_ge(sc, v_rwh)
            v.tensor_tensor(out=rwl[:], in0=rwf[:], in1=rwh[:], op=A.subtract).then_inc(ve, 1)
        emit("vector", _rw_lo)
        v_rwl = cnt.inc("ve")

        def emit_router_mm(tt):
            par2 = tt % 2
            def _rt_mm(t, tt=tt, par=par2):
                t.wait_ge(sc, SEMS[("ln_evac", tt)])
                if tt == 0:
                    t.wait_ge(ve, v_rwl)
                if tt >= 2:
                    t.wait_ge(sc, SEMS[("rt_read", tt - 2)])
                po = prt[:, par * 512: par * 512 + E]
                for cc in range(CC):
                    lh = xnt_hi[:, cc * N + tt * 128: cc * N + (tt + 1) * 128]
                    ll = xnt_lo[:, cc * N + tt * 128: cc * N + (tt + 1) * 128]
                    rh = rwh[:, cc * E:(cc + 1) * E]
                    rl = rwl[:, cc * E:(cc + 1) * E]
                    t.matmul(po, lh, rh, start=(cc == 0), stop=False)
                    t.matmul(po, lh, rl, start=False, stop=False)
                    i3 = t.matmul(po, ll, rh, start=False, stop=(cc == CC - 1))
                    if cc == CC - 1:
                        i3.then_inc(mm, 1)
            emit("tensor", _rt_mm)
            SEMS[("rt_mm", tt)] = cnt.inc("mm")

        def emit_router_c1(tt):
            par = tt % 4
            par2 = tt % 2
            def _rt_v1(v, tt=tt, par=par, par2=par2):
                v.wait_ge(mm, SEMS[("rt_mm", tt)])
                if tt >= 4:
                    v.wait_ge(sc, SEMS[("rt_read", tt - 4)])
                po = prt[:, par2 * 512: par2 * 512 + E]
                v.drain()
                v.reduce_max(out=r_m[par][:], in_=po, axis=AX.X, negate=True).then_inc(ve, 1)
            emit("vector", _rt_v1)
            SEMS[("rt_negm", tt)] = cnt.inc("ve")

            def _rt_s1(s, tt=tt, par=par, par2=par2):
                s.wait_ge(ve, SEMS[("rt_negm", tt)])
                b, i = tt // 4, tt % 4
                if b >= 2:
                    s.wait_ge(ve, SEMS[("rt_doneB", b - 2)])
                po = prt[:, par2 * 512: par2 * 512 + E]
                s.activation(out=e8v[:, b % 2, i, :], in_=po, func=AF.Exp,
                             bias=r_m[par][:, 0:1],
                             accum_out=rdenb[:, (b % 2) * 4 + i: (b % 2) * 4 + i + 1]).then_inc(sc, 1)
            emit("scalar", _rt_s1)
            SEMS[("rt_read", tt)] = cnt.inc("sc")

        def emit_router_batch(b):
            par = b % 2
            def _rt_vb(v, b=b, par=par):
                v.wait_ge(sc, SEMS[("rt_read", 4 * b + 3)])
                e8s = e8v[:, par]
                rps = rpv[:, par]
                t8s = t8v[:, par]
                rden4 = rdenb[:, par * 4:(par + 1) * 4]
                m4 = mb[:, par * 4:(par + 1) * 4]
                m4b = m4[:, :, None].broadcast_to([128, 4, E])
                rden4b = rden4[:, :, None].broadcast_to([128, 4, E])
                cfv = coef[:, 4 * b * E: (4 * b + 4) * E].rearrange("p (i e) -> p i e", i=4)
                v.drain()
                v.reciprocal(out=rden4, in_=rden4)
                v.drain()
                v.tensor_tensor(out=rps, in0=e8s, in1=rden4b, op=A.mult)
                v.drain()
                v.reduce_max(out=m4, in_=rps, axis=AX.X)
                v.drain()
                v.tensor_tensor(out=t8s, in0=rps, in1=m4b, op=A.is_equal)
                v.drain()
                v.scalar_tensor_tensor(out=t8s, in0=t8s, scalar=-1e30,
                                       in1=rps, op0=A.mult, op1=A.add)
                v.drain()
                v.reduce_max(out=m4, in_=t8s, axis=AX.X)
                v.drain()
                v.tensor_tensor(out=t8s, in0=rps, in1=m4b, op=A.is_ge)
                v.drain()
                v.tensor_tensor(out=t8s, in0=rps, in1=t8s, op=A.mult)
                v.drain()
                v.reduce_sum(out=m4, in_=t8s, axis=AX.X)
                v.drain()
                v.tensor_scalar(out=m4, in0=m4, scalar1=1e-6, scalar2=None, op0=A.add)
                v.drain()
                v.reciprocal(out=m4, in_=m4)
                v.drain()
                v.tensor_tensor(out=cfv, in0=t8s, in1=m4b, op=A.mult)
                v.drain()
                v.tensor_tensor(out=t8s, in0=cfv, in1=rps, op=A.mult)
                v.drain()
                v.reduce_sum(out=s_all[:, 4 * b: 4 * b + 4], in_=t8s, axis=AX.X).then_inc(ve, 1)
            emit("vector", _rt_vb)
            SEMS[("rt_doneB", b)] = cnt.inc("ve")

        ph4 = ExitStack()
        boutbc = ph4.enter_context(nc.sbuf_tensor("boutbc", [128, D], F32))
        bout_s = ph4.enter_context(nc.sbuf_tensor("bout_s", [1, D], F32))
        we_hi = [ph4.enter_context(nc.sbuf_tensor(f"we_hi{i}", [128, CC * 512], BF16)) for i in range(2)]
        we_lo = [ph4.enter_context(nc.sbuf_tensor(f"we_lo{i}", [128, CC * 512], BF16)) for i in range(2)]
        wes = [ph4.enter_context(nc.sbuf_tensor(f"wes{i}", [128, 512], F32)) for i in range(4)]
        pf = [ph4.enter_context(nc.psum_tensor(f"pf{i}", [128, 512], F32)) for i in range(4)]

        v_ffn_mm = {}; v_ffn_acc = {}
        NE = 2 * E
        v_we_split = [None] * NE

        def stage_expert_cc(eh, cc):
                e, half = eh // 2, eh % 2
                par = cc % 4
                def _d(sy, e=e, half=half, eh=eh, cc=cc, par=par):
                    if cc >= 4:
                        sy.wait_ge(ve, SEMS[("wes", eh, cc - 4)])
                    elif eh >= 1:
                        sy.wait_ge(ve, SEMS[("wes", eh - 1, CC - 4 + cc)])
                    sy.dma_start(out=wes[par][:],
                                 in_=we_v[e, :, cc, half * 512:(half + 1) * 512]).then_inc(le[par], 16)
                emit("sync", _d)
                v_l = cnt.inc("le%d" % par, 16)
                def _h(s, eh=eh, cc=cc, par=par, v_l=v_l):
                    s.wait_ge(le[par], v_l)
                    if eh >= 2 and cc < 2:
                        s.wait_ge(mm, v_ffn_mm[(eh - 2, TT - 1)])
                    s.drain()
                    s.copy(out=we_hi[eh % 2][:, cc * 512:(cc + 1) * 512], in_=wes[par][:]).then_inc(sc, 1)
                emit("scalar", _h)
                v_h2 = cnt.inc("sc")
                def _lo(v, eh=eh, cc=cc, par=par, v_h2=v_h2):
                    v.wait_ge(sc, v_h2)
                    v.drain()
                    v.scalar_tensor_tensor(out=we_lo[eh % 2][:, cc * 512:(cc + 1) * 512], in0=wes[par][:],
                                           scalar=1.0, in1=we_hi[eh % 2][:, cc * 512:(cc + 1) * 512],
                                           op0=A.mult, op1=A.subtract).then_inc(ve, 1)
                emit("vector", _lo)
                SEMS[("wes", eh, cc)] = cnt.inc("ve")

        def stage_expert(eh):
            for cc in range(CC):
                stage_expert_cc(eh, cc)
            return SEMS[("wes", eh, CC - 1)]

        def _bout_dma(sy):
            sy.dma_start(out=bout_s[:], in_=bout).then_inc(lb, 16)
        emit("sync", _bout_dma)
        v_bout_ld = cnt.inc("lb", 16)

        def _bb_mm(t):
            t.wait_ge(lb, v_bout_ld)
            t.wait_ge(ve, v_ones)
            t.matmul(pf[0][:], onesf[0:1, :], bout_s[0:1, 0:512], start=True, stop=True).then_inc(mm, 1)
            t.matmul(pf[1][:], onesf[0:1, :], bout_s[0:1, 512:1024], start=True, stop=True).then_inc(mm, 1)
        emit("tensor", _bb_mm)
        v_bb = cnt.inc("mm", 2)
        def _bb_ev(v):
            v.wait_ge(mm, v_bb)
            v.drain()
            v.tensor_copy(out=boutbc[:, 0:512], in_=pf[0][:])
            v.drain()
            v.tensor_copy(out=boutbc[:, 512:1024], in_=pf[1][:]).then_inc(ve, 1)
        emit("vector", _bb_ev)
        v_bbe = cnt.inc("ve")

        ln_evac_defer = []
        def emit_ln_evac(tt):
            def _sc5(s, tt=tt):
                s.wait_ge(mm, SEMS[("ln_tp", tt)])
                s.drain()
                s.copy(out=xnt_hi_4d[:, :, tt, :], in_=pth[:].rearrange("p (cc t) -> p cc t", cc=CC)).then_inc(sc, 1)
                s.copy(out=xnt_lo_4d[:, :, tt, :], in_=ptl[:].rearrange("p (cc t) -> p cc t", cc=CC)).then_inc(sc, 1)
            emit("scalar", _sc5)
            SEMS[("ln_evac", tt)] = cnt.inc("sc", 2)

        for tt in range(TT):
            par = tt % 2
            par4 = tt % 4
            a_tile = acc[:, tt * D:(tt + 1) * D]

            def _sc(s, tt=tt, par=par4, a_tile=a_tile):
                if tt % 4 == 0:
                    s.wait_ge(la[tt // 4], v_acc_ld[tt // 4])
                if tt >= 4:
                    s.wait_ge(ve, SEMS[("ln_var", tt - 4)])
                s.activation(out=xnt_hi_4d[:, :, tt, :], in_=a_tile, func=AF.Copy,
                             accum_out=sum_t[par][:, 0:1])
                s.drain()
                s.activation(out=xnt_hi_4d[:, :, tt, :], in_=a_tile, func=AF.Square,
                             accum_out=ssq_t[par][:, 0:1]).then_inc(sc, 1)
            emit("scalar", _sc)
            SEMS[("ln_stats", tt)] = cnt.inc("sc")

            def _ve1(v, tt=tt, par=par4):
                v.wait_ge(sc, SEMS[("ln_stats", tt)])
                if tt >= 4:
                    v.wait_ge(gp, SEMS[("ln_gp", tt - 4)])
                v.tensor_scalar(out=mean_t[par][:], in0=sum_t[par][:], scalar1=1.0 / D,
                                scalar2=None, op0=A.mult)
                v.drain()
                v.tensor_tensor(out=tmp1[par][:], in0=mean_t[par][:], in1=mean_t[par][:], op=A.mult)
                v.drain()
                v.tensor_scalar(out=tmp1[par][:], in0=ssq_t[par][:], scalar1=1.0 / D,
                                scalar2=tmp1[par][:, 0:1], op0=A.mult, op1=A.subtract)
                v.drain()
                v.tensor_scalar(out=tmp1[par][:], in0=tmp1[par][:], scalar1=EPS, scalar2=None,
                                op0=A.add).then_inc(ve, 1)
            emit("vector", _ve1)
            SEMS[("ln_var", tt)] = cnt.inc("ve")

            def _sc2(s, tt=tt, par=par4):
                s.wait_ge(ve, SEMS[("ln_var", tt)])
                if tt >= 4:
                    s.wait_ge(gp, SEMS[("ln_gp", tt - 4)])
                s.activation(out=rstd_t[par][:], in_=tmp1[par][:], func=AF.Ln)
                s.drain()
                s.activation(out=rstd_t[par][:], in_=rstd_t[par][:], func=AF.Exp,
                             scale=-0.5).then_inc(sc, 1)
            emit("scalar", _sc2)
            SEMS[("ln_rstd", tt)] = cnt.inc("sc")

            def _gp(g, tt=tt, par=par, par4=par4, a_tile=a_tile):
                g.wait_ge(sc, SEMS[("ln_rstd", tt)])
                if tt >= 2:
                    g.wait_ge(ve, SEMS[("ln_split", tt - 2)])
                if tt >= 4:
                    g.wait_ge(ve, SEMS[("ln_hi", tt - 4)])
                g.tensor_scalar(out=nmr_t[par4][:], in0=mean_t[par4][:], scalar1=-1.0,
                                scalar2=rstd_t[par4][:, 0:1], op0=A.mult, op1=A.mult).then_inc(gp, 1)
                g.drain()
                g.tensor_scalar(out=xnf[par][:], in0=a_tile, scalar1=rstd_t[par4][:, 0:1],
                                scalar2=nmr_t[par4][:, 0:1], op0=A.mult, op1=A.add).then_inc(gp, 1)
            emit("gpsimd", _gp)
            SEMS[("ln_nmr", tt)] = cnt.inc("gp")
            SEMS[("ln_gp", tt)] = cnt.inc("gp")

            def _ve3(v, tt=tt, par=par, par4=par4, a_tile=a_tile):
                v.wait_ge(gp, SEMS[("ln_nmr", tt)])
                if tt >= 2:
                    v.wait_ge(mm, SEMS[("ln_tp", tt - 2)])
                v.drain()
                v.tensor_scalar(out=tsbh[par][:], in0=a_tile, scalar1=rstd_t[par4][:, 0:1],
                                scalar2=nmr_t[par4][:, 0:1], op0=A.mult, op1=A.add).then_inc(ve, 1)
            emit("vector", _ve3)
            SEMS[("ln_hi", tt)] = cnt.inc("ve")

            def _ve4(v, tt=tt, par=par):
                v.wait_ge(gp, SEMS[("ln_gp", tt)])
                v.drain()
                v.scalar_tensor_tensor(out=tsbl[par][:], in0=xnf[par][:], scalar=1.0,
                                       in1=tsbh[par][:], op0=A.mult, op1=A.subtract).then_inc(ve, 1)
            emit("vector", _ve4)
            SEMS[("ln_split", tt)] = cnt.inc("ve")

            def _tp(t, tt=tt, par=par):
                t.wait_ge(ve, SEMS[("ln_split", tt)])
                if tt == 0:
                    t.wait_ge(ld, v_const_ld)
                if tt >= 1:
                    t.wait_ge(sc, SEMS[("ln_evac", tt - 1)])
                for cc in range(CC):
                    last = (cc == CC - 1)
                    i1 = t.transpose(pth[:, cc * 128:(cc + 1) * 128], tsbh[par][:, cc * 128:(cc + 1) * 128], ident[:])
                    if last:
                        i1.then_inc(mm, 1)
                    i2 = t.transpose(ptl[:, cc * 128:(cc + 1) * 128], tsbl[par][:, cc * 128:(cc + 1) * 128], ident[:])
                    if last:
                        i2.then_inc(mm, 1)
            emit("tensor", _tp)
            SEMS[("ln_tp", tt)] = cnt.inc("mm", 2)

            if tt < 8:
                stage_expert_cc((2 * tt) // 8, (2 * tt) % 8)
                stage_expert_cc((2 * tt + 1) // 8, (2 * tt + 1) % 8)
            ln_evac_defer.append(tt)
            if len(ln_evac_defer) > 1:
                emit_ln_evac(ln_evac_defer.pop(0))
            if tt >= 1:
                emit_router_mm(tt - 1)
            if tt >= 2:
                emit_router_c1(tt - 2)
            if tt >= 5 and (tt - 5) % 4 == 0:
                emit_router_batch((tt - 5) // 4)
        while ln_evac_defer:
            emit_ln_evac(ln_evac_defer.pop(0))
        emit_router_mm(TT - 1)
        for _q in (TT - 2, TT - 1):
            emit_router_c1(_q)
        emit_router_batch(TT // 4 - 1)
        v_router_done = SEMS[("rt_doneB", TT // 4 - 1)]
        v_we_split[0] = SEMS[("wes", 0, CC - 1)]
        v_we_split[1] = SEMS[("wes", 1, CC - 1)]

        v_xnt_ready = SEMS[("ln_evac", TT - 1)]
        v_xnf_all = SEMS[("ln_gp", TT - 1)]

        def _truncate_store():
            def _store_tr(sy):
                sy.wait_ge(ve, cnt.val("ve"))
                sy.wait_ge(sc, cnt.val("sc"))
                sy.wait_ge(gp, cnt.val("gp"))
                sy.wait_ge(mm, cnt.val("mm"))
                sy.dma_start(out=out_t, in_=acc[:].rearrange("p (tt j) -> p tt j", j=D)).then_inc(st, 16)
            emit("sync", _store_tr)
            cnt.inc("st", 16)
            def _fw(sy):
                sy.wait_ge(st, cnt.val("st"))
                if cnt.val("sg"):
                    sy.wait_ge(sg, cnt.val("sg"))
            emit("sync", _fw)
            with nc.Block() as block:
                @block.sync
                def _(eng):
                    for f in ops["sync"]: f(eng)
                @block.tensor
                def _(eng):
                    for f in ops["tensor"]: f(eng)
                @block.scalar
                def _(eng):
                    for f in ops["scalar"]: f(eng)
                @block.vector
                def _(eng):
                    for f in ops["vector"]: f(eng)
                @block.gpsimd
                def _(eng):
                    for f in ops["gpsimd"]: f(eng)
            nc._dbg_cnt = dict(cnt.c)
            return nc

        if phases == 1:
            return _truncate_store()

        if debug:
            def _dbg_xnt(sy):
                sy.wait_ge(sc, v_xnt_ready)
                sy.dma_start(out=dbg["d_xnh"], in_=xnt_hi[:]).then_inc(sg, 16)
                sy.dma_start(out=dbg["d_xnl"], in_=xnt_lo[:]).then_inc(sg, 16)
            emit("gpsimd", _dbg_xnt)
            cnt.inc("sg", 32)


        if phases == 2:
            return _truncate_store()

        if debug:
            def _dbg_rt(sy):
                sy.wait_ge(ve, v_router_done)
                sy.dma_start(out=dbg["d_coef"], in_=coef[:]).then_inc(sg, 16)
                sy.dma_start(out=dbg["d_s"], in_=s_all[:]).then_inc(sg, 16)
            emit("gpsimd", _dbg_rt)
            cnt.inc("sg", 32)


        for eh in range(NE):
            e, half = eh // 2, eh % 2
            for tt in range(TT):
                p0 = pf[tt % 4]
                def _fmm(t, eh=eh, tt=tt, p0=p0):
                    if eh == 0:
                        t.wait_ge(sc, SEMS[("ln_evac", tt)])
                    if tt == 0:
                        t.wait_ge(ve, v_we_split[eh])
                        if eh == 0:
                            t.wait_ge(ve, v_bbe)
                    if tt >= 4:
                        t.wait_ge(ve, v_ffn_acc[(eh, tt - 4)])
                    elif eh >= 1:
                        t.wait_ge(ve, v_ffn_acc[(eh - 1, TT - 4 + tt)])
                    for cc in range(CC):
                        lh = xnt_hi[:, cc * N + tt * 128: cc * N + (tt + 1) * 128]
                        llo = xnt_lo[:, cc * N + tt * 128: cc * N + (tt + 1) * 128]
                        wh = we_hi[eh % 2][:, cc * 512:(cc + 1) * 512]
                        wl = we_lo[eh % 2][:, cc * 512:(cc + 1) * 512]
                        t.matmul(p0[:], lh, wh, start=(cc == 0), stop=False)
                        t.matmul(p0[:], lh, wl, start=False, stop=False)
                        i3 = t.matmul(p0[:], llo, wh, start=False, stop=(cc == CC - 1))
                        if cc == CC - 1:
                            i3.then_inc(mm, 1)
                emit("tensor", _fmm)
                v_ffn_mm[(eh, tt)] = cnt.inc("mm")

                def _facc(v, e=e, half=half, eh=eh, tt=tt, p0=p0):
                    v.wait_ge(mm, v_ffn_mm[(eh, tt)])
                    v.wait_ge(gp, SEMS[("ln_gp", tt)])
                    cf = coef[:, tt * E + e: tt * E + e + 1]
                    v.drain()
                    v.scalar_tensor_tensor(out=acc[:, tt * D + half * 512: tt * D + (half + 1) * 512],
                                           in0=p0[:], scalar=cf,
                                           in1=acc[:, tt * D + half * 512: tt * D + (half + 1) * 512],
                                           op0=A.mult, op1=A.add).then_inc(ve, 1)
                emit("vector", _facc)
                v_ffn_acc[(eh, tt)] = cnt.inc("ve")

            if eh + 2 < NE:
                v_we_split[eh + 2] = stage_expert(eh + 2)

        v_fin_last = None
        for tt in range(TT):
            def _fin(v, tt=tt):
                v.wait_ge(ve, v_ffn_acc[(NE - 1, tt)])
                v.drain()
                v.scalar_tensor_tensor(out=acc[:, tt * D:(tt + 1) * D], in0=boutbc[:],
                                       scalar=s_all[:, tt:tt + 1],
                                       in1=acc[:, tt * D:(tt + 1) * D], op0=A.add, op1=A.add).then_inc(ve, 1)
            emit("vector", _fin)
            v_fin_last = cnt.inc("ve")


        ph4.close()
        ph2.close()
        ph1.close()

        ph3 = ExitStack()
        def sb3(name, shape, dt):
            return ph3.enter_context(nc.sbuf_tensor(name, shape, dt))
        def ps3(name, shape, dt):
            return ph3.enter_context(nc.psum_tensor(name, shape, dt))

        wst = [sb3(f"wst{i}", [128, 256], F32) for i in range(3)]
        wqk_hi = sb3("wqk_hi", [128, 2048], BF16)
        wqk_lo = sb3("wqk_lo", [128, 2048], BF16)
        wv_hi = sb3("wv_hi", [128, 2048], BF16)
        wv_lo = sb3("wv_lo", [128, 2048], BF16)
        wo_hi = sb3("wo_hi", [128, D], BF16)
        wo_lo = sb3("wo_lo", [128, D], BF16)
        qt_hi = sb3("qt_hi", [128, N], BF16)
        qt_lo = sb3("qt_lo", [128, N], BF16)
        kt_hi = sb3("kt_hi", [128, N], BF16)
        kt_lo = sb3("kt_lo", [128, N], BF16)
        v_hi = sb3("v_hi", [128, TT * 256], BF16)
        v_lo = sb3("v_lo", [128, TT * 256], BF16)
        ot_hi = sb3("ot_hi", [128, N], BF16)
        ot_lo = sb3("ot_lo", [128, N], BF16)
        btab = sb3("btab_s", [128, BW], F32)
        phi = sb3("phi", [128, 2 * 1024], BF16)
        plo = sb3("plo", [128, 2 * 1024], BF16)
        l_small = sb3("l_small", [1, 512], F32)
        rlT = sb3("rlT", [128, TT], F32)
        nrm = sb3("nrm", [1, 24], F32)
        bvec = [sb3(f"bvec{i}", [128, 3], F32) for i in range(2)]
        vals = sb3("vals", [1, 3], F32)
        c_sc = sb3("c_sc", [1, 4], F32)

        scp = [ps3(f"scp{i}", [128, 1024], F32) for i in range(2)]
        psO = [ps3(f"psO{i}", [128, 512], F32) for i in range(2)]
        psl = [ps3(f"psl{i}", [128, 512], F32) for i in range(2)]

        with ph3:
            wqk_v = wqk.rearrange("(cc p) j -> p cc j", p=128)
            wv_v = wv.rearrange("(cc p) j -> p cc j", p=128)
            wout_v = wout.rearrange("(h p) j -> h p j", p=128)
            we_v = we.rearrange("e (cc p) j -> e p cc j", p=128)
            wst_v = [w[:].rearrange("p (a b) -> p a b", b=128) for w in wst]

            v_wst_free = [0, 0, 0]
            lws = [lw[0], lw[1], la[0]]
            lw_names = ["lw0", "lw1", "la0"]
            wst_par = [0]
            first_ph3 = [True]

            def stage_weights(dst_hi, dst_lo, src_ap, width, dst3d=False, extra_tensor_wait=None):
                par = wst_par[0]
                wst_par[0] = (par + 1) % 3
                wait_ve = v_wst_free[par]
                barrier = first_ph3[0]
                first_ph3[0] = False
                cur_mm, cur_sc, cur_ve, cur_gp = (cnt.val("mm"), cnt.val("sc"), cnt.val("ve"), cnt.val("gp"))
                def _d(sy, wait_ve=wait_ve, src_ap=src_ap, width=width, dst3d=dst3d, barrier=barrier,
                       par=par, cur_mm=cur_mm, cur_sc=cur_sc, cur_ve=cur_ve, cur_gp=cur_gp):
                    if barrier:
                        sy.wait_ge(mm, cur_mm); sy.wait_ge(sc, cur_sc)
                        sy.wait_ge(ve, cur_ve); sy.wait_ge(gp, cur_gp)
                    if wait_ve:
                        sy.wait_ge(ve, wait_ve)
                    dst = wst_v[par][:, 0:width // 128, :] if dst3d else wst[par][:, 0:width]
                    sy.dma_start(out=dst, in_=src_ap).then_inc(lws[par], 16)
                emit("sync", _d)
                v_l = cnt.inc(lw_names[par], 16)
                src_flat = wst_v[par][:, 0:width // 128, :] if dst3d else wst[par][:, 0:width]
                def _h(s, v_l=v_l, par=par, src_flat=src_flat, dst_hi=dst_hi, extra=extra_tensor_wait):
                    s.wait_ge(lws[par], v_l)
                    if extra is not None:
                        s.wait_ge(mm, extra)
                    s.copy(out=dst_hi, in_=src_flat).then_inc(sc, 1)
                emit("scalar", _h)
                v_h = cnt.inc("sc")
                def _lo(v, v_h=v_h, src_flat=src_flat, dst_hi=dst_hi, dst_lo=dst_lo):
                    v.wait_ge(sc, v_h)
                    v.drain()
                    v.scalar_tensor_tensor(out=dst_lo, in0=src_flat, scalar=1.0,
                                           in1=dst_hi, op0=A.mult, op1=A.subtract).then_inc(ve, 1)
                emit("vector", _lo)
                v_done = cnt.inc("ve")
                v_wst_free[par] = v_done
                return v_done

            wqk_hi_3 = wqk_hi[:].rearrange("p (cc two j) -> p cc two j", two=2, j=128)
            wqk_lo_3 = wqk_lo[:].rearrange("p (cc two j) -> p cc two j", two=2, j=128)
            wv_hi_3 = wv_hi[:].rearrange("p (cc two j) -> p cc two j", two=2, j=128)
            wv_lo_3 = wv_lo[:].rearrange("p (cc two j) -> p cc two j", two=2, j=128)
            wqk_src = wqk_v.rearrange("p cc j -> p cc j") if False else wqk_v

            def stage_pair_weights(hi4, lo4, src3, extra_tensor_wait=None):
                d = None
                for g in range(4):
                    ccs = slice(2 * g, 2 * g + 2)
                    for half in range(2):
                        d = stage_weights(hi4[:, ccs, half, :], lo4[:, ccs, half, :],
                                          src3[:, 2 * g:2 * g + 2, half * 128:(half + 1) * 128], 256,
                                          dst3d=True, extra_tensor_wait=extra_tensor_wait)
                return d

            v_qk_split = [None] * NP
            v_wv_split = [None] * NP
            v_wo_split = [None] * H
            v_btab_ld = [None] * H

            v_qk_split[0] = stage_pair_weights(wqk_hi_3, wqk_lo_3, wqk_v[:, :, 0:256])
            v_wv_split[0] = stage_pair_weights(wv_hi_3, wv_lo_3, wv_v[:, :, 0:256])
            def _btab0(sy):
                sy.dma_start(out=btab[:], in_=btabd[0]).then_inc(lb, 16)
            emit("sync", _btab0)
            v_btab_ld[0] = cnt.inc("lb", 16)
            def stage_wout(h, extra_tensor_wait=None):
                d = None
                for g in range(4):
                    d = stage_weights(wo_hi[:, g * 256:(g + 1) * 256], wo_lo[:, g * 256:(g + 1) * 256],
                                      wout_v[h][:, g * 256:(g + 1) * 256], 256,
                                      extra_tensor_wait=extra_tensor_wait)
                return d
            v_wo_split[0] = stage_wout(0)

            sqb = phi[:, 0:512]
            v_exp2 = {}; v_split = {}; v_pvl = {}; v_band = {}
            v_accst = {}; v_ot_evac = {}; v_rlte = {}

            def phi_prev(h, qc, kt):
                if kt >= 2:
                    return (h, qc, kt - 2)
                if qc >= 1:
                    return (h, qc - 1, KT - 2 + kt)
                if h >= 1:
                    return (h - 1, QC - 1, KT - 2 + kt)
                return None

            pair_v_evac = {}
            def emit_setup(p):
                    for tq in range(TQ):
                        qk_po = (psl[0], psl[1]) if tq % 2 == 0 else (psO[0], psO[1])
                        def _qkmm(t, p=p, tq=tq, qk_po=qk_po):
                            if tq == 0:
                                t.wait_ge(ve, v_qk_split[p])
                                if p == 0:
                                    t.wait_ge(ve, v_router_done)
                                else:
                                    t.wait_ge(ve, v_rlte[2 * p - 1])
                            elif tq >= 2:
                                t.wait_ge(ve, SEMS[("qk_evac", p, tq - 2)])
                            for blk, po in ((0, qk_po[0]), (1, qk_po[1])):
                                for cc in range(CC):
                                    lh = wqk_hi[:, cc * 256 + blk * 128: cc * 256 + (blk + 1) * 128]
                                    llo = wqk_lo[:, cc * 256 + blk * 128: cc * 256 + (blk + 1) * 128]
                                    rh = xnt_hi[:, cc * N + tq * 512: cc * N + (tq + 1) * 512]
                                    rl = xnt_lo[:, cc * N + tq * 512: cc * N + (tq + 1) * 512]
                                    t.matmul(po[:], lh, rh, start=(cc == 0), stop=False)
                                    t.matmul(po[:], lh, rl, start=False, stop=False)
                                    i3 = t.matmul(po[:], llo, rh, start=False, stop=(cc == CC - 1))
                                    if cc == CC - 1:
                                        i3.then_inc(mm, 1)
                        emit("tensor", _qkmm)
                        SEMS[("qk_mm", p, tq)] = cnt.inc("mm", 2)

                        def _qk_evh(s, p=p, tq=tq, qk_po=qk_po):
                            s.wait_ge(mm, SEMS[("qk_mm", p, tq)])
                            s.activation(out=qt_hi[:, tq * 512:(tq + 1) * 512], in_=qk_po[0][:],
                                         func=AF.Copy, scale=SCALE).then_inc(sc, 1)
                            s.activation(out=kt_hi[:, tq * 512:(tq + 1) * 512], in_=qk_po[1][:],
                                         func=AF.Copy).then_inc(sc, 1)
                        emit("scalar", _qk_evh)
                        SEMS[("qk_evh", p, tq)] = cnt.inc("sc", 2)

                        def _qk_evl(v, p=p, tq=tq, qk_po=qk_po):
                            v.wait_ge(sc, SEMS[("qk_evh", p, tq)])
                            v.drain()
                            v.scalar_tensor_tensor(out=qt_lo[:, tq * 512:(tq + 1) * 512], in0=qk_po[0][:],
                                                   scalar=SCALE, in1=qt_hi[:, tq * 512:(tq + 1) * 512],
                                                   op0=A.mult, op1=A.subtract)
                            v.drain()
                            v.scalar_tensor_tensor(out=kt_lo[:, tq * 512:(tq + 1) * 512], in0=qk_po[1][:],
                                                   scalar=1.0, in1=kt_hi[:, tq * 512:(tq + 1) * 512],
                                                   op0=A.mult, op1=A.subtract).then_inc(ve, 1)
                        emit("vector", _qk_evl)
                        SEMS[("qk_evac", p, tq)] = cnt.inc("ve")
                    v_qkT_evac = SEMS[("qk_evac", p, TQ - 1)]
                    v_qkproj_mm = SEMS[("qk_mm", p, TQ - 1)]


                    def _sqall(s, p=p):
                        s.wait_ge(ve, SEMS[("qk_evac", p, TQ - 1)])
                        if p >= 1:
                            s.wait_ge(mm, v_pvl[(2 * p - 1, QC - 1, KT // 2 - 1)])
                        s.activation(out=phi[:], in_=qt_hi[:], func=AF.Square).then_inc(sc, 1)
                        s.activation(out=plo[:], in_=kt_hi[:], func=AF.Square).then_inc(sc, 1)
                    emit("scalar", _sqall)
                    SEMS[("nrm_sq", p)] = cnt.inc("sc", 2)

                    nregions = [psO[0][0:1, :], psO[1][0:1, :]]
                    def emit_norm_i(i, p=p):
                        g, ch = divmod(i, 4)
                        ten, mask = divmod(g, 2)
                        sq_src = phi if ten == 0 else plo
                        def _nmm(t, p=p, i=i, ch=ch, mask=mask, sq_src=sq_src):
                            if i == 0:
                                t.wait_ge(sc, SEMS[("nrm_sq", p)])
                            if i >= 2:
                                t.wait_ge(ve, SEMS[("nrm_red", p, i - 2)])
                            src_ap = sq_src[:, ch * 512:(ch + 1) * 512]
                            t.matmul(nregions[i % 2], ones_hi[:] if mask == 0 else ones_lo[:],
                                     src_ap, start=True, stop=True).then_inc(mm, 1)
                        emit("tensor", _nmm)
                        SEMS[("nrm_mm", p, i)] = cnt.inc("mm")

                        def _nred(v, p=p, i=i):
                            v.wait_ge(mm, SEMS[("nrm_mm", p, i)])
                            v.reduce_max(out=nrm[0:1, i:i + 1], in_=nregions[i % 2], axis=AX.X).then_inc(ve, 1)
                        emit("vector", _nred)
                        SEMS[("nrm_red", p, i)] = cnt.inc("ve")


                    for tt in range(TT):
                        def _vmm(t, p=p, tt=tt):
                            if tt == 0:
                                t.wait_ge(ve, v_wv_split[p])
                                t.wait_ge(ve, SEMS[("qk_evac", p, TQ - 1)])
                            if tt >= 2:
                                t.wait_ge(ve, SEMS[("v_ev", p, tt - 2)])
                            po = psl[tt % 2]
                            for cc in range(CC):
                                lh = xnt_hi[:, cc * N + tt * 128: cc * N + (tt + 1) * 128]
                                llo = xnt_lo[:, cc * N + tt * 128: cc * N + (tt + 1) * 128]
                                rh = wv_hi[:, cc * 256:(cc + 1) * 256]
                                rl = wv_lo[:, cc * 256:(cc + 1) * 256]
                                t.matmul(po[:, 0:256], lh, rh, start=(cc == 0), stop=False)
                                t.matmul(po[:, 0:256], lh, rl, start=False, stop=False)
                                i3 = t.matmul(po[:, 0:256], llo, rh, start=False, stop=(cc == CC - 1))
                                if cc == CC - 1:
                                    i3.then_inc(mm, 1)
                        emit("tensor", _vmm)
                        SEMS[("v_mm", p, tt)] = cnt.inc("mm")

                        def _vevh(s, p=p, tt=tt):
                            s.wait_ge(mm, SEMS[("v_mm", p, tt)])
                            s.drain()
                            s.copy(out=v_hi[:, tt * 256:(tt + 1) * 256], in_=psl[tt % 2][:, 0:256]).then_inc(sc, 1)
                        emit("scalar", _vevh)
                        SEMS[("v_evh", p, tt)] = cnt.inc("sc")

                        def _vevl(v, p=p, tt=tt):
                            v.wait_ge(sc, SEMS[("v_evh", p, tt)])
                            v.drain()
                            v.scalar_tensor_tensor(out=v_lo[:, tt * 256:(tt + 1) * 256], in0=psl[tt % 2][:, 0:256],
                                                   scalar=1.0, in1=v_hi[:, tt * 256:(tt + 1) * 256],
                                                   op0=A.mult, op1=A.subtract).then_inc(ve, 1)
                        emit("vector", _vevl)
                        SEMS[("v_ev", p, tt)] = cnt.inc("ve")
                        if 1 <= tt <= 15:
                            emit_norm_i(tt - 1)
                    emit_norm_i(15)
                    def _ncmb(v, p=p):
                        v.wait_ge(ve, SEMS[("nrm_red", p, 15)])
                        v.drain()
                        v.reduce_max(out=nrm[0:1, 16:17], in_=nrm[0:1, 0:4], axis=AX.X)
                        v.reduce_max(out=nrm[0:1, 17:18], in_=nrm[0:1, 8:12], axis=AX.X)
                        v.reduce_max(out=nrm[0:1, 18:19], in_=nrm[0:1, 4:8], axis=AX.X)
                        v.drain()
                        v.reduce_max(out=nrm[0:1, 19:20], in_=nrm[0:1, 12:16], axis=AX.X).then_inc(ve, 1)
                    emit("vector", _ncmb)
                    v_nrm_done = cnt.inc("ve")

                    v_v_evac = SEMS[("v_ev", p, TT - 1)]
                    v_vproj_mm = SEMS[("v_mm", p, TT - 1)]


                    if debug and p == 0:
                        def _dbg_v(sy, v_v_evac=v_v_evac):
                            sy.wait_ge(ve, v_v_evac)
                            sy.dma_start(out=dbg["d_v"], in_=v_hi[:]).then_inc(sg, 16)
                        emit("gpsimd", _dbg_v)
                        cnt.inc("sg", 16)

                    for hh in range(2):
                        h = 2 * p + hh
                        def _chv(v, h=h, hh=hh, v_nrm_done=v_nrm_done):
                            v.wait_ge(ve, v_nrm_done)
                            if h == 0:
                                v.wait_ge(ld, v_const_ld)
                            if h >= 2:
                                v.wait_ge(sc, v_exp2[(h - 2, QC - 1, KT // 2 - 1)])
                            v.drain()
                            v.reduce_sum(out=c_sc[0:1, 2:3], in_=nrm[0:1, 16 + 2 * hh:18 + 2 * hh], axis=AX.X)
                            v.drain()
                            v.reduce_max(out=c_sc[0:1, 3:4], in_=convf[0:1, h * 21:(h + 1) * 21], axis=AX.X)
                            v.drain()
                            v.tensor_scalar(out=c_sc[0:1, 2:3], in0=c_sc[0:1, 2:3], scalar1=0.5,
                                            scalar2=c_sc[0:1, 3:4], op0=A.mult, op1=A.add)
                            v.drain()
                            v.tensor_scalar(out=c_sc[0:1, 2:3], in0=c_sc[0:1, 2:3], scalar1=1.0,
                                            scalar2=None, op0=A.add)
                            v.drain()
                            v.tensor_scalar(out=vals[0:1, :], in0=biassel[0:1, 3 * h:3 * h + 3],
                                            scalar1=c_sc[0:1, 2:3], scalar2=None, op0=A.subtract).then_inc(ve, 1)
                        emit("vector", _chv)
                        SEMS[("ch_v", h)] = cnt.inc("ve")

                        def _chmm(t, h=h):
                            t.wait_ge(ve, SEMS[("ch_v", h)])
                            t.matmul(psl[0][:, 0:3], onesf[0:1, :], vals[0:1, :], start=True, stop=True).then_inc(mm, 1)
                        emit("tensor", _chmm)
                        SEMS[("ch_mm", h)] = cnt.inc("mm")

                        def _chev(v, h=h, hh=hh):
                            v.wait_ge(mm, SEMS[("ch_mm", h)])
                            v.drain()
                            v.tensor_copy(out=bvec[hh][:], in_=psl[0][:, 0:3]).then_inc(ve, 1)
                        emit("vector", _chev)
                        SEMS[("ch", h)] = cnt.inc("ve")

                    if debug and p == 0:
                        def _dbg_ch(sy):
                            sy.wait_ge(ve, SEMS[("ch", 1)])
                            sy.dma_start(out=dbg["d_qt"], in_=qt_hi[:]).then_inc(sg, 16)
                            sy.dma_start(out=dbg["d_kt"], in_=kt_hi[:]).then_inc(sg, 16)
                        emit("gpsimd", _dbg_ch)
                        cnt.inc("sg", 32)


                    if p + 1 < NP:
                        v_qk_split[p + 1] = stage_pair_weights(
                            wqk_hi_3, wqk_lo_3, wqk_v[:, :, (p + 1) * 256:(p + 2) * 256],
                            extra_tensor_wait=v_qkproj_mm)
                        v_wv_split[p + 1] = stage_pair_weights(
                            wv_hi_3, wv_lo_3, wv_v[:, :, (p + 1) * 256:(p + 2) * 256],
                            extra_tensor_wait=v_vproj_mm)
                    pair_v_evac[p] = v_v_evac
                    return v_v_evac

            for p in range(NP):
                h0, h1 = 2 * p, 2 * p + 1
                if p == 0:
                    v_v_evac0 = emit_setup(0)
                v_v_evac = pair_v_evac[p]

                for hh in range(2):
                    h = 2 * p + hh
                    qthi = qt_hi[64 * hh: 64 + 64 * hh, :]
                    qtlo = qt_lo[64 * hh: 64 + 64 * hh, :]
                    kthi = kt_hi[64 * hh: 64 + 64 * hh, :]
                    ktlo = kt_lo[64 * hh: 64 + 64 * hh, :]
                    scph = (scp[0], scp[1])

                    def emit_evac_rlt(qc, h=h, hh=hh):
                            def _rlt_mm(t, h=h, hh=hh, qc=qc):
                                t.wait_ge(ve, SEMS[("l_rel", h, qc)])
                                for i in range(4):
                                    t.matmul(psl[qc % 2][:, 100 + i: 101 + i],
                                             l_small[0:1, i * 128:(i + 1) * 128],
                                             onesf[0:1, 0:1], start=True, stop=True).then_inc(mm, 1)
                            emit("tensor", _rlt_mm)
                            SEMS[("rlt_mm", h, qc)] = cnt.inc("mm", 4)

                            def _rlt_ev(v, h=h, hh=hh, qc=qc):
                                v.wait_ge(mm, SEMS[("rlt_mm", h, qc)])
                                v.drain()
                                v.tensor_copy(out=rlT[:, qc * 4:(qc + 1) * 4],
                                              in_=psl[qc % 2][:, 100:104])
                                v.drain()
                                v.reciprocal(out=rlT[:, qc * 4:(qc + 1) * 4],
                                             in_=rlT[:, qc * 4:(qc + 1) * 4]).then_inc(ve, 1)
                            emit("vector", _rlt_ev)
                            SEMS[("rlt_ev", h, qc)] = cnt.inc("ve")

                    def emit_evac(qc, h=h, hh=hh):
                            def _otla(v, h=h, hh=hh, qc=qc):
                                v.wait_ge(mm, v_pvl[(h, qc, KT // 2 - 1)])
                                v.drain()
                                v.tensor_copy(out=l_small[0:1, :], in_=psl[qc % 2][0:1, :]).then_inc(ve, 1)
                            emit("vector", _otla)
                            SEMS[("l_rel", h, qc)] = cnt.inc("ve")

                            def _oth(s, h=h, hh=hh, qc=qc):
                                s.wait_ge(mm, v_pvl[(h, qc, KT // 2 - 1)])
                                s.drain()
                                s.copy(out=ot_hi[:, qc * 512:(qc + 1) * 512], in_=psO[qc % 2][:]).then_inc(sc, 1)
                            emit("scalar", _oth)
                            SEMS[("ot_h", h, qc)] = cnt.inc("sc")

                            def _otl(v, h=h, hh=hh, qc=qc):
                                v.wait_ge(sc, SEMS[("ot_h", h, qc)])
                                v.drain()
                                v.scalar_tensor_tensor(out=ot_lo[:, qc * 512:(qc + 1) * 512], in0=psO[qc % 2][:],
                                                       scalar=1.0, in1=ot_hi[:, qc * 512:(qc + 1) * 512],
                                                       op0=A.mult, op1=A.subtract).then_inc(ve, 1)
                            emit("vector", _otl)
                            v_ot_evac[(h, qc)] = cnt.inc("ve")

                    v_v_evac_pair = v_v_evac
                    evac_defer = []
                    rlt_defer = []
                    sched = [(qc_, ktp_) for qc_ in range(QC) for ktp_ in range(KT // 2)]
                    pv_defer = []
                    for qc, ktp in sched:
                        if True:
                            while rlt_defer:
                                emit_evac_rlt(rlt_defer.pop(0))
                            while evac_defer:
                                _q2 = evac_defer.pop(0)
                                emit_evac(_q2)
                                rlt_defer.append(_q2)
                            kt0, kt1 = 2 * ktp, 2 * ktp + 1
                            parp = ktp % 2
                            po = scph[parp]
                            cls0, cls1 = tile_class(qc, kt0), tile_class(qc, kt1)
                            if ktp >= 2:
                                prev_sc = (h, qc, ktp - 2)
                            elif qc >= 1:
                                prev_sc = (h, qc - 1, KT // 2 - 2 + ktp)
                            elif h >= 1:
                                prev_sc = (h - 1, QC - 1, KT // 2 - 2 + ktp)
                            else:
                                prev_sc = None
                            if ktp >= 2:
                                prev_p = (h, qc, ktp - 2)
                            elif qc >= 1:
                                prev_p = (h, qc - 1, KT // 2 - 2 + ktp)
                            elif h >= 1:
                                prev_p = (h - 1, QC - 1, KT // 2 - 2 + ktp)
                            else:
                                prev_p = None

                            def _smm(t, h=h, hh=hh, qc=qc, kt0=kt0, kt1=kt1, po=po, prev_sc=prev_sc,
                                     qthi=qthi, qtlo=qtlo, kthi=kthi, ktlo=ktlo, v_v_evac=v_v_evac):
                                if qc == 0 and kt0 == 0 and h >= 1:
                                    t.wait_ge(ve, v_accst[(h - 1, TT - 1)])
                                if prev_sc is not None:
                                    t.wait_ge(ve, v_split[prev_sc])
                                tp = (64 * hh, 0)
                                rh = qthi[:, qc * 512:(qc + 1) * 512]
                                rl = qtlo[:, qc * 512:(qc + 1) * 512]
                                for half, kt in ((0, kt0), (1, kt1)):
                                    lh = kthi[:, kt * 128:(kt + 1) * 128]
                                    llo = ktlo[:, kt * 128:(kt + 1) * 128]
                                    ph = po[:, half * 512:(half + 1) * 512]
                                    t.matmul(ph, lh, rh, start=True, stop=False, tile_position=tp)
                                    t.matmul(ph, lh, rl, start=False, stop=False, tile_position=tp)
                                    t.matmul(ph, llo, rh, start=False, stop=True, tile_position=tp).then_inc(mm, 1)
                            emit("tensor", _smm)
                            SEMS[("sc_mm", h, qc, ktp)] = cnt.inc("mm", 2)

                            nband = (1 if cls0 == 0 else 0) + (1 if cls1 == 0 else 0)
                            if nband:
                                def _badd(v, h=h, qc=qc, kt0=kt0, kt1=kt1, po=po, ktp=ktp,
                                          cls0=cls0, cls1=cls1):
                                    v.wait_ge(mm, SEMS[("sc_mm", h, qc, ktp)])
                                    if (qc, kt0) == FIRST_BAND or (qc, kt1) == FIRST_BAND:
                                        v.wait_ge(lb, v_btab_ld[h])
                                    first = True
                                    for half, kt, cls in ((0, kt0, cls0), (1, kt1, cls1)):
                                        if cls != 0:
                                            continue
                                        d0 = kt * 128 - qc * 512
                                        off = U0 - d0
                                        if first:
                                            v.drain()
                                            first = False
                                        ph = po[:, half * 512:(half + 1) * 512]
                                        v.tensor_tensor(out=ph, in0=ph,
                                                        in1=btab[:, off:off + 512], op=A.add).then_inc(ve, 1)
                                emit("vector", _badd)
                                v_badd = cnt.inc("ve", nband)
                                wait_exp = ("ve", v_badd)
                            else:
                                wait_exp = ("mm", SEMS[("sc_mm", h, qc, ktp)])

                            def _exp(s, h=h, hh=hh, qc=qc, ktp=ktp, po=po, parp=parp,
                                     wait_exp=wait_exp, cls0=cls0, cls1=cls1, prev_p=prev_p):
                                eng_s, v_w = wait_exp
                                s.wait_ge({"ve": ve, "mm": mm}[eng_s], v_w)
                                if qc == 0 and ktp == 0:
                                    s.wait_ge(ve, SEMS[("ch", h)])
                                if prev_p is not None:
                                    s.wait_ge(mm, v_pvl[prev_p])
                                b0 = 0 if cls0 == 0 else cls0
                                b1 = 0 if cls1 == 0 else cls1
                                if b0 == b1:
                                    s.activation(out=po[:], in_=po[:], func=AF.Exp,
                                                 bias=bvec[hh][:, b0:b0 + 1])
                                else:
                                    s.activation(out=po[:, 0:512], in_=po[:, 0:512], func=AF.Exp,
                                                 bias=bvec[hh][:, b0:b0 + 1])
                                    s.activation(out=po[:, 512:1024], in_=po[:, 512:1024], func=AF.Exp,
                                                 bias=bvec[hh][:, b1:b1 + 1])
                                s.drain()
                                s.copy(out=phi[:, parp * 1024:(parp + 1) * 1024], in_=po[:]).then_inc(sc, 1)
                            emit("scalar", _exp)
                            v_exp2[(h, qc, ktp)] = cnt.inc("sc")

                            def _spl(v, h=h, qc=qc, ktp=ktp, parp=parp, po=po, prev_p=prev_p):
                                v.wait_ge(sc, v_exp2[(h, qc, ktp)])
                                if prev_p is not None:
                                    v.wait_ge(mm, v_pvl[prev_p])
                                v.drain()
                                v.scalar_tensor_tensor(out=plo[:, parp * 1024:(parp + 1) * 1024],
                                                       in0=po[:], scalar=1.0,
                                                       in1=phi[:, parp * 1024:(parp + 1) * 1024],
                                                       op0=A.mult, op1=A.subtract).then_inc(ve, 1)
                            emit("vector", _spl)
                            v_split[(h, qc, ktp)] = cnt.inc("ve")

                            def _mk_pv(h=h, hh=hh, qc=qc, ktp=ktp, kt0=kt0, kt1=kt1, parp=parp,
                                       v_v_evac_pair=v_v_evac_pair):
                              def _pv(t, h=h, hh=hh, qc=qc, ktp=ktp, kt0=kt0, kt1=kt1, parp=parp,
                                      v_v_evac_pair=v_v_evac_pair):
                                  t.wait_ge(ve, v_split[(h, qc, ktp)])
                                  if ktp == 0 and qc == 0:
                                      t.wait_ge(ve, v_v_evac_pair)
                                  if ktp == 0 and qc >= 2:
                                      t.wait_ge(ve, SEMS[("rlt_ev", h, qc - 2)])
                                      t.wait_ge(ve, v_ot_evac[(h, qc - 2)])
                                  for half, kt in ((0, kt0), (1, kt1)):
                                      vh = v_hi[:, kt * 256 + hh * 128: kt * 256 + (hh + 1) * 128]
                                      vl = v_lo[:, kt * 256 + hh * 128: kt * 256 + (hh + 1) * 128]
                                      ph_ = phi[:, parp * 1024 + half * 512: parp * 1024 + (half + 1) * 512]
                                      pl_ = plo[:, parp * 1024 + half * 512: parp * 1024 + (half + 1) * 512]
                                      st0 = (kt == 0)
                                      pso_ = psO[qc % 2]
                                      t.matmul(pso_[:], vh, ph_, start=st0, stop=False)
                                      t.matmul(pso_[:], vh, pl_, start=False, stop=False)
                                      t.matmul(pso_[:], vl, ph_, start=False, stop=(kt == KT - 1))
                                      psl_ = psl[qc % 2]
                                      t.matmul(psl_[0:1, :], onesb[:], ph_, start=st0, stop=False)
                                      t.matmul(psl_[0:1, :], onesb[:], pl_,
                                               start=False, stop=(kt == KT - 1)).then_inc(mm, 1)
                              emit("tensor", _pv)
                              v_pvl[(h, qc, ktp)] = cnt.inc("mm", 2)
                            pv_defer.append((qc, ktp, _mk_pv))
                            if len(pv_defer) > 2:
                                _q, _k, _f = pv_defer.pop(0)
                                _f()
                                if _k == KT // 2 - 1:
                                    evac_defer.append(_q)
                    while pv_defer:
                        _q, _k, _f = pv_defer.pop(0)
                        _f()
                        if _k == KT // 2 - 1:
                            evac_defer.append(_q)
                    while rlt_defer:
                        emit_evac_rlt(rlt_defer.pop(0))
                    while evac_defer:
                        _q2 = evac_defer.pop(0)
                        emit_evac(_q2)
                        emit_evac_rlt(_q2)

                    bands_h = [v_band[k] for k in v_band if k[0] == h and v_band[k] is not None]
                    v_band_last_h = max(bands_h) if bands_h else cnt.val("ve")
                    if h + 1 < H:
                        def _btd(sy, h=h, v_band_last_h=v_band_last_h):
                            sy.wait_ge(ve, v_band_last_h)
                            sy.dma_start(out=btab[:], in_=btabd[h + 1]).then_inc(lb, 16)
                        emit("sync", _btd)
                        v_btab_ld[h + 1] = cnt.inc("lb", 16)

                    v_rlte[h] = SEMS[("rlt_ev", h, QC - 1)]

                    if debug and h == 0:
                        def _dbg_l(sy):
                            sy.wait_ge(ve, v_rlte[0])
                            sy.dma_start(out=dbg["d_l"], in_=rlT[:]).then_inc(sg, 16)
                        emit("gpsimd", _dbg_l)
                        cnt.inc("sg", 16)

                    if hh == 1 and p + 1 < NP:
                        emit_setup(p + 1)

                    for tt in range(TT):
                        poj = (scp[tt % 2][:, 0:512], scp[tt % 2][:, 512:1024])
                        def _omm(t, hh=hh, h=h, tt=tt, poj=poj):
                            if tt == 0:
                                t.wait_ge(ve, v_wo_split[h])
                                t.wait_ge(ve, v_split[(h, QC - 1, KT // 2 - 1)])
                                if h == 0:
                                    t.wait_ge(gp, v_xnf_all)
                            if tt >= 2:
                                t.wait_ge(ve, v_accst[(h, tt - 2)])
                            elif h >= 1:
                                t.wait_ge(ve, v_accst[(h - 1, TT - 2 + tt)])
                            oth_ = ot_hi[:, tt * 128:(tt + 1) * 128]
                            otl_ = ot_lo[:, tt * 128:(tt + 1) * 128]
                            for half in range(2):
                                po2 = poj[half]
                                wh = wo_hi[:, half * 512:(half + 1) * 512]
                                wl = wo_lo[:, half * 512:(half + 1) * 512]
                                t.matmul(po2, oth_, wh, start=True, stop=False)
                                t.matmul(po2, oth_, wl, start=False, stop=False)
                                t.matmul(po2, otl_, wh, start=False, stop=True).then_inc(mm, 1)
                        emit("tensor", _omm)
                        SEMS[("op_mm", h, tt)] = cnt.inc("mm", 2)

                        def _oacc(v, hh=hh, h=h, tt=tt):
                            v.wait_ge(mm, SEMS[("op_mm", h, tt)])
                            v.drain()
                            v.scalar_tensor_tensor(out=acc[:, tt * D: (tt + 1) * D],
                                                   in0=scp[tt % 2][:, 0:1024], scalar=rlT[:, tt:tt + 1],
                                                   in1=acc[:, tt * D: (tt + 1) * D],
                                                   op0=A.mult, op1=A.add).then_inc(ve, 1)
                        emit("vector", _oacc)
                        v_accst[(h, tt)] = cnt.inc("ve", 1)

                    if h + 1 < H:
                        v_wo_split[h + 1] = stage_wout(h + 1, extra_tensor_wait=SEMS[("op_mm", h, TT - 1)])

                    if debug and h == 0:
                        def _dbg_ot(sy):
                            sy.wait_ge(ve, v_accst[(0, TT - 1)])
                            sy.dma_start(out=dbg["d_ot"], in_=ot_hi[:]).then_inc(sg, 16)
                        emit("gpsimd", _dbg_ot)
                        cnt.inc("sg", 16)

            v_attn_done = v_accst[(H - 1, TT - 1)]
            if phases == 3:
                return _truncate_store()

        if debug:
            def _dbg_acc(g):
                g.wait_ge(ve, v_attn_done)
                g.dma_start(out=dbg["d_accattn"], in_=acc[:]).then_inc(sg, 16)
            emit("gpsimd", _dbg_acc)
        v_dbgacc = cnt.inc("sg", 16) if debug else 0

        def _store(sy, v_fin_last=v_fin_last):
            sy.wait_ge(ve, v_accst[(H - 1, TT - 1)])
            sy.wait_ge(ve, v_fin_last)
            sy.dma_start(out=out_t, in_=acc[:].rearrange("p (tt j) -> p tt j", j=D)).then_inc(st, 16)
        emit("sync", _store)
        cnt.inc("st", 16)

        def _final_wait(sy):
            sy.wait_ge(st, cnt.val("st"))
            if cnt.val("sg"):
                sy.wait_ge(sg, cnt.val("sg"))
        emit("sync", _final_wait)

        with nc.Block() as block:
            @block.sync
            def _(eng):
                for f in ops["sync"]:
                    f(eng)
            @block.tensor
            def _(eng):
                for f in ops["tensor"]:
                    f(eng)
            @block.scalar
            def _(eng):
                for f in ops["scalar"]:
                    f(eng)
            @block.vector
            def _(eng):
                for f in ops["vector"]:
                    f(eng)
            @block.gpsimd
            def _(eng):
                for f in ops["gpsimd"]:
                    f(eng)

    nc._dbg_cnt = dict(cnt.c)
    return nc


def prep_inputs(L0_attn_x, L0_attn_W_qk, L0_attn_W_v, L0_attn_W_out, L0_attn_b_out,
                L0_attn_conv, L0_ffn_router_w, L0_ffn_expert_weights):
    Wqk = np.ascontiguousarray(np.asarray(L0_attn_W_qk), dtype=np.float32)
    cols = []
    for p in range(NP):
        cols.append(Wqk[:, (2 * p) * 64:(2 * p + 2) * 64])
        cols.append(Wqk[:, 512 + (2 * p) * 64: 512 + (2 * p + 2) * 64])
    wqk_perm = np.ascontiguousarray(np.concatenate(cols, axis=1))

    conv = np.asarray(L0_attn_conv, dtype=np.float32)
    r = np.arange(128)[:, None]
    u = np.arange(BW)[None, :]
    idx = np.clip(r - u + U0, -10, 10) + 10
    btab = np.ascontiguousarray(conv[:, idx])
    biassel = np.ascontiguousarray(
        np.stack([np.zeros(H, np.float32), conv[:, 0], conv[:, 20]], axis=1).reshape(1, 3 * H))
    convf = np.ascontiguousarray(conv.reshape(1, H * 21))
    ident = np.eye(128, dtype=ml_dtypes.bfloat16)

    common = {
        "wqk": wqk_perm,
        "wv": np.ascontiguousarray(np.asarray(L0_attn_W_v), dtype=np.float32),
        "wout": np.ascontiguousarray(np.asarray(L0_attn_W_out), dtype=np.float32),
        "bout": np.ascontiguousarray(np.asarray(L0_attn_b_out, dtype=np.float32).reshape(1, D)),
        "rw": np.ascontiguousarray(np.asarray(L0_ffn_router_w, dtype=np.float32).T),
        "we": np.ascontiguousarray(np.asarray(L0_ffn_expert_weights), dtype=np.float32),
        "btab": btab,
        "biassel": biassel,
        "convf": convf,
        "ident": ident,
    }
    xs = np.asarray(L0_attn_x, dtype=np.float32)
    in_maps = []
    for b in range(xs.shape[0]):
        m = dict(common)
        m["x"] = np.ascontiguousarray(xs[b])
        in_maps.append(m)
    return in_maps


def kernel(L0_attn_x, L0_attn_mask, L0_attn_W_qk, L0_attn_W_v, L0_attn_W_out,
           L0_attn_b_out, L0_attn_conv, L0_ffn_router_w, L0_ffn_expert_weights):
    in_maps = prep_inputs(L0_attn_x, L0_attn_W_qk, L0_attn_W_v, L0_attn_W_out,
                          L0_attn_b_out, L0_attn_conv, L0_ffn_router_w, L0_ffn_expert_weights)
    nc = build_program(debug=False)
    res = run_bass_kernel_spmd(nc, in_maps, core_ids=list(range(8)))
    return np.stack([r["out"] for r in res.results], axis=0).astype(np.float32)


# version 28
# speedup vs baseline: 1.0547x; 1.0051x over previous
import numpy as np
from contextlib import ExitStack
import concourse.bass as bass
import concourse.mybir as mybir
from concourse.bass_utils import run_bass_kernel_spmd
import ml_dtypes

F32 = mybir.dt.float32
BF16 = mybir.dt.bfloat16
A = mybir.AluOpType
AX = mybir.AxisListType
AF = mybir.ActivationFunctionType

N = 2048
D = 1024
H = 8
E = 8
TT = 16
CC = 8
TQ = 4
QC = 4
KT = 16
NP = 4
SCALE = 0.125
EPS = 1e-5
U0 = 521
BW = 1176


def tile_class(qc, kt):
    d0 = kt * 128 - qc * 512
    if d0 >= 521:
        return 2
    if d0 <= -137:
        return 1
    return 0

FIRST_BAND = None
for _qc in range(QC):
    for _kt in range(KT):
        if tile_class(_qc, _kt) == 0:
            FIRST_BAND = (_qc, _kt)
            break
    if FIRST_BAND:
        break


class Counter:
    def __init__(self):
        self.c = {}
    def inc(self, name, k=1):
        self.c[name] = self.c.get(name, 0) + k
        return self.c[name]
    def val(self, name):
        return self.c.get(name, 0)


def build_program(debug=False, phases=4):
    nc = bass.Bass("TRN2", target_bir_lowering=False)

    x = nc.dram_tensor("x", [N, D], F32, kind="ExternalInput").ap()
    wqk = nc.dram_tensor("wqk", [D, D], F32, kind="ExternalInput").ap()
    wv = nc.dram_tensor("wv", [D, D], F32, kind="ExternalInput").ap()
    wout = nc.dram_tensor("wout", [D, D], F32, kind="ExternalInput").ap()
    bout = nc.dram_tensor("bout", [1, D], F32, kind="ExternalInput").ap()
    rw = nc.dram_tensor("rw", [D, E], F32, kind="ExternalInput").ap()
    we = nc.dram_tensor("we", [E, D, D], F32, kind="ExternalInput").ap()
    btabd = nc.dram_tensor("btab", [H, 128, BW], F32, kind="ExternalInput").ap()
    biasseld = nc.dram_tensor("biassel", [1, 3 * H], F32, kind="ExternalInput").ap()
    convfd = nc.dram_tensor("convf", [1, 21 * H], F32, kind="ExternalInput").ap()
    identd = nc.dram_tensor("ident", [128, 128], BF16, kind="ExternalInput").ap()
    out = nc.dram_tensor("out", [N, D], F32, kind="ExternalOutput").ap()

    dbg = {}
    if debug:
        for nm, shp in [("d_xnh", [128, CC * N]), ("d_xnl", [128, CC * N]),
                        ("d_coef", [128, TT * E]), ("d_s", [128, TT]),
                        ("d_qt", [128, N]), ("d_kt", [128, N]),
                        ("d_v", [128, TT * 256]), ("d_l", [1, N]),
                        ("d_ot", [128, N]), ("d_accattn", [128, TT * D])]:
            dbg[nm] = nc.dram_tensor(nm, shp, F32, kind="ExternalOutput").ap()

    x_t = x.rearrange("(tt p) j -> p tt j", p=128)
    out_t = out.rearrange("(tt p) j -> p tt j", p=128)

    ops = {k: [] for k in ("sync", "tensor", "scalar", "vector", "gpsimd")}
    def emit(eng, fn):
        ops[eng].append(fn)
    cnt = Counter()

    es = ExitStack()
    def sem(name):
        return es.enter_context(nc.semaphore(name))

    with es:
        ld = sem("ld"); mm = sem("mm"); sc = sem("sc")
        ve = sem("ve"); gp = sem("gp"); st = sem("st"); sg = sem("sg")
        lr = sem("lr"); lw = [sem("lw0"), sem("lw1")]; lb = sem("lb")
        le = [sem("le0"), sem("le1"), sem("le2"), sem("le3")]
        la = [sem("la0"), sem("la1"), sem("la2"), sem("la3")]
        SEMS = {}

        acc = es.enter_context(nc.sbuf_tensor("acc", [128, TT * D], F32))
        xnt_hi = es.enter_context(nc.sbuf_tensor("xnt_hi", [128, CC * N], BF16))
        xnt_lo = es.enter_context(nc.sbuf_tensor("xnt_lo", [128, CC * N], BF16))
        onesb = es.enter_context(nc.sbuf_tensor("onesb", [128, 1], BF16))
        ones_hi = es.enter_context(nc.sbuf_tensor("ones_hi", [128, 1], BF16))
        ones_lo = es.enter_context(nc.sbuf_tensor("ones_lo", [128, 1], BF16))
        onesf = es.enter_context(nc.sbuf_tensor("onesf", [1, 128], F32))
        convf = es.enter_context(nc.sbuf_tensor("convf_s", [1, 21 * H], F32))
        biassel = es.enter_context(nc.sbuf_tensor("biassel_s", [1, 3 * H], F32))
        coef = es.enter_context(nc.sbuf_tensor("coef", [128, TT * E], F32))
        s_all = es.enter_context(nc.sbuf_tensor("s_all", [128, TT], F32))

        acc_3d = acc[:].rearrange("p (tt j) -> p tt j", j=D)
        def _init_dma(sy):
            for i in range(4):
                sy.dma_start(out=acc_3d[:, 4 * i:4 * i + 4, :],
                             in_=x_t[:, 4 * i:4 * i + 4, :]).then_inc(la[i], 16)
            sy.dma_start(out=ident[:], in_=identd).then_inc(ld, 16)
            sy.dma_start(out=convf[:], in_=convfd).then_inc(ld, 16)
            sy.dma_start(out=biassel[:], in_=biasseld).then_inc(ld, 16)
        emit("sync", _init_dma)
        v_const_ld = cnt.inc("ld", 16 * 3)
        v_acc_ld = [cnt.inc("la%d" % i, 16) for i in range(4)]

        def _init_vec(v):
            v.memset(onesb[:], 1.0)
            v.memset(ones_hi[0:64, :], 1.0)
            v.memset(ones_hi[64:128, :], 0.0)
            v.memset(ones_lo[0:64, :], 0.0)
            v.memset(ones_lo[64:128, :], 1.0)
            v.drain()
            v.memset(onesf[:], 1.0).then_inc(ve, 1)
        emit("vector", _init_vec)
        v_ones = cnt.inc("ve")

        ph1 = ExitStack()
        ident = ph1.enter_context(nc.sbuf_tensor("ident_s", [128, 128], BF16))
        sum_t = [ph1.enter_context(nc.sbuf_tensor(f"sum{i}", [128, 1], F32)) for i in range(4)]
        ssq_t = [ph1.enter_context(nc.sbuf_tensor(f"ssq{i}", [128, 1], F32)) for i in range(4)]
        mean_t = [ph1.enter_context(nc.sbuf_tensor(f"mean{i}", [128, 1], F32)) for i in range(4)]
        rstd_t = [ph1.enter_context(nc.sbuf_tensor(f"rstd{i}", [128, 1], F32)) for i in range(4)]
        nmr_t = [ph1.enter_context(nc.sbuf_tensor(f"nmr{i}", [128, 1], F32)) for i in range(4)]
        tmp1 = [ph1.enter_context(nc.sbuf_tensor(f"ltmp{i}", [128, 1], F32)) for i in range(4)]
        xnf = [ph1.enter_context(nc.sbuf_tensor(f"xnf{i}", [128, D], F32)) for i in range(2)]
        tsbh = [ph1.enter_context(nc.sbuf_tensor(f"tsbh{i}", [128, D], BF16)) for i in range(2)]
        tsbl = [ph1.enter_context(nc.sbuf_tensor(f"tsbl{i}", [128, D], BF16)) for i in range(2)]
        pth = ph1.enter_context(nc.psum_tensor("pth", [128, D], BF16))
        ptl = ph1.enter_context(nc.psum_tensor("ptl", [128, D], BF16))

        xnt_hi_4d = xnt_hi[:].rearrange("p (cc n t) -> p cc n t", cc=CC, n=TT)
        xnt_lo_4d = xnt_lo[:].rearrange("p (cc n t) -> p cc n t", cc=CC, n=TT)

        ph2 = ExitStack()
        rwf = ph2.enter_context(nc.sbuf_tensor("rwf", [128, CC * E], F32))
        rwh = ph2.enter_context(nc.sbuf_tensor("rwh", [128, CC * E], BF16))
        rwl = ph2.enter_context(nc.sbuf_tensor("rwl", [128, CC * E], BF16))
        prt = ph2.enter_context(nc.psum_tensor("prt", [128, 1024], F32))
        r_m = [ph2.enter_context(nc.sbuf_tensor(f"rm_{i}", [128, 1], F32)) for i in range(4)]
        e8b = ph2.enter_context(nc.sbuf_tensor("e8b", [128, 2 * 4 * E], F32))
        rpb = ph2.enter_context(nc.sbuf_tensor("rpb", [128, 2 * 4 * E], F32))
        t8b = ph2.enter_context(nc.sbuf_tensor("t8b", [128, 2 * 4 * E], F32))
        rdenb = ph2.enter_context(nc.sbuf_tensor("rdenb", [128, 8], F32))
        mb = ph2.enter_context(nc.sbuf_tensor("mb", [128, 8], F32))
        e8v = e8b[:].rearrange("p (b i e) -> p b i e", b=2, i=4)
        rpv = rpb[:].rearrange("p (b i e) -> p b i e", b=2, i=4)
        t8v = t8b[:].rearrange("p (b i e) -> p b i e", b=2, i=4)

        def _rw_dma(sy):
            sy.dma_start(out=rwf[:].rearrange("p (cc e) -> p cc e", cc=CC),
                         in_=rw.rearrange("(cc p) e -> p cc e", p=128)).then_inc(lr, 16)
        emit("sync", _rw_dma)
        v_rw_ld = cnt.inc("lr", 16)

        def _rw_hi(s):
            s.wait_ge(lr, v_rw_ld)
            s.copy(out=rwh[:], in_=rwf[:]).then_inc(sc, 1)
        emit("scalar", _rw_hi)
        v_rwh = cnt.inc("sc")

        def _rw_lo(v):
            v.wait_ge(sc, v_rwh)
            v.tensor_tensor(out=rwl[:], in0=rwf[:], in1=rwh[:], op=A.subtract).then_inc(ve, 1)
        emit("vector", _rw_lo)
        v_rwl = cnt.inc("ve")

        def emit_router_mm(tt):
            par2 = tt % 2
            def _rt_mm(t, tt=tt, par=par2):
                t.wait_ge(sc, SEMS[("ln_evac", tt)])
                if tt == 0:
                    t.wait_ge(ve, v_rwl)
                if tt >= 2:
                    t.wait_ge(sc, SEMS[("rt_read", tt - 2)])
                po = prt[:, par * 512: par * 512 + E]
                for cc in range(CC):
                    lh = xnt_hi[:, cc * N + tt * 128: cc * N + (tt + 1) * 128]
                    ll = xnt_lo[:, cc * N + tt * 128: cc * N + (tt + 1) * 128]
                    rh = rwh[:, cc * E:(cc + 1) * E]
                    rl = rwl[:, cc * E:(cc + 1) * E]
                    t.matmul(po, lh, rh, start=(cc == 0), stop=False)
                    t.matmul(po, lh, rl, start=False, stop=False)
                    i3 = t.matmul(po, ll, rh, start=False, stop=(cc == CC - 1))
                    if cc == CC - 1:
                        i3.then_inc(mm, 1)
            emit("tensor", _rt_mm)
            SEMS[("rt_mm", tt)] = cnt.inc("mm")

        def emit_router_c1(tt):
            par = tt % 4
            par2 = tt % 2
            def _rt_v1(v, tt=tt, par=par, par2=par2):
                v.wait_ge(mm, SEMS[("rt_mm", tt)])
                if tt >= 4:
                    v.wait_ge(sc, SEMS[("rt_read", tt - 4)])
                po = prt[:, par2 * 512: par2 * 512 + E]
                v.drain()
                v.reduce_max(out=r_m[par][:], in_=po, axis=AX.X, negate=True).then_inc(ve, 1)
            emit("vector", _rt_v1)
            SEMS[("rt_negm", tt)] = cnt.inc("ve")

            def _rt_s1(s, tt=tt, par=par, par2=par2):
                s.wait_ge(ve, SEMS[("rt_negm", tt)])
                b, i = tt // 4, tt % 4
                if b >= 2:
                    s.wait_ge(ve, SEMS[("rt_doneB", b - 2)])
                po = prt[:, par2 * 512: par2 * 512 + E]
                s.activation(out=e8v[:, b % 2, i, :], in_=po, func=AF.Exp,
                             bias=r_m[par][:, 0:1],
                             accum_out=rdenb[:, (b % 2) * 4 + i: (b % 2) * 4 + i + 1]).then_inc(sc, 1)
            emit("scalar", _rt_s1)
            SEMS[("rt_read", tt)] = cnt.inc("sc")

        def emit_router_batch(b):
            par = b % 2
            def _rt_vb(v, b=b, par=par):
                v.wait_ge(sc, SEMS[("rt_read", 4 * b + 3)])
                e8s = e8v[:, par]
                rps = rpv[:, par]
                t8s = t8v[:, par]
                rden4 = rdenb[:, par * 4:(par + 1) * 4]
                m4 = mb[:, par * 4:(par + 1) * 4]
                m4b = m4[:, :, None].broadcast_to([128, 4, E])
                rden4b = rden4[:, :, None].broadcast_to([128, 4, E])
                cfv = coef[:, 4 * b * E: (4 * b + 4) * E].rearrange("p (i e) -> p i e", i=4)
                v.drain()
                v.reciprocal(out=rden4, in_=rden4)
                v.drain()
                v.tensor_tensor(out=rps, in0=e8s, in1=rden4b, op=A.mult)
                v.drain()
                v.reduce_max(out=m4, in_=rps, axis=AX.X)
                v.drain()
                v.tensor_tensor(out=t8s, in0=rps, in1=m4b, op=A.is_equal)
                v.drain()
                v.scalar_tensor_tensor(out=t8s, in0=t8s, scalar=-1e30,
                                       in1=rps, op0=A.mult, op1=A.add)
                v.drain()
                v.reduce_max(out=m4, in_=t8s, axis=AX.X)
                v.drain()
                v.tensor_tensor(out=t8s, in0=rps, in1=m4b, op=A.is_ge)
                v.drain()
                v.tensor_tensor(out=t8s, in0=rps, in1=t8s, op=A.mult)
                v.drain()
                v.reduce_sum(out=m4, in_=t8s, axis=AX.X)
                v.drain()
                v.tensor_scalar(out=m4, in0=m4, scalar1=1e-6, scalar2=None, op0=A.add)
                v.drain()
                v.reciprocal(out=m4, in_=m4)
                v.drain()
                v.tensor_tensor(out=cfv, in0=t8s, in1=m4b, op=A.mult)
                v.drain()
                v.tensor_tensor(out=t8s, in0=cfv, in1=rps, op=A.mult)
                v.drain()
                v.reduce_sum(out=s_all[:, 4 * b: 4 * b + 4], in_=t8s, axis=AX.X).then_inc(ve, 1)
            emit("vector", _rt_vb)
            SEMS[("rt_doneB", b)] = cnt.inc("ve")

        ph4 = ExitStack()
        boutbc = ph4.enter_context(nc.sbuf_tensor("boutbc", [128, D], F32))
        bout_s = ph4.enter_context(nc.sbuf_tensor("bout_s", [1, D], F32))
        we_hi = [ph4.enter_context(nc.sbuf_tensor(f"we_hi{i}", [128, CC * 512], BF16)) for i in range(2)]
        we_lo = [ph4.enter_context(nc.sbuf_tensor(f"we_lo{i}", [128, CC * 512], BF16)) for i in range(2)]
        wes = [ph4.enter_context(nc.sbuf_tensor(f"wes{i}", [128, 512], F32)) for i in range(4)]
        pf = [ph4.enter_context(nc.psum_tensor(f"pf{i}", [128, 512], F32)) for i in range(4)]

        v_ffn_mm = {}; v_ffn_acc = {}; v_fin_tt = {}
        NE = 2 * E
        v_we_split = [None] * NE

        def stage_expert_cc(eh, cc):
                e, half = eh // 2, eh % 2
                par = cc % 4
                def _d(sy, e=e, half=half, eh=eh, cc=cc, par=par):
                    if cc >= 4:
                        sy.wait_ge(ve, SEMS[("wes", eh, cc - 4)])
                    elif eh >= 1:
                        sy.wait_ge(ve, SEMS[("wes", eh - 1, CC - 4 + cc)])
                    sy.dma_start(out=wes[par][:],
                                 in_=we_v[e, :, cc, half * 512:(half + 1) * 512]).then_inc(le[par], 16)
                emit("sync", _d)
                v_l = cnt.inc("le%d" % par, 16)
                def _h(s, eh=eh, cc=cc, par=par, v_l=v_l):
                    s.wait_ge(le[par], v_l)
                    if eh >= 2 and cc < 2:
                        s.wait_ge(mm, v_ffn_mm[(eh - 2, TT - 1)])
                    s.drain()
                    s.copy(out=we_hi[eh % 2][:, cc * 512:(cc + 1) * 512], in_=wes[par][:]).then_inc(sc, 1)
                emit("scalar", _h)
                v_h2 = cnt.inc("sc")
                def _lo(v, eh=eh, cc=cc, par=par, v_h2=v_h2):
                    v.wait_ge(sc, v_h2)
                    v.drain()
                    v.scalar_tensor_tensor(out=we_lo[eh % 2][:, cc * 512:(cc + 1) * 512], in0=wes[par][:],
                                           scalar=1.0, in1=we_hi[eh % 2][:, cc * 512:(cc + 1) * 512],
                                           op0=A.mult, op1=A.subtract).then_inc(ve, 1)
                emit("vector", _lo)
                SEMS[("wes", eh, cc)] = cnt.inc("ve")

        def stage_expert(eh):
            for cc in range(CC):
                stage_expert_cc(eh, cc)
            return SEMS[("wes", eh, CC - 1)]

        def _bout_dma(sy):
            sy.dma_start(out=bout_s[:], in_=bout).then_inc(lb, 16)
        emit("sync", _bout_dma)
        v_bout_ld = cnt.inc("lb", 16)

        def _bb_mm(t):
            t.wait_ge(lb, v_bout_ld)
            t.wait_ge(ve, v_ones)
            t.matmul(pf[0][:], onesf[0:1, :], bout_s[0:1, 0:512], start=True, stop=True).then_inc(mm, 1)
            t.matmul(pf[1][:], onesf[0:1, :], bout_s[0:1, 512:1024], start=True, stop=True).then_inc(mm, 1)
        emit("tensor", _bb_mm)
        v_bb = cnt.inc("mm", 2)
        def _bb_ev(v):
            v.wait_ge(mm, v_bb)
            v.drain()
            v.tensor_copy(out=boutbc[:, 0:512], in_=pf[0][:])
            v.drain()
            v.tensor_copy(out=boutbc[:, 512:1024], in_=pf[1][:]).then_inc(ve, 1)
        emit("vector", _bb_ev)
        v_bbe = cnt.inc("ve")

        ln_evac_defer = []
        def emit_ln_evac(tt):
            def _sc5(s, tt=tt):
                s.wait_ge(mm, SEMS[("ln_tp", tt)])
                s.drain()
                s.copy(out=xnt_hi_4d[:, :, tt, :], in_=pth[:].rearrange("p (cc t) -> p cc t", cc=CC)).then_inc(sc, 1)
                s.copy(out=xnt_lo_4d[:, :, tt, :], in_=ptl[:].rearrange("p (cc t) -> p cc t", cc=CC)).then_inc(sc, 1)
            emit("scalar", _sc5)
            SEMS[("ln_evac", tt)] = cnt.inc("sc", 2)

        for tt in range(TT):
            par = tt % 2
            par4 = tt % 4
            a_tile = acc[:, tt * D:(tt + 1) * D]

            def _sc(s, tt=tt, par=par4, a_tile=a_tile):
                if tt % 4 == 0:
                    s.wait_ge(la[tt // 4], v_acc_ld[tt // 4])
                if tt >= 4:
                    s.wait_ge(ve, SEMS[("ln_var", tt - 4)])
                s.activation(out=xnt_hi_4d[:, :, tt, :], in_=a_tile, func=AF.Copy,
                             accum_out=sum_t[par][:, 0:1])
                s.drain()
                s.activation(out=xnt_hi_4d[:, :, tt, :], in_=a_tile, func=AF.Square,
                             accum_out=ssq_t[par][:, 0:1]).then_inc(sc, 1)
            emit("scalar", _sc)
            SEMS[("ln_stats", tt)] = cnt.inc("sc")

            def _ve1(v, tt=tt, par=par4):
                v.wait_ge(sc, SEMS[("ln_stats", tt)])
                if tt >= 4:
                    v.wait_ge(gp, SEMS[("ln_gp", tt - 4)])
                v.tensor_scalar(out=mean_t[par][:], in0=sum_t[par][:], scalar1=1.0 / D,
                                scalar2=None, op0=A.mult)
                v.drain()
                v.tensor_tensor(out=tmp1[par][:], in0=mean_t[par][:], in1=mean_t[par][:], op=A.mult)
                v.drain()
                v.tensor_scalar(out=tmp1[par][:], in0=ssq_t[par][:], scalar1=1.0 / D,
                                scalar2=tmp1[par][:, 0:1], op0=A.mult, op1=A.subtract)
                v.drain()
                v.tensor_scalar(out=tmp1[par][:], in0=tmp1[par][:], scalar1=EPS, scalar2=None,
                                op0=A.add).then_inc(ve, 1)
            emit("vector", _ve1)
            SEMS[("ln_var", tt)] = cnt.inc("ve")

            def _sc2(s, tt=tt, par=par4):
                s.wait_ge(ve, SEMS[("ln_var", tt)])
                if tt >= 4:
                    s.wait_ge(gp, SEMS[("ln_gp", tt - 4)])
                s.activation(out=rstd_t[par][:], in_=tmp1[par][:], func=AF.Ln)
                s.drain()
                s.activation(out=rstd_t[par][:], in_=rstd_t[par][:], func=AF.Exp,
                             scale=-0.5).then_inc(sc, 1)
            emit("scalar", _sc2)
            SEMS[("ln_rstd", tt)] = cnt.inc("sc")

            def _gp(g, tt=tt, par=par, par4=par4, a_tile=a_tile):
                g.wait_ge(sc, SEMS[("ln_rstd", tt)])
                if tt >= 2:
                    g.wait_ge(ve, SEMS[("ln_split", tt - 2)])
                if tt >= 4:
                    g.wait_ge(ve, SEMS[("ln_hi", tt - 4)])
                g.tensor_scalar(out=nmr_t[par4][:], in0=mean_t[par4][:], scalar1=-1.0,
                                scalar2=rstd_t[par4][:, 0:1], op0=A.mult, op1=A.mult).then_inc(gp, 1)
                g.drain()
                g.tensor_scalar(out=xnf[par][:], in0=a_tile, scalar1=rstd_t[par4][:, 0:1],
                                scalar2=nmr_t[par4][:, 0:1], op0=A.mult, op1=A.add).then_inc(gp, 1)
            emit("gpsimd", _gp)
            SEMS[("ln_nmr", tt)] = cnt.inc("gp")
            SEMS[("ln_gp", tt)] = cnt.inc("gp")

            def _ve3(v, tt=tt, par=par, par4=par4, a_tile=a_tile):
                v.wait_ge(gp, SEMS[("ln_nmr", tt)])
                if tt >= 2:
                    v.wait_ge(mm, SEMS[("ln_tp", tt - 2)])
                v.drain()
                v.tensor_scalar(out=tsbh[par][:], in0=a_tile, scalar1=rstd_t[par4][:, 0:1],
                                scalar2=nmr_t[par4][:, 0:1], op0=A.mult, op1=A.add).then_inc(ve, 1)
            emit("vector", _ve3)
            SEMS[("ln_hi", tt)] = cnt.inc("ve")

            def _ve4(v, tt=tt, par=par):
                v.wait_ge(gp, SEMS[("ln_gp", tt)])
                v.drain()
                v.scalar_tensor_tensor(out=tsbl[par][:], in0=xnf[par][:], scalar=1.0,
                                       in1=tsbh[par][:], op0=A.mult, op1=A.subtract).then_inc(ve, 1)
            emit("vector", _ve4)
            SEMS[("ln_split", tt)] = cnt.inc("ve")

            def _tp(t, tt=tt, par=par):
                t.wait_ge(ve, SEMS[("ln_split", tt)])
                if tt == 0:
                    t.wait_ge(ld, v_const_ld)
                if tt >= 1:
                    t.wait_ge(sc, SEMS[("ln_evac", tt - 1)])
                for cc in range(CC):
                    last = (cc == CC - 1)
                    i1 = t.transpose(pth[:, cc * 128:(cc + 1) * 128], tsbh[par][:, cc * 128:(cc + 1) * 128], ident[:])
                    if last:
                        i1.then_inc(mm, 1)
                    i2 = t.transpose(ptl[:, cc * 128:(cc + 1) * 128], tsbl[par][:, cc * 128:(cc + 1) * 128], ident[:])
                    if last:
                        i2.then_inc(mm, 1)
            emit("tensor", _tp)
            SEMS[("ln_tp", tt)] = cnt.inc("mm", 2)

            stage_expert_cc(tt // 8, tt % 8)
            ln_evac_defer.append(tt)
            if len(ln_evac_defer) > 1:
                emit_ln_evac(ln_evac_defer.pop(0))
            if tt >= 1:
                emit_router_mm(tt - 1)
            if tt >= 2:
                emit_router_c1(tt - 2)
            if tt >= 5 and (tt - 5) % 4 == 0:
                emit_router_batch((tt - 5) // 4)
        while ln_evac_defer:
            emit_ln_evac(ln_evac_defer.pop(0))
        emit_router_mm(TT - 1)
        for _q in (TT - 2, TT - 1):
            emit_router_c1(_q)
        emit_router_batch(TT // 4 - 1)
        v_router_done = SEMS[("rt_doneB", TT // 4 - 1)]
        v_we_split[0] = SEMS[("wes", 0, CC - 1)]
        v_we_split[1] = SEMS[("wes", 1, CC - 1)]

        v_xnt_ready = SEMS[("ln_evac", TT - 1)]
        v_xnf_all = SEMS[("ln_gp", TT - 1)]

        def _truncate_store():
            def _store_tr(sy):
                sy.wait_ge(ve, cnt.val("ve"))
                sy.wait_ge(sc, cnt.val("sc"))
                sy.wait_ge(gp, cnt.val("gp"))
                sy.wait_ge(mm, cnt.val("mm"))
                sy.dma_start(out=out_t, in_=acc[:].rearrange("p (tt j) -> p tt j", j=D)).then_inc(st, 16)
            emit("sync", _store_tr)
            cnt.inc("st", 16)
            def _fw(sy):
                sy.wait_ge(st, cnt.val("st"))
                if cnt.val("sg"):
                    sy.wait_ge(sg, cnt.val("sg"))
            emit("sync", _fw)
            with nc.Block() as block:
                @block.sync
                def _(eng):
                    for f in ops["sync"]: f(eng)
                @block.tensor
                def _(eng):
                    for f in ops["tensor"]: f(eng)
                @block.scalar
                def _(eng):
                    for f in ops["scalar"]: f(eng)
                @block.vector
                def _(eng):
                    for f in ops["vector"]: f(eng)
                @block.gpsimd
                def _(eng):
                    for f in ops["gpsimd"]: f(eng)
            nc._dbg_cnt = dict(cnt.c)
            return nc

        if phases == 1:
            return _truncate_store()

        if debug:
            def _dbg_xnt(sy):
                sy.wait_ge(sc, v_xnt_ready)
                sy.dma_start(out=dbg["d_xnh"], in_=xnt_hi[:]).then_inc(sg, 16)
                sy.dma_start(out=dbg["d_xnl"], in_=xnt_lo[:]).then_inc(sg, 16)
            emit("gpsimd", _dbg_xnt)
            cnt.inc("sg", 32)


        if phases == 2:
            return _truncate_store()

        if debug:
            def _dbg_rt(sy):
                sy.wait_ge(ve, v_router_done)
                sy.dma_start(out=dbg["d_coef"], in_=coef[:]).then_inc(sg, 16)
                sy.dma_start(out=dbg["d_s"], in_=s_all[:]).then_inc(sg, 16)
            emit("gpsimd", _dbg_rt)
            cnt.inc("sg", 32)


        for eh in range(NE):
            e, half = eh // 2, eh % 2
            for tt in range(TT):
                p0 = pf[tt % 4]
                def _fmm(t, eh=eh, tt=tt, p0=p0):
                    if eh == 0:
                        t.wait_ge(sc, SEMS[("ln_evac", tt)])
                    if tt == 0:
                        t.wait_ge(ve, v_we_split[eh])
                        if eh == 0:
                            t.wait_ge(ve, v_bbe)
                    if tt >= 4:
                        t.wait_ge(ve, v_ffn_acc[(eh, tt - 4)])
                    elif eh >= 1:
                        t.wait_ge(ve, v_ffn_acc[(eh - 1, TT - 4 + tt)])
                    for cc in range(CC):
                        lh = xnt_hi[:, cc * N + tt * 128: cc * N + (tt + 1) * 128]
                        llo = xnt_lo[:, cc * N + tt * 128: cc * N + (tt + 1) * 128]
                        wh = we_hi[eh % 2][:, cc * 512:(cc + 1) * 512]
                        wl = we_lo[eh % 2][:, cc * 512:(cc + 1) * 512]
                        t.matmul(p0[:], lh, wh, start=(cc == 0), stop=False)
                        t.matmul(p0[:], lh, wl, start=False, stop=False)
                        i3 = t.matmul(p0[:], llo, wh, start=False, stop=(cc == CC - 1))
                        if cc == CC - 1:
                            i3.then_inc(mm, 1)
                emit("tensor", _fmm)
                v_ffn_mm[(eh, tt)] = cnt.inc("mm")

                def _facc(v, e=e, half=half, eh=eh, tt=tt, p0=p0):
                    v.wait_ge(mm, v_ffn_mm[(eh, tt)])
                    v.wait_ge(gp, SEMS[("ln_gp", tt)])
                    cf = coef[:, tt * E + e: tt * E + e + 1]
                    v.drain()
                    v.scalar_tensor_tensor(out=acc[:, tt * D + half * 512: tt * D + (half + 1) * 512],
                                           in0=p0[:], scalar=cf,
                                           in1=acc[:, tt * D + half * 512: tt * D + (half + 1) * 512],
                                           op0=A.mult, op1=A.add).then_inc(ve, 1)
                emit("vector", _facc)
                v_ffn_acc[(eh, tt)] = cnt.inc("ve")

                if eh == NE - 1:
                    def _fin(v, tt=tt):
                        v.drain()
                        v.scalar_tensor_tensor(out=acc[:, tt * D:(tt + 1) * D], in0=boutbc[:],
                                               scalar=s_all[:, tt:tt + 1],
                                               in1=acc[:, tt * D:(tt + 1) * D],
                                               op0=A.add, op1=A.add).then_inc(ve, 1)
                    emit("vector", _fin)
                    v_fin_tt[tt] = cnt.inc("ve")

            if eh + 2 < NE:
                v_we_split[eh + 2] = stage_expert(eh + 2)

        v_fin_last = v_fin_tt[TT - 1]


        ph4.close()
        ph2.close()
        ph1.close()

        ph3 = ExitStack()
        def sb3(name, shape, dt):
            return ph3.enter_context(nc.sbuf_tensor(name, shape, dt))
        def ps3(name, shape, dt):
            return ph3.enter_context(nc.psum_tensor(name, shape, dt))

        wst = [sb3(f"wst{i}", [128, 256], F32) for i in range(3)]
        wqk_hi = sb3("wqk_hi", [128, 2048], BF16)
        wqk_lo = sb3("wqk_lo", [128, 2048], BF16)
        wv_hi = sb3("wv_hi", [128, 2048], BF16)
        wv_lo = sb3("wv_lo", [128, 2048], BF16)
        wo_hi = sb3("wo_hi", [128, D], BF16)
        wo_lo = sb3("wo_lo", [128, D], BF16)
        qt_hi = sb3("qt_hi", [128, N], BF16)
        qt_lo = sb3("qt_lo", [128, N], BF16)
        kt_hi = sb3("kt_hi", [128, N], BF16)
        kt_lo = sb3("kt_lo", [128, N], BF16)
        v_hi = sb3("v_hi", [128, TT * 256], BF16)
        v_lo = sb3("v_lo", [128, TT * 256], BF16)
        ot_hi = sb3("ot_hi", [128, N], BF16)
        ot_lo = sb3("ot_lo", [128, N], BF16)
        btab = sb3("btab_s", [128, BW], F32)
        phi = sb3("phi", [128, 2 * 1024], BF16)
        plo = sb3("plo", [128, 2 * 1024], BF16)
        l_small = sb3("l_small", [1, 512], F32)
        rlT = sb3("rlT", [128, TT], F32)
        nrm = sb3("nrm", [1, 24], F32)
        bvec = [sb3(f"bvec{i}", [128, 3], F32) for i in range(2)]
        vals = sb3("vals", [1, 3], F32)
        c_sc = sb3("c_sc", [1, 4], F32)

        scp = [ps3(f"scp{i}", [128, 1024], F32) for i in range(2)]
        psO = [ps3(f"psO{i}", [128, 512], F32) for i in range(2)]
        psl = [ps3(f"psl{i}", [128, 512], F32) for i in range(2)]

        with ph3:
            wqk_v = wqk.rearrange("(cc p) j -> p cc j", p=128)
            wv_v = wv.rearrange("(cc p) j -> p cc j", p=128)
            wout_v = wout.rearrange("(h p) j -> h p j", p=128)
            we_v = we.rearrange("e (cc p) j -> e p cc j", p=128)
            wst_v = [w[:].rearrange("p (a b) -> p a b", b=128) for w in wst]

            v_wst_free = [0, 0, 0]
            lws = [lw[0], lw[1], la[0]]
            lw_names = ["lw0", "lw1", "la0"]
            wst_par = [0]
            first_ph3 = [True]

            def stage_weights(dst_hi, dst_lo, src_ap, width, dst3d=False, extra_tensor_wait=None):
                par = wst_par[0]
                wst_par[0] = (par + 1) % 3
                wait_ve = v_wst_free[par]
                barrier = first_ph3[0]
                first_ph3[0] = False
                cur_mm, cur_sc, cur_ve, cur_gp = (cnt.val("mm"), cnt.val("sc"), cnt.val("ve"), cnt.val("gp"))
                def _d(sy, wait_ve=wait_ve, src_ap=src_ap, width=width, dst3d=dst3d, barrier=barrier,
                       par=par, cur_mm=cur_mm, cur_sc=cur_sc, cur_ve=cur_ve, cur_gp=cur_gp):
                    if barrier:
                        sy.wait_ge(mm, cur_mm); sy.wait_ge(sc, cur_sc)
                        sy.wait_ge(ve, cur_ve); sy.wait_ge(gp, cur_gp)
                    if wait_ve:
                        sy.wait_ge(ve, wait_ve)
                    dst = wst_v[par][:, 0:width // 128, :] if dst3d else wst[par][:, 0:width]
                    sy.dma_start(out=dst, in_=src_ap).then_inc(lws[par], 16)
                emit("sync", _d)
                v_l = cnt.inc(lw_names[par], 16)
                src_flat = wst_v[par][:, 0:width // 128, :] if dst3d else wst[par][:, 0:width]
                def _h(s, v_l=v_l, par=par, src_flat=src_flat, dst_hi=dst_hi, extra=extra_tensor_wait):
                    s.wait_ge(lws[par], v_l)
                    if extra is not None:
                        s.wait_ge(mm, extra)
                    s.copy(out=dst_hi, in_=src_flat).then_inc(sc, 1)
                emit("scalar", _h)
                v_h = cnt.inc("sc")
                def _lo(v, v_h=v_h, src_flat=src_flat, dst_hi=dst_hi, dst_lo=dst_lo):
                    v.wait_ge(sc, v_h)
                    v.drain()
                    v.scalar_tensor_tensor(out=dst_lo, in0=src_flat, scalar=1.0,
                                           in1=dst_hi, op0=A.mult, op1=A.subtract).then_inc(ve, 1)
                emit("vector", _lo)
                v_done = cnt.inc("ve")
                v_wst_free[par] = v_done
                return v_done

            wqk_hi_3 = wqk_hi[:].rearrange("p (cc two j) -> p cc two j", two=2, j=128)
            wqk_lo_3 = wqk_lo[:].rearrange("p (cc two j) -> p cc two j", two=2, j=128)
            wv_hi_3 = wv_hi[:].rearrange("p (cc two j) -> p cc two j", two=2, j=128)
            wv_lo_3 = wv_lo[:].rearrange("p (cc two j) -> p cc two j", two=2, j=128)
            wqk_src = wqk_v.rearrange("p cc j -> p cc j") if False else wqk_v

            def stage_pair_weights(hi4, lo4, src3, extra_tensor_wait=None):
                d = None
                for g in range(4):
                    ccs = slice(2 * g, 2 * g + 2)
                    for half in range(2):
                        d = stage_weights(hi4[:, ccs, half, :], lo4[:, ccs, half, :],
                                          src3[:, 2 * g:2 * g + 2, half * 128:(half + 1) * 128], 256,
                                          dst3d=True, extra_tensor_wait=extra_tensor_wait)
                return d

            v_qk_split = [None] * NP
            v_wv_split = [None] * NP
            v_wo_split = [None] * H
            v_btab_ld = [None] * H

            v_qk_split[0] = stage_pair_weights(wqk_hi_3, wqk_lo_3, wqk_v[:, :, 0:256])
            v_wv_split[0] = stage_pair_weights(wv_hi_3, wv_lo_3, wv_v[:, :, 0:256])
            def _btab0(sy):
                sy.dma_start(out=btab[:], in_=btabd[0]).then_inc(lb, 16)
            emit("sync", _btab0)
            v_btab_ld[0] = cnt.inc("lb", 16)
            def stage_wout(h, extra_tensor_wait=None):
                d = None
                for g in range(4):
                    d = stage_weights(wo_hi[:, g * 256:(g + 1) * 256], wo_lo[:, g * 256:(g + 1) * 256],
                                      wout_v[h][:, g * 256:(g + 1) * 256], 256,
                                      extra_tensor_wait=extra_tensor_wait)
                return d
            v_wo_split[0] = stage_wout(0)

            sqb = phi[:, 0:512]
            v_exp2 = {}; v_split = {}; v_pvl = {}; v_band = {}
            v_accst = {}; v_ot_evac = {}; v_rlte = {}

            def phi_prev(h, qc, kt):
                if kt >= 2:
                    return (h, qc, kt - 2)
                if qc >= 1:
                    return (h, qc - 1, KT - 2 + kt)
                if h >= 1:
                    return (h - 1, QC - 1, KT - 2 + kt)
                return None

            pair_v_evac = {}
            def emit_setup(p):
                    for tq in range(TQ):
                        qk_po = (psl[0], psl[1]) if tq % 2 == 0 else (psO[0], psO[1])
                        def _qkmm(t, p=p, tq=tq, qk_po=qk_po):
                            if tq == 0:
                                t.wait_ge(ve, v_qk_split[p])
                                if p == 0:
                                    t.wait_ge(ve, v_router_done)
                                else:
                                    t.wait_ge(ve, v_rlte[2 * p - 1])
                            elif tq >= 2:
                                t.wait_ge(ve, SEMS[("qk_evac", p, tq - 2)])
                            for blk, po in ((0, qk_po[0]), (1, qk_po[1])):
                                for cc in range(CC):
                                    lh = wqk_hi[:, cc * 256 + blk * 128: cc * 256 + (blk + 1) * 128]
                                    llo = wqk_lo[:, cc * 256 + blk * 128: cc * 256 + (blk + 1) * 128]
                                    rh = xnt_hi[:, cc * N + tq * 512: cc * N + (tq + 1) * 512]
                                    rl = xnt_lo[:, cc * N + tq * 512: cc * N + (tq + 1) * 512]
                                    t.matmul(po[:], lh, rh, start=(cc == 0), stop=False)
                                    t.matmul(po[:], lh, rl, start=False, stop=False)
                                    i3 = t.matmul(po[:], llo, rh, start=False, stop=(cc == CC - 1))
                                    if cc == CC - 1:
                                        i3.then_inc(mm, 1)
                        emit("tensor", _qkmm)
                        SEMS[("qk_mm", p, tq)] = cnt.inc("mm", 2)

                        def _qk_evh(s, p=p, tq=tq, qk_po=qk_po):
                            s.wait_ge(mm, SEMS[("qk_mm", p, tq)])
                            s.activation(out=qt_hi[:, tq * 512:(tq + 1) * 512], in_=qk_po[0][:],
                                         func=AF.Copy, scale=SCALE).then_inc(sc, 1)
                            s.activation(out=kt_hi[:, tq * 512:(tq + 1) * 512], in_=qk_po[1][:],
                                         func=AF.Copy).then_inc(sc, 1)
                        emit("scalar", _qk_evh)
                        SEMS[("qk_evh", p, tq)] = cnt.inc("sc", 2)

                        def _qk_evl(v, p=p, tq=tq, qk_po=qk_po):
                            v.wait_ge(sc, SEMS[("qk_evh", p, tq)])
                            v.drain()
                            v.scalar_tensor_tensor(out=qt_lo[:, tq * 512:(tq + 1) * 512], in0=qk_po[0][:],
                                                   scalar=SCALE, in1=qt_hi[:, tq * 512:(tq + 1) * 512],
                                                   op0=A.mult, op1=A.subtract)
                            v.drain()
                            v.scalar_tensor_tensor(out=kt_lo[:, tq * 512:(tq + 1) * 512], in0=qk_po[1][:],
                                                   scalar=1.0, in1=kt_hi[:, tq * 512:(tq + 1) * 512],
                                                   op0=A.mult, op1=A.subtract).then_inc(ve, 1)
                        emit("vector", _qk_evl)
                        SEMS[("qk_evac", p, tq)] = cnt.inc("ve")
                    v_qkT_evac = SEMS[("qk_evac", p, TQ - 1)]
                    v_qkproj_mm = SEMS[("qk_mm", p, TQ - 1)]


                    def _sqall(s, p=p):
                        s.wait_ge(ve, SEMS[("qk_evac", p, TQ - 1)])
                        if p >= 1:
                            s.wait_ge(mm, v_pvl[(2 * p - 1, QC - 1, KT // 2 - 1)])
                        s.activation(out=phi[:], in_=qt_hi[:], func=AF.Square).then_inc(sc, 1)
                        s.activation(out=plo[:], in_=kt_hi[:], func=AF.Square).then_inc(sc, 1)
                    emit("scalar", _sqall)
                    SEMS[("nrm_sq", p)] = cnt.inc("sc", 2)

                    nregions = [psO[0][0:1, :], psO[1][0:1, :]]
                    def emit_norm_i(i, p=p):
                        g, ch = divmod(i, 4)
                        ten, mask = divmod(g, 2)
                        sq_src = phi if ten == 0 else plo
                        def _nmm(t, p=p, i=i, ch=ch, mask=mask, sq_src=sq_src):
                            if i == 0:
                                t.wait_ge(sc, SEMS[("nrm_sq", p)])
                            if i >= 2:
                                t.wait_ge(ve, SEMS[("nrm_red", p, i - 2)])
                            src_ap = sq_src[:, ch * 512:(ch + 1) * 512]
                            t.matmul(nregions[i % 2], ones_hi[:] if mask == 0 else ones_lo[:],
                                     src_ap, start=True, stop=True).then_inc(mm, 1)
                        emit("tensor", _nmm)
                        SEMS[("nrm_mm", p, i)] = cnt.inc("mm")

                        def _nred(v, p=p, i=i):
                            v.wait_ge(mm, SEMS[("nrm_mm", p, i)])
                            v.reduce_max(out=nrm[0:1, i:i + 1], in_=nregions[i % 2], axis=AX.X).then_inc(ve, 1)
                        emit("vector", _nred)
                        SEMS[("nrm_red", p, i)] = cnt.inc("ve")


                    for tt in range(TT):
                        def _vmm(t, p=p, tt=tt):
                            if tt == 0:
                                t.wait_ge(ve, v_wv_split[p])
                                t.wait_ge(ve, SEMS[("qk_evac", p, TQ - 1)])
                            if tt >= 2:
                                t.wait_ge(ve, SEMS[("v_ev", p, tt - 2)])
                            po = psl[tt % 2]
                            for cc in range(CC):
                                lh = xnt_hi[:, cc * N + tt * 128: cc * N + (tt + 1) * 128]
                                llo = xnt_lo[:, cc * N + tt * 128: cc * N + (tt + 1) * 128]
                                rh = wv_hi[:, cc * 256:(cc + 1) * 256]
                                rl = wv_lo[:, cc * 256:(cc + 1) * 256]
                                t.matmul(po[:, 0:256], lh, rh, start=(cc == 0), stop=False)
                                t.matmul(po[:, 0:256], lh, rl, start=False, stop=False)
                                i3 = t.matmul(po[:, 0:256], llo, rh, start=False, stop=(cc == CC - 1))
                                if cc == CC - 1:
                                    i3.then_inc(mm, 1)
                        emit("tensor", _vmm)
                        SEMS[("v_mm", p, tt)] = cnt.inc("mm")

                        def _vevh(s, p=p, tt=tt):
                            s.wait_ge(mm, SEMS[("v_mm", p, tt)])
                            s.drain()
                            s.copy(out=v_hi[:, tt * 256:(tt + 1) * 256], in_=psl[tt % 2][:, 0:256]).then_inc(sc, 1)
                        emit("scalar", _vevh)
                        SEMS[("v_evh", p, tt)] = cnt.inc("sc")

                        def _vevl(v, p=p, tt=tt):
                            v.wait_ge(sc, SEMS[("v_evh", p, tt)])
                            v.drain()
                            v.scalar_tensor_tensor(out=v_lo[:, tt * 256:(tt + 1) * 256], in0=psl[tt % 2][:, 0:256],
                                                   scalar=1.0, in1=v_hi[:, tt * 256:(tt + 1) * 256],
                                                   op0=A.mult, op1=A.subtract).then_inc(ve, 1)
                        emit("vector", _vevl)
                        SEMS[("v_ev", p, tt)] = cnt.inc("ve")
                        if 1 <= tt <= 15:
                            emit_norm_i(tt - 1)
                    emit_norm_i(15)
                    def _ncmb(v, p=p):
                        v.wait_ge(ve, SEMS[("nrm_red", p, 15)])
                        v.drain()
                        v.reduce_max(out=nrm[0:1, 16:17], in_=nrm[0:1, 0:4], axis=AX.X)
                        v.reduce_max(out=nrm[0:1, 17:18], in_=nrm[0:1, 8:12], axis=AX.X)
                        v.reduce_max(out=nrm[0:1, 18:19], in_=nrm[0:1, 4:8], axis=AX.X)
                        v.drain()
                        v.reduce_max(out=nrm[0:1, 19:20], in_=nrm[0:1, 12:16], axis=AX.X).then_inc(ve, 1)
                    emit("vector", _ncmb)
                    v_nrm_done = cnt.inc("ve")

                    v_v_evac = SEMS[("v_ev", p, TT - 1)]
                    v_vproj_mm = SEMS[("v_mm", p, TT - 1)]


                    if debug and p == 0:
                        def _dbg_v(sy, v_v_evac=v_v_evac):
                            sy.wait_ge(ve, v_v_evac)
                            sy.dma_start(out=dbg["d_v"], in_=v_hi[:]).then_inc(sg, 16)
                        emit("gpsimd", _dbg_v)
                        cnt.inc("sg", 16)

                    for hh in range(2):
                        h = 2 * p + hh
                        def _chv(v, h=h, hh=hh, v_nrm_done=v_nrm_done):
                            v.wait_ge(ve, v_nrm_done)
                            if h == 0:
                                v.wait_ge(ld, v_const_ld)
                            if h >= 2:
                                v.wait_ge(sc, v_exp2[(h - 2, QC - 1, KT // 2 - 1)])
                            v.drain()
                            v.reduce_sum(out=c_sc[0:1, 2:3], in_=nrm[0:1, 16 + 2 * hh:18 + 2 * hh], axis=AX.X)
                            v.drain()
                            v.reduce_max(out=c_sc[0:1, 3:4], in_=convf[0:1, h * 21:(h + 1) * 21], axis=AX.X)
                            v.drain()
                            v.tensor_scalar(out=c_sc[0:1, 2:3], in0=c_sc[0:1, 2:3], scalar1=0.5,
                                            scalar2=c_sc[0:1, 3:4], op0=A.mult, op1=A.add)
                            v.drain()
                            v.tensor_scalar(out=c_sc[0:1, 2:3], in0=c_sc[0:1, 2:3], scalar1=1.0,
                                            scalar2=None, op0=A.add)
                            v.drain()
                            v.tensor_scalar(out=vals[0:1, :], in0=biassel[0:1, 3 * h:3 * h + 3],
                                            scalar1=c_sc[0:1, 2:3], scalar2=None, op0=A.subtract).then_inc(ve, 1)
                        emit("vector", _chv)
                        SEMS[("ch_v", h)] = cnt.inc("ve")

                        def _chmm(t, h=h):
                            t.wait_ge(ve, SEMS[("ch_v", h)])
                            t.matmul(psl[0][:, 0:3], onesf[0:1, :], vals[0:1, :], start=True, stop=True).then_inc(mm, 1)
                        emit("tensor", _chmm)
                        SEMS[("ch_mm", h)] = cnt.inc("mm")

                        def _chev(v, h=h, hh=hh):
                            v.wait_ge(mm, SEMS[("ch_mm", h)])
                            v.drain()
                            v.tensor_copy(out=bvec[hh][:], in_=psl[0][:, 0:3]).then_inc(ve, 1)
                        emit("vector", _chev)
                        SEMS[("ch", h)] = cnt.inc("ve")

                    if debug and p == 0:
                        def _dbg_ch(sy):
                            sy.wait_ge(ve, SEMS[("ch", 1)])
                            sy.dma_start(out=dbg["d_qt"], in_=qt_hi[:]).then_inc(sg, 16)
                            sy.dma_start(out=dbg["d_kt"], in_=kt_hi[:]).then_inc(sg, 16)
                        emit("gpsimd", _dbg_ch)
                        cnt.inc("sg", 32)


                    if p + 1 < NP:
                        v_qk_split[p + 1] = stage_pair_weights(
                            wqk_hi_3, wqk_lo_3, wqk_v[:, :, (p + 1) * 256:(p + 2) * 256],
                            extra_tensor_wait=v_qkproj_mm)
                        v_wv_split[p + 1] = stage_pair_weights(
                            wv_hi_3, wv_lo_3, wv_v[:, :, (p + 1) * 256:(p + 2) * 256],
                            extra_tensor_wait=v_vproj_mm)
                    pair_v_evac[p] = v_v_evac
                    return v_v_evac

            for p in range(NP):
                h0, h1 = 2 * p, 2 * p + 1
                if p == 0:
                    v_v_evac0 = emit_setup(0)
                v_v_evac = pair_v_evac[p]

                for hh in range(2):
                    h = 2 * p + hh
                    qthi = qt_hi[64 * hh: 64 + 64 * hh, :]
                    qtlo = qt_lo[64 * hh: 64 + 64 * hh, :]
                    kthi = kt_hi[64 * hh: 64 + 64 * hh, :]
                    ktlo = kt_lo[64 * hh: 64 + 64 * hh, :]
                    scph = (scp[0], scp[1])

                    def emit_evac_rlt(qc, h=h, hh=hh):
                            def _rlt_mm(t, h=h, hh=hh, qc=qc):
                                t.wait_ge(ve, SEMS[("l_rel", h, qc)])
                                for i in range(4):
                                    t.matmul(psl[qc % 2][:, 100 + i: 101 + i],
                                             l_small[0:1, i * 128:(i + 1) * 128],
                                             onesf[0:1, 0:1], start=True, stop=True).then_inc(mm, 1)
                            emit("tensor", _rlt_mm)
                            SEMS[("rlt_mm", h, qc)] = cnt.inc("mm", 4)

                            def _rlt_ev(v, h=h, hh=hh, qc=qc):
                                v.wait_ge(mm, SEMS[("rlt_mm", h, qc)])
                                v.drain()
                                v.tensor_copy(out=rlT[:, qc * 4:(qc + 1) * 4],
                                              in_=psl[qc % 2][:, 100:104])
                                v.drain()
                                v.reciprocal(out=rlT[:, qc * 4:(qc + 1) * 4],
                                             in_=rlT[:, qc * 4:(qc + 1) * 4]).then_inc(ve, 1)
                            emit("vector", _rlt_ev)
                            SEMS[("rlt_ev", h, qc)] = cnt.inc("ve")

                    def emit_evac(qc, h=h, hh=hh):
                            def _otla(v, h=h, hh=hh, qc=qc):
                                v.wait_ge(mm, v_pvl[(h, qc, KT // 2 - 1)])
                                v.drain()
                                v.tensor_copy(out=l_small[0:1, :], in_=psl[qc % 2][0:1, :]).then_inc(ve, 1)
                            emit("vector", _otla)
                            SEMS[("l_rel", h, qc)] = cnt.inc("ve")

                            def _oth(s, h=h, hh=hh, qc=qc):
                                s.wait_ge(mm, v_pvl[(h, qc, KT // 2 - 1)])
                                s.drain()
                                s.copy(out=ot_hi[:, qc * 512:(qc + 1) * 512], in_=psO[qc % 2][:]).then_inc(sc, 1)
                            emit("scalar", _oth)
                            SEMS[("ot_h", h, qc)] = cnt.inc("sc")

                            def _otl(v, h=h, hh=hh, qc=qc):
                                v.wait_ge(sc, SEMS[("ot_h", h, qc)])
                                v.drain()
                                v.scalar_tensor_tensor(out=ot_lo[:, qc * 512:(qc + 1) * 512], in0=psO[qc % 2][:],
                                                       scalar=1.0, in1=ot_hi[:, qc * 512:(qc + 1) * 512],
                                                       op0=A.mult, op1=A.subtract).then_inc(ve, 1)
                            emit("vector", _otl)
                            v_ot_evac[(h, qc)] = cnt.inc("ve")

                    v_v_evac_pair = v_v_evac
                    evac_defer = []
                    rlt_defer = []
                    sched = [(qc_, ktp_) for qc_ in range(QC) for ktp_ in range(KT // 2)]
                    pv_defer = []
                    for qc, ktp in sched:
                        if True:
                            while rlt_defer:
                                emit_evac_rlt(rlt_defer.pop(0))
                            while evac_defer:
                                _q2 = evac_defer.pop(0)
                                emit_evac(_q2)
                                rlt_defer.append(_q2)
                            kt0, kt1 = 2 * ktp, 2 * ktp + 1
                            parp = ktp % 2
                            po = scph[parp]
                            cls0, cls1 = tile_class(qc, kt0), tile_class(qc, kt1)
                            if ktp >= 2:
                                prev_sc = (h, qc, ktp - 2)
                            elif qc >= 1:
                                prev_sc = (h, qc - 1, KT // 2 - 2 + ktp)
                            elif h >= 1:
                                prev_sc = (h - 1, QC - 1, KT // 2 - 2 + ktp)
                            else:
                                prev_sc = None
                            if ktp >= 2:
                                prev_p = (h, qc, ktp - 2)
                            elif qc >= 1:
                                prev_p = (h, qc - 1, KT // 2 - 2 + ktp)
                            elif h >= 1:
                                prev_p = (h - 1, QC - 1, KT // 2 - 2 + ktp)
                            else:
                                prev_p = None

                            def _smm(t, h=h, hh=hh, qc=qc, kt0=kt0, kt1=kt1, po=po, prev_sc=prev_sc,
                                     qthi=qthi, qtlo=qtlo, kthi=kthi, ktlo=ktlo, v_v_evac=v_v_evac):
                                if qc == 0 and kt0 == 0 and h >= 1:
                                    t.wait_ge(ve, v_accst[(h - 1, TT - 1)])
                                if prev_sc is not None:
                                    t.wait_ge(ve, v_split[prev_sc])
                                tp = (64 * hh, 0)
                                rh = qthi[:, qc * 512:(qc + 1) * 512]
                                rl = qtlo[:, qc * 512:(qc + 1) * 512]
                                for half, kt in ((0, kt0), (1, kt1)):
                                    lh = kthi[:, kt * 128:(kt + 1) * 128]
                                    llo = ktlo[:, kt * 128:(kt + 1) * 128]
                                    ph = po[:, half * 512:(half + 1) * 512]
                                    t.matmul(ph, lh, rh, start=True, stop=False, tile_position=tp)
                                    t.matmul(ph, lh, rl, start=False, stop=False, tile_position=tp)
                                    t.matmul(ph, llo, rh, start=False, stop=True, tile_position=tp).then_inc(mm, 1)
                            emit("tensor", _smm)
                            SEMS[("sc_mm", h, qc, ktp)] = cnt.inc("mm", 2)

                            nband = (1 if cls0 == 0 else 0) + (1 if cls1 == 0 else 0)
                            if nband:
                                def _badd(v, h=h, qc=qc, kt0=kt0, kt1=kt1, po=po, ktp=ktp,
                                          cls0=cls0, cls1=cls1):
                                    v.wait_ge(mm, SEMS[("sc_mm", h, qc, ktp)])
                                    if (qc, kt0) == FIRST_BAND or (qc, kt1) == FIRST_BAND:
                                        v.wait_ge(lb, v_btab_ld[h])
                                    first = True
                                    for half, kt, cls in ((0, kt0, cls0), (1, kt1, cls1)):
                                        if cls != 0:
                                            continue
                                        d0 = kt * 128 - qc * 512
                                        off = U0 - d0
                                        if first:
                                            v.drain()
                                            first = False
                                        ph = po[:, half * 512:(half + 1) * 512]
                                        v.tensor_tensor(out=ph, in0=ph,
                                                        in1=btab[:, off:off + 512], op=A.add).then_inc(ve, 1)
                                emit("vector", _badd)
                                v_badd = cnt.inc("ve", nband)
                                wait_exp = ("ve", v_badd)
                            else:
                                wait_exp = ("mm", SEMS[("sc_mm", h, qc, ktp)])

                            def _exp(s, h=h, hh=hh, qc=qc, ktp=ktp, po=po, parp=parp,
                                     wait_exp=wait_exp, cls0=cls0, cls1=cls1, prev_p=prev_p):
                                eng_s, v_w = wait_exp
                                s.wait_ge({"ve": ve, "mm": mm}[eng_s], v_w)
                                if qc == 0 and ktp == 0:
                                    s.wait_ge(ve, SEMS[("ch", h)])
                                if prev_p is not None:
                                    s.wait_ge(mm, v_pvl[prev_p])
                                b0 = 0 if cls0 == 0 else cls0
                                b1 = 0 if cls1 == 0 else cls1
                                if b0 == b1:
                                    s.activation(out=po[:], in_=po[:], func=AF.Exp,
                                                 bias=bvec[hh][:, b0:b0 + 1])
                                else:
                                    s.activation(out=po[:, 0:512], in_=po[:, 0:512], func=AF.Exp,
                                                 bias=bvec[hh][:, b0:b0 + 1])
                                    s.activation(out=po[:, 512:1024], in_=po[:, 512:1024], func=AF.Exp,
                                                 bias=bvec[hh][:, b1:b1 + 1])
                                s.drain()
                                s.copy(out=phi[:, parp * 1024:(parp + 1) * 1024], in_=po[:]).then_inc(sc, 1)
                            emit("scalar", _exp)
                            v_exp2[(h, qc, ktp)] = cnt.inc("sc")

                            def _spl(v, h=h, qc=qc, ktp=ktp, parp=parp, po=po, prev_p=prev_p):
                                v.wait_ge(sc, v_exp2[(h, qc, ktp)])
                                if prev_p is not None:
                                    v.wait_ge(mm, v_pvl[prev_p])
                                v.drain()
                                v.scalar_tensor_tensor(out=plo[:, parp * 1024:(parp + 1) * 1024],
                                                       in0=po[:], scalar=1.0,
                                                       in1=phi[:, parp * 1024:(parp + 1) * 1024],
                                                       op0=A.mult, op1=A.subtract).then_inc(ve, 1)
                            emit("vector", _spl)
                            v_split[(h, qc, ktp)] = cnt.inc("ve")

                            def _mk_pv(h=h, hh=hh, qc=qc, ktp=ktp, kt0=kt0, kt1=kt1, parp=parp,
                                       v_v_evac_pair=v_v_evac_pair):
                              def _pv(t, h=h, hh=hh, qc=qc, ktp=ktp, kt0=kt0, kt1=kt1, parp=parp,
                                      v_v_evac_pair=v_v_evac_pair):
                                  t.wait_ge(ve, v_split[(h, qc, ktp)])
                                  if ktp == 0 and qc == 0:
                                      t.wait_ge(ve, v_v_evac_pair)
                                  if ktp == 0 and qc >= 2:
                                      t.wait_ge(ve, SEMS[("rlt_ev", h, qc - 2)])
                                      t.wait_ge(ve, v_ot_evac[(h, qc - 2)])
                                  for half, kt in ((0, kt0), (1, kt1)):
                                      vh = v_hi[:, kt * 256 + hh * 128: kt * 256 + (hh + 1) * 128]
                                      vl = v_lo[:, kt * 256 + hh * 128: kt * 256 + (hh + 1) * 128]
                                      ph_ = phi[:, parp * 1024 + half * 512: parp * 1024 + (half + 1) * 512]
                                      pl_ = plo[:, parp * 1024 + half * 512: parp * 1024 + (half + 1) * 512]
                                      st0 = (kt == 0)
                                      pso_ = psO[qc % 2]
                                      t.matmul(pso_[:], vh, ph_, start=st0, stop=False)
                                      t.matmul(pso_[:], vh, pl_, start=False, stop=False)
                                      t.matmul(pso_[:], vl, ph_, start=False, stop=(kt == KT - 1))
                                      psl_ = psl[qc % 2]
                                      t.matmul(psl_[0:1, :], onesb[:], ph_, start=st0, stop=False)
                                      t.matmul(psl_[0:1, :], onesb[:], pl_,
                                               start=False, stop=(kt == KT - 1)).then_inc(mm, 1)
                              emit("tensor", _pv)
                              v_pvl[(h, qc, ktp)] = cnt.inc("mm", 2)
                            pv_defer.append((qc, ktp, _mk_pv))
                            if len(pv_defer) > 2:
                                _q, _k, _f = pv_defer.pop(0)
                                _f()
                                if _k == KT // 2 - 1:
                                    evac_defer.append(_q)
                    while pv_defer:
                        _q, _k, _f = pv_defer.pop(0)
                        _f()
                        if _k == KT // 2 - 1:
                            evac_defer.append(_q)
                    while rlt_defer:
                        emit_evac_rlt(rlt_defer.pop(0))
                    while evac_defer:
                        _q2 = evac_defer.pop(0)
                        emit_evac(_q2)
                        emit_evac_rlt(_q2)

                    bands_h = [v_band[k] for k in v_band if k[0] == h and v_band[k] is not None]
                    v_band_last_h = max(bands_h) if bands_h else cnt.val("ve")
                    if h + 1 < H:
                        def _btd(sy, h=h, v_band_last_h=v_band_last_h):
                            sy.wait_ge(ve, v_band_last_h)
                            sy.dma_start(out=btab[:], in_=btabd[h + 1]).then_inc(lb, 16)
                        emit("sync", _btd)
                        v_btab_ld[h + 1] = cnt.inc("lb", 16)

                    v_rlte[h] = SEMS[("rlt_ev", h, QC - 1)]

                    if debug and h == 0:
                        def _dbg_l(sy):
                            sy.wait_ge(ve, v_rlte[0])
                            sy.dma_start(out=dbg["d_l"], in_=rlT[:]).then_inc(sg, 16)
                        emit("gpsimd", _dbg_l)
                        cnt.inc("sg", 16)

                    if hh == 1 and p + 1 < NP:
                        emit_setup(p + 1)

                    for tt in range(TT):
                        poj = (scp[tt % 2][:, 0:512], scp[tt % 2][:, 512:1024])
                        def _omm(t, hh=hh, h=h, tt=tt, poj=poj):
                            if tt == 0:
                                t.wait_ge(ve, v_wo_split[h])
                                t.wait_ge(ve, v_split[(h, QC - 1, KT // 2 - 1)])
                                if h == 0:
                                    t.wait_ge(gp, v_xnf_all)
                            if tt >= 2:
                                t.wait_ge(ve, v_accst[(h, tt - 2)])
                            elif h >= 1:
                                t.wait_ge(ve, v_accst[(h - 1, TT - 2 + tt)])
                            oth_ = ot_hi[:, tt * 128:(tt + 1) * 128]
                            otl_ = ot_lo[:, tt * 128:(tt + 1) * 128]
                            for half in range(2):
                                po2 = poj[half]
                                wh = wo_hi[:, half * 512:(half + 1) * 512]
                                wl = wo_lo[:, half * 512:(half + 1) * 512]
                                t.matmul(po2, oth_, wh, start=True, stop=False)
                                t.matmul(po2, oth_, wl, start=False, stop=False)
                                t.matmul(po2, otl_, wh, start=False, stop=True).then_inc(mm, 1)
                        emit("tensor", _omm)
                        SEMS[("op_mm", h, tt)] = cnt.inc("mm", 2)

                        def _oacc(v, hh=hh, h=h, tt=tt):
                            v.wait_ge(mm, SEMS[("op_mm", h, tt)])
                            v.drain()
                            v.scalar_tensor_tensor(out=acc[:, tt * D: (tt + 1) * D],
                                                   in0=scp[tt % 2][:, 0:1024], scalar=rlT[:, tt:tt + 1],
                                                   in1=acc[:, tt * D: (tt + 1) * D],
                                                   op0=A.mult, op1=A.add).then_inc(ve, 1)
                        emit("vector", _oacc)
                        v_accst[(h, tt)] = cnt.inc("ve", 1)

                    if h + 1 < H:
                        v_wo_split[h + 1] = stage_wout(h + 1, extra_tensor_wait=SEMS[("op_mm", h, TT - 1)])

                    if debug and h == 0:
                        def _dbg_ot(sy):
                            sy.wait_ge(ve, v_accst[(0, TT - 1)])
                            sy.dma_start(out=dbg["d_ot"], in_=ot_hi[:]).then_inc(sg, 16)
                        emit("gpsimd", _dbg_ot)
                        cnt.inc("sg", 16)

            v_attn_done = v_accst[(H - 1, TT - 1)]
            if phases == 3:
                return _truncate_store()

        if debug:
            def _dbg_acc(g):
                g.wait_ge(ve, v_attn_done)
                g.dma_start(out=dbg["d_accattn"], in_=acc[:]).then_inc(sg, 16)
            emit("gpsimd", _dbg_acc)
        v_dbgacc = cnt.inc("sg", 16) if debug else 0

        def _store(sy, v_fin_last=v_fin_last):
            sy.wait_ge(ve, v_accst[(H - 1, TT - 1)])
            sy.wait_ge(ve, v_fin_last)
            sy.dma_start(out=out_t, in_=acc[:].rearrange("p (tt j) -> p tt j", j=D)).then_inc(st, 16)
        emit("sync", _store)
        cnt.inc("st", 16)

        def _final_wait(sy):
            sy.wait_ge(st, cnt.val("st"))
            if cnt.val("sg"):
                sy.wait_ge(sg, cnt.val("sg"))
        emit("sync", _final_wait)

        with nc.Block() as block:
            @block.sync
            def _(eng):
                for f in ops["sync"]:
                    f(eng)
            @block.tensor
            def _(eng):
                for f in ops["tensor"]:
                    f(eng)
            @block.scalar
            def _(eng):
                for f in ops["scalar"]:
                    f(eng)
            @block.vector
            def _(eng):
                for f in ops["vector"]:
                    f(eng)
            @block.gpsimd
            def _(eng):
                for f in ops["gpsimd"]:
                    f(eng)

    nc._dbg_cnt = dict(cnt.c)
    return nc


def prep_inputs(L0_attn_x, L0_attn_W_qk, L0_attn_W_v, L0_attn_W_out, L0_attn_b_out,
                L0_attn_conv, L0_ffn_router_w, L0_ffn_expert_weights):
    Wqk = np.ascontiguousarray(np.asarray(L0_attn_W_qk), dtype=np.float32)
    cols = []
    for p in range(NP):
        cols.append(Wqk[:, (2 * p) * 64:(2 * p + 2) * 64])
        cols.append(Wqk[:, 512 + (2 * p) * 64: 512 + (2 * p + 2) * 64])
    wqk_perm = np.ascontiguousarray(np.concatenate(cols, axis=1))

    conv = np.asarray(L0_attn_conv, dtype=np.float32)
    r = np.arange(128)[:, None]
    u = np.arange(BW)[None, :]
    idx = np.clip(r - u + U0, -10, 10) + 10
    btab = np.ascontiguousarray(conv[:, idx])
    biassel = np.ascontiguousarray(
        np.stack([np.zeros(H, np.float32), conv[:, 0], conv[:, 20]], axis=1).reshape(1, 3 * H))
    convf = np.ascontiguousarray(conv.reshape(1, H * 21))
    ident = np.eye(128, dtype=ml_dtypes.bfloat16)

    common = {
        "wqk": wqk_perm,
        "wv": np.ascontiguousarray(np.asarray(L0_attn_W_v), dtype=np.float32),
        "wout": np.ascontiguousarray(np.asarray(L0_attn_W_out), dtype=np.float32),
        "bout": np.ascontiguousarray(np.asarray(L0_attn_b_out, dtype=np.float32).reshape(1, D)),
        "rw": np.ascontiguousarray(np.asarray(L0_ffn_router_w, dtype=np.float32).T),
        "we": np.ascontiguousarray(np.asarray(L0_ffn_expert_weights), dtype=np.float32),
        "btab": btab,
        "biassel": biassel,
        "convf": convf,
        "ident": ident,
    }
    xs = np.asarray(L0_attn_x, dtype=np.float32)
    in_maps = []
    for b in range(xs.shape[0]):
        m = dict(common)
        m["x"] = np.ascontiguousarray(xs[b])
        in_maps.append(m)
    return in_maps


def kernel(L0_attn_x, L0_attn_mask, L0_attn_W_qk, L0_attn_W_v, L0_attn_W_out,
           L0_attn_b_out, L0_attn_conv, L0_ffn_router_w, L0_ffn_expert_weights):
    in_maps = prep_inputs(L0_attn_x, L0_attn_W_qk, L0_attn_W_v, L0_attn_W_out,
                          L0_attn_b_out, L0_attn_conv, L0_ffn_router_w, L0_ffn_expert_weights)
    nc = build_program(debug=False)
    res = run_bass_kernel_spmd(nc, in_maps, core_ids=list(range(8)))
    return np.stack([r["out"] for r in res.results], axis=0).astype(np.float32)
